# version 02
# baseline (speedup 1.0000x reference)
import numpy as np
from contextlib import ExitStack

import concourse.bass as bass
import concourse.mybir as mybir
from concourse.bass_utils import run_bass_kernel_spmd

F32 = mybir.dt.float32
BF16 = mybir.dt.bfloat16
I32 = mybir.dt.int32
AF = mybir.ActivationFunctionType
ALU = mybir.AluOpType
AX = mybir.AxisListType

NCORES = 8
B, S, D = 4, 4096, 1024
DH = 64
CD = 512
DFF = 4096
OWN = 512
WIN = 516
NT = 17
TP = NT * 128
EPS = 1e-5
NKT = S // 128
QT = [128, 128, 128, 128, 4]
QCH = [(0, 258), (258, 258)]
SHUF = [i + 1 if i % 2 == 0 else i - 1 for i in range(32)]
HTW = TP + 8


class Em:
    ENGS = ("sync", "gpsimd", "vector", "scalar", "tensor")

    def __init__(self, nc, ctx):
        self.nc = nc
        self.plan = {e: [] for e in self.ENGS}
        self.sems = {}
        self.cnt = {}
        for name in ("dsy", "dgp", "v", "s", "t"):
            self.sems[name] = ctx.enter_context(nc.semaphore(name))
            self.cnt[name] = 0

    def waits(self, eng, toks):
        best = {}
        for tk in toks:
            if tk is None:
                continue
            n, v = tk
            assert v <= self.cnt[n], (n, v, self.cnt[n])
            best[n] = max(best.get(n, 0), v)
        for n, v in best.items():
            h = self.sems[n]
            self.plan[eng].append(lambda e, h=h, v=v: e.wait_ge(h, v))

    def op(self, eng, fn, inc=None, k=1):
        if inc is None:
            self.plan[eng].append(lambda e, fn=fn: fn(e))
            return None
        h = self.sems[inc]
        self.plan[eng].append(lambda e, fn=fn, h=h, k=k: fn(e).then_inc(h, k))
        self.cnt[inc] += k
        return (inc, self.cnt[inc])

    def dma(self, out, in_, after=(), eng="sync", **kw):
        self.waits(eng, after)
        sem = "dgp" if eng == "gpsimd" else "dsy"
        return self.op(eng, lambda e: e.dma_start(out=out, in_=in_, **kw),
                       inc=sem, k=16)

    def v(self, fn, after=()):
        self.waits("vector", after)
        return self.op("vector", fn, inc="v")

    def s(self, fn, after=()):
        self.waits("scalar", after)
        return self.op("scalar", fn, inc="s")

    def t(self, fn, after=()):
        self.waits("tensor", after)
        return self.op("tensor", fn, inc="t")

    def barrier(self):
        snap = [(n, v) for n, v in self.cnt.items() if v > 0]
        for eng in self.ENGS:
            self.waits(eng, snap)

    def flush(self):
        with self.nc.Block() as blk:
            for eng in self.ENGS:
                ops = self.plan[eng]
                if not ops:
                    continue
                def body(e, ops=ops):
                    for f in ops:
                        f(e)
                getattr(blk, eng)(body)
        self.plan = {e: [] for e in self.ENGS}


class Bank:
    def __init__(self, ap):
        self.ap = ap
        self.consumers = []

    def wtoks(self):
        t, self.consumers = self.consumers, []
        return t

    def read(self, tok):
        self.consumers.append(tok)
        return tok


def build_nc(dbg=(), phases=("g1", "ffn1", "attnb", "ffn2")):
    nc = bass.Bass("TRN2", target_bir_lowering=False)

    def din(n, sh):
        return nc.dram_tensor(n, sh, F32, kind="ExternalInput").ap()

    xwin = din("xwin", [B, WIN, D])
    xfull = din("xfull", [B, S, D])
    wdqT = din("wdqT", [D, CD])
    wdkvT = din("wdkvT", [D, CD])
    wuqT = din("wuqT", [CD, D])
    wukT = din("wukT", [CD, D])
    wuvT = din("wuvT", [CD, D])
    woutTa = din("woutTa", [D, D])
    lnsc = din("lnsc", [128, 8])
    lnbi = din("lnbi", [128, 8])
    winT = din("winT", [D, 3 * D])
    binb = din("binb", [128, 24])
    woutTb = din("woutTb", [D, D])
    boutb = din("boutb", [128, 8])
    boutf = din("boutf", [1, D])
    w1T = din("w1T", [D, DFF])
    w2T = din("w2T", [DFF, D])
    cw0 = din("cw0", [128, 32])
    cw1 = din("cw1", [128, 32])
    cw2 = din("cw2", [128, 32])
    cbv = din("cbv", [128, 32])
    ident = din("ident", [128, 128])
    kC = din("kC", [128, S])
    kS = din("kS", [128, S])
    qC = din("qC", [128, WIN])
    qS = din("qS", [128, WIN])
    bmask = din("bmask", [128, 128])
    maskv = din("maskv", [128, NT])

    yout = nc.dram_tensor("yout", [B, OWN, D], F32, kind="ExternalOutput").ap()

    r1buf = nc.dram_tensor("r1buf", [B, WIN, D], F32).ap()
    x1buf = nc.dram_tensor("x1buf", [B, WIN, D], F32).ap()
    x2buf = nc.dram_tensor("x2buf", [TP, D], F32).ap()
    x3buf = nc.dram_tensor("x3buf", [TP, D], F32).ap()
    w1qc = nc.dram_tensor("w1qc", [D, DFF], BF16).ap()
    w2qc = nc.dram_tensor("w2qc", [DFF, D], BF16).ap()
    hsbuf = nc.dram_tensor("hsbuf", [DFF, TP], F32).ap()
    sscr = nc.dram_tensor("sscr", [3 * TP], F32).ap()

    dbg_aps = {}
    _dbg_shapes = {"x1": ([B, WIN, D], F32), "x2": ([TP, D], F32),
                   "x3": ([TP, D], F32),
                   "ckv": ([128, 4 * S], BF16), "cq": ([128, 4 * WIN], BF16),
                   "krot": ([128, S], BF16), "qrot": ([128, WIN], BF16),
                   "nmax": ([128, 10], F32), "oacc": ([128, 8 * WIN], BF16),
                   "vsb": ([128, NKT * 520], BF16),
                   "htm": ([128, D], BF16), "hfm": ([128, 8 * 512], BF16),
                   "sts": ([128, 32], F32),
                   "qkb0": ([128, TP], BF16), "qkb1": ([128, TP], BF16),
                   "qkb2": ([128, TP], BF16), "ob0": ([128, TP], BF16),
                   "lnb": ([128, TP], BF16)}
    for name in dbg:
        sh, dty = _dbg_shapes[name]
        dbg_aps[name] = nc.dram_tensor("dbg_" + name, sh, dty,
                                       kind="ExternalOutput").ap()

    octx = ExitStack()
    sbg = lambda n, sh, dty: octx.enter_context(nc.sbuf_tensor(n, sh, dty))
    em = Em(nc, octx)

    id32 = sbg("id32", [128, 128], F32)
    idb = sbg("idb", [128, 128], BF16)
    ones_b = sbg("ones_b", [1, 128], BF16)
    ones_f = sbg("ones_f", [1, 128], F32)
    ones_p = sbg("ones_p", [128, 1], F32)
    lnsc_f = sbg("lnsc_f", [128, 8], F32)
    lnbi_f = sbg("lnbi_f", [128, 8], F32)
    lnbi_b = sbg("lnbi_b", [128, 8], BF16)
    maskv_s = sbg("maskv_s", [128, NT], F32)
    st1 = sbg("st1", [128, 8], F32)
    st2 = sbg("st2", [128, 8], F32)
    st3 = sbg("st3", [128, 8], F32)
    st4 = sbg("st4", [128, 8], F32)
    st5 = sbg("st5", [128, 40], F32)

    ld_id = em.dma(id32[:], ident)
    ld_idb = em.dma(idb[:], ident, eng="gpsimd")
    em.op("vector", lambda e: e.memset(ones_b[:], 1.0))
    em.op("vector", lambda e: e.memset(ones_f[:], 1.0))
    em.op("vector", lambda e: e.memset(ones_p[:], 1.0))
    ones_tok = em.v(lambda e: e.memset(st5[:], 0.0))
    ld_g = em.dma(lnsc_f[:], lnsc)
    ld_bi = em.dma(lnbi_f[:], lnbi)
    ld_bib = em.dma(lnbi_b[:], lnbi, eng="gpsimd")
    ld_mv = em.dma(maskv_s[:], maskv)

    def ln_stats(xa, toks, prev_norm, two, rows=128):
        R = rows
        t_sum = em.v(lambda e: e.reduce_sum(st1[0:R, 0:1], xa, axis=AX.X),
                     after=list(toks) + [prev_norm])
        t_sq = em.s(lambda e: e.activation(xsq[0:R, 0:D], xa, AF.Square,
                                           accum_out=st1[0:R, 1:2]),
                    after=list(toks) + [prev_norm])
        t1 = em.v(lambda e: e.tensor_scalar(st2[0:R, 0:2], st1[0:R, 0:2],
                                            1.0 / D, None, ALU.mult),
                  after=[t_sum, t_sq])
        t2 = em.v(lambda e: e.tensor_tensor(st2[0:R, 2:3], st2[0:R, 0:1],
                                            st2[0:R, 0:1], ALU.mult), after=[t1])
        t3 = em.v(lambda e: e.tensor_tensor(st2[0:R, 1:2], st2[0:R, 1:2],
                                            st2[0:R, 2:3], ALU.subtract), after=[t2])
        q1 = em.s(lambda e: e.activation(st3[0:R, 0:1], st2[0:R, 1:2], AF.Sqrt,
                                         bias=np.float32(EPS), scale=1.0), after=[t3])
        rsv = em.v(lambda e: e.reciprocal(st3[0:R, 0:1], st3[0:R, 0:1]), after=[q1])
        if not two:
            b1 = em.v(lambda e: e.tensor_tensor(st4[0:R, 1:2], st2[0:R, 0:1],
                                                st3[0:R, 0:1], ALU.mult), after=[rsv])
            b2 = em.v(lambda e: e.tensor_scalar(st4[0:R, 1:2], st4[0:R, 1:2],
                                                -1.0, None, ALU.mult), after=[b1])
            return b2
        v1 = em.v(lambda e: e.tensor_tensor(st2[0:R, 3:4], st3[0:R, 0:1],
                                            st3[0:R, 0:1], ALU.mult), after=[rsv])
        v2 = em.v(lambda e: e.tensor_tensor(st2[0:R, 4:5], st2[0:R, 1:2],
                                            st2[0:R, 3:4], ALU.mult), after=[v1])
        q2 = em.s(lambda e: e.activation(st3[0:R, 1:2], st2[0:R, 4:5], AF.Sqrt,
                                         bias=np.float32(EPS), scale=1.0), after=[v2])
        q3 = em.v(lambda e: e.reciprocal(st3[0:R, 1:2], st3[0:R, 1:2]), after=[q2])
        s1 = em.v(lambda e: e.tensor_tensor(st3[0:R, 3:4], st3[0:R, 0:1],
                                            st3[0:R, 1:2], ALU.mult), after=[q3])
        b1 = em.v(lambda e: e.tensor_tensor(st4[0:R, 0:1], st2[0:R, 0:1],
                                            st3[0:R, 3:4], ALU.mult), after=[s1])
        b2 = em.v(lambda e: e.tensor_scalar(st4[0:R, 0:1], st4[0:R, 0:1],
                                            -1.0, None, ALU.mult), after=[b1])
        return b2

    g1 = ExitStack()
    sb1 = lambda n, sh, dty: g1.enter_context(nc.sbuf_tensor(n, sh, dty))
    pb = [Bank(g1.enter_context(nc.psum_tensor(f"pb{i}", [128, 512], F32)))
          for i in range(7)]
    pA = [pb[0], pb[1]]
    pB = [pb[2], pb[3]]
    pO = [pb[4], pb[5]]
    pR = pb[6]
    pTb = Bank(g1.enter_context(nc.psum_tensor("pTb", [128, 1024], BF16)))

    wdqT_s = sb1("wdqT_s", [128, 8 * CD], BF16)
    wdkvT_s = sb1("wdkvT_s", [128, 8 * CD], BF16)
    wuqT_s = sb1("wuqT_s", [128, 4 * D], BF16)
    wukT_s = sb1("wukT_s", [128, 4 * D], BF16)
    wuvT_s = sb1("wuvT_s", [128, 4 * D], BF16)
    kC_s = sb1("kC_s", [128, S], BF16)
    kS_s = sb1("kS_s", [128, S], BF16)
    qC_s = sb1("qC_s", [128, WIN], BF16)
    qS_s = sb1("qS_s", [128, WIN], BF16)
    xta = sb1("xta", [128, D], F32)
    xtb = sb1("xtb", [128, D], F32)
    xsq = sb1("xsq", [128, D], F32)
    htm = sb1("htm", [128, D], BF16)
    hfm = sb1("hfm", [128, 8 * 512], BF16)
    hwin = sb1("hwin", [128, 8 * WIN], BF16)
    ckv = sb1("ckv", [128, 4 * S], BF16)
    cq = sb1("cq", [128, 4 * WIN], BF16)
    kraw = sb1("kraw", [128, 512], BF16)
    ksh = sb1("ksh", [128, 512], BF16)
    ktp = sb1("ktp", [128, 512], BF16)
    krot = sb1("krot", [128, S], BF16)
    qraw = sb1("qraw", [128, WIN], BF16)
    qsh = sb1("qsh", [128, WIN], BF16)
    qtp = sb1("qtp", [128, WIN], BF16)
    qrot = sb1("qrot", [128, WIN], BF16)
    vsb = sb1("vsb", [128, NKT * 520], BF16)
    expS = [sb1(f"expS{i}", [128, 258], BF16) for i in range(3)]
    oacc = sb1("oacc", [128, 8 * WIN], BF16)
    nmax = sb1("nmax", [128, 10], F32)
    nmaxT = sb1("nmaxT", [1, 1280], BF16)
    recip = sb1("recip", [1, 258], F32)
    rbc = sb1("rbc", [64, 258], F32)
    bqkv = sb1("bqkv", [128, 8], F32)
    r1t = sb1("r1t", [128, D], F32)
    x1t = sb1("x1t", [128, D], F32)

    wl = {}
    wl["dq"] = em.dma(wdqT_s[:].rearrange("p (a c) -> p a c", a=8),
                      wdqT.rearrange("(a p) c -> p a c", p=128), eng="gpsimd")
    wl["dkv"] = em.dma(wdkvT_s[:].rearrange("p (a c) -> p a c", a=8),
                       wdkvT.rearrange("(a p) c -> p a c", p=128), eng="gpsimd")
    wl["uq"] = em.dma(wuqT_s[:].rearrange("p (a c) -> p a c", a=4),
                      wuqT.rearrange("(a p) c -> p a c", p=128), eng="gpsimd")
    wl["uk"] = em.dma(wukT_s[:].rearrange("p (a c) -> p a c", a=4),
                      wukT.rearrange("(a p) c -> p a c", p=128), eng="gpsimd")
    wl["uv"] = em.dma(wuvT_s[:].rearrange("p (a c) -> p a c", a=4),
                      wuvT.rearrange("(a p) c -> p a c", p=128), eng="gpsimd")
    wl["kC"] = em.dma(kC_s[:], kC, eng="gpsimd")
    wl["kS"] = em.dma(kS_s[:], kS, eng="gpsimd")
    wl["qC"] = em.dma(qC_s[:], qC, eng="gpsimd")
    wl["qS"] = em.dma(qS_s[:], qS, eng="gpsimd")

    mmtok = None
    for col, w_s, wt in ((0, wdqT_s, wl["dq"]), (4, wdkvT_s, wl["dkv"])):
        for jc in range(4):
            for dc in range(8):
                mmtok = em.t(lambda e, col=col, w_s=w_s, jc=jc, dc=dc:
                    e.matmul(pR.ap[:, col + jc:col + jc + 1],
                             w_s[:, dc * CD + jc * 128:dc * CD + jc * 128 + 128],
                             lnbi_b[:, dc:dc + 1],
                             start=(dc == 0), stop=(dc == 7)),
                    after=[wt, ld_bib] if dc == 0 else ())
    ev_bias = pR.read(em.v(lambda e: e.tensor_copy(bqkv[:], pR.ap[:, 0:8]),
                           after=[mmtok]))
    folds = []
    for w_s, wt in ((wdqT_s, wl["dq"]), (wdkvT_s, wl["dkv"])):
        for dc in range(8):
            folds.append(em.v(lambda e, w_s=w_s, dc=dc:
                e.tensor_scalar(w_s[:, dc * CD:(dc + 1) * CD],
                                w_s[:, dc * CD:(dc + 1) * CD],
                                lnsc_f[:, dc:dc + 1], None, ALU.mult),
                after=[wt, ld_g, ev_bias]))
    fold_tok = folds[-1]

    prev_norm = ones_tok
    last_exp = {0: None, 1: None}
    xbuf_last = {0: [], 1: []}
    htm_last = []

    for b in range(B):
        r1_toks = []
        hwin_evs = []
        for j in range(5):
            rows = QT[j]
            xv = xta if j % 2 == 0 else xtb
            ld = em.dma(xv[0:rows, :], xwin[b, j * 128:j * 128 + rows, :],
                        after=xbuf_last[j % 2])
            xbuf_last[j % 2] = []
            btok = ln_stats(xv[0:rows, :], [ld], prev_norm, two=True,
                            rows=rows)
            h0 = em.v(lambda e, xv=xv, rows=rows:
                e.tensor_scalar(r1t[0:rows, :], xv[0:rows, :], st2[0:rows, 0:1],
                                st3[0:rows, 0:1], ALU.subtract, ALU.mult),
                after=[btok])
            r1 = em.v(lambda e, xv=xv, rows=rows:
                e.tensor_add(r1t[0:rows, :], r1t[0:rows, :], xv[0:rows, :]),
                after=[h0])
            r1_toks.append(em.dma(r1buf[b, j * 128:j * 128 + rows, :],
                                  r1t[0:rows, :], after=[r1]))
            hh = em.s(lambda e, xv=xv, rows=rows:
                e.activation(htm[0:rows, :], xv[0:rows, :], AF.Identity,
                             bias=st4[0:rows, 0:1], scale=st3[0:rows, 3:4]),
                after=[btok, h0] + htm_last)
            htm_last = []
            prev_norm = hh
            xbuf_last[j % 2] = [hh, r1]
            for dc in range(8):
                bk = pTb
                tr = em.t(lambda e, bk=bk, dc=dc, rows=rows:
                    e.transpose(bk.ap[:, 0:rows],
                                htm[0:rows, dc * 128:(dc + 1) * 128],
                                idb[0:rows, 0:rows]),
                    after=[hh, ld_idb] + bk.wtoks())
                ev = bk.read(em.s(lambda e, bk=bk, dc=dc, j=j, rows=rows:
                    e.activation(hwin[:, dc * WIN + j * 128:
                                       dc * WIN + j * 128 + rows],
                                 bk.ap[:, 0:rows], AF.Copy), after=[tr]))
                hwin_evs.append(ev)
                htm_last.append(tr)
        last_exp = {0: None, 1: None}
        cq_evs = []
        for cc in range(4):
            for (q0, qw) in QCH:
                mm = None
                for dc in range(8):
                    mm = em.t(lambda e, cc=cc, dc=dc, q0=q0, qw=qw:
                        e.matmul(pB[0].ap[:, 0:qw],
                                 wdqT_s[:, dc * CD + cc * 128:
                                        dc * CD + cc * 128 + 128],
                                 hwin[:, dc * WIN + q0:dc * WIN + q0 + qw],
                                 start=(dc == 0), stop=(dc == 7)),
                        after=([fold_tok] + hwin_evs + pB[0].wtoks())
                        if dc == 0 else ())
                ev = pB[0].read(em.s(lambda e, cc=cc, q0=q0, qw=qw:
                    e.activation(cq[:, cc * WIN + q0:cc * WIN + q0 + qw],
                                 pB[0].ap[:, 0:qw], AF.Identity,
                                 bias=bqkv[:, cc:cc + 1], scale=1.0),
                    after=[mm, ev_bias]))
                cq_evs.append(ev)

        ckv_evs = []
        for stile in range(8):
            hfm_evs = []
            for i in range(4):
                tile = stile * 4 + i
                xv = xta if i % 2 == 0 else xtb
                ld = em.dma(xv[:], xfull[b, tile * 128:(tile + 1) * 128, :],
                            after=xbuf_last[i % 2])
                xbuf_last[i % 2] = []
                btok = ln_stats(xv[:, :], [ld], prev_norm, two=True)
                hh = em.s(lambda e, xv=xv:
                    e.activation(htm[:], xv[:], AF.Identity,
                                 bias=st4[:, 0:1], scale=st3[:, 3:4]),
                    after=[btok] + htm_last)
                htm_last = []
                prev_norm = hh
                xbuf_last[i % 2] = [hh]
                if b == 0 and stile == 0 and i == 0:
                    if "htm" in dbg_aps:
                        em.dma(dbg_aps["htm"], htm[:], after=[hh])
                    if "sts" in dbg_aps:
                        for si, stt_ in enumerate((st1, st2, st3, st4)):
                            em.dma(dbg_aps["sts"][:, si * 8:(si + 1) * 8],
                                   stt_[:], after=[hh])
                    if "htm" in dbg_aps or "sts" in dbg_aps:
                        em.barrier()
                for dc in range(8):
                    bk = pTb
                    tr = em.t(lambda e, bk=bk, dc=dc:
                        e.transpose(bk.ap[:, 0:128],
                                    htm[:, dc * 128:(dc + 1) * 128], idb[:]),
                        after=[hh, ld_idb] + bk.wtoks())
                    ev = bk.read(em.s(lambda e, bk=bk, dc=dc, i=i:
                        e.activation(hfm[:, dc * 512 + i * 128:
                                          dc * 512 + (i + 1) * 128],
                                     bk.ap[:, 0:128], AF.Copy), after=[tr]))
                    hfm_evs.append(ev)
                    htm_last.append(tr)
            if b == 0 and stile == 0 and "hfm" in dbg_aps:
                em.dma(dbg_aps["hfm"], hfm[:], after=hfm_evs)
                em.barrier()
            for cc in range(4):
                bk = pA[cc % 2]
                mm = None
                for dc in range(8):
                    mm = em.t(lambda e, bk=bk, cc=cc, dc=dc:
                        e.matmul(bk.ap[:, 0:512],
                                 wdkvT_s[:, dc * CD + cc * 128:
                                         dc * CD + cc * 128 + 128],
                                 hfm[:, dc * 512:(dc + 1) * 512],
                                 start=(dc == 0), stop=(dc == 7)),
                        after=([fold_tok] + hfm_evs + bk.wtoks())
                        if dc == 0 else ())
                ev = bk.read(em.s(lambda e, bk=bk, cc=cc, stile=stile:
                    e.activation(ckv[:, cc * S + stile * 512:
                                      cc * S + (stile + 1) * 512],
                                 bk.ap[:, 0:512], AF.Identity,
                                 bias=bqkv[:, 4 + cc:5 + cc], scale=1.0),
                    after=[mm, ev_bias]))
                ckv_evs.append(ev)

        for half in range(2):
            vz = em.v(lambda e: e.memset(
                vsb[:].rearrange("p (k h c) -> p k h c", k=NKT, h=8)
                [:, :, :, 64:65], 1.0), after=ckv_evs)
            v_evs = [vz]
            for kt in range(NKT):
                bk = pA[kt % 2]
                mm = None
                for cc in range(4):
                    mm = em.t(lambda e, bk=bk, kt=kt, cc=cc, half=half:
                        e.matmul(bk.ap[:, 0:512],
                                 ckv[:, cc * S + kt * 128:cc * S + (kt + 1) * 128],
                                 wuvT_s[:, cc * D + half * 512:
                                        cc * D + half * 512 + 512],
                                 start=(cc == 0), stop=(cc == 3)),
                        after=(ckv_evs + [wl["uv"], vz] + bk.wtoks())
                        if cc == 0 else ())
                v_evs.append(bk.read(em.s(lambda e, bk=bk, kt=kt:
                    e.activation(vsb[:].rearrange("p (k h c) -> p k h c",
                                                  k=NKT, h=8)[:, kt, :, 0:64],
                                 bk.ap[:, 0:512], AF.Copy), after=[mm])))

            for pp4 in range(4):
                hp = half * 4 + pp4
                k_toks = []
                for ktc in range(8):
                    bk = pA[ktc % 2]
                    mm = None
                    for cc in range(4):
                        mm = em.t(lambda e, bk=bk, ktc=ktc, cc=cc, hp=hp:
                            e.matmul(bk.ap[:, 0:512],
                                     wukT_s[:, cc * D + hp * 128:
                                            cc * D + hp * 128 + 128],
                                     ckv[:, cc * S + ktc * 512:
                                         cc * S + (ktc + 1) * 512],
                                     start=(cc == 0), stop=(cc == 3)),
                            after=(ckv_evs + [wl["uk"]] + bk.wtoks() + v_evs)
                            if cc == 0 else ())
                    evk = bk.read(em.s(lambda e, bk=bk:
                        e.activation(kraw[:, 0:512], bk.ap[:, 0:512], AF.Copy),
                        after=[mm] + k_toks[-1:]))
                    m1 = em.v(lambda e, ktc=ktc:
                        e.tensor_tensor(ktp[:], kraw[:],
                                        kS_s[:, ktc * 512:(ktc + 1) * 512],
                                        ALU.mult), after=[evk, wl["kS"]])
                    m2 = em.v(lambda e: e.stream_shuffle(ksh[:], ktp[:], SHUF),
                              after=[m1])
                    m3 = em.v(lambda e, ktc=ktc:
                        e.tensor_tensor(ktp[:], kraw[:],
                                        kC_s[:, ktc * 512:(ktc + 1) * 512],
                                        ALU.mult), after=[m2, wl["kC"]])
                    m4 = em.v(lambda e, ktc=ktc:
                        e.tensor_add(krot[:, ktc * 512:(ktc + 1) * 512],
                                     ktp[:], ksh[:]), after=[m3])
                    k_toks.append(m4)
                q_toks = []
                for (q0, qw) in QCH:
                    mm = None
                    for cc in range(4):
                        mm = em.t(lambda e, q0=q0, qw=qw, cc=cc, hp=hp:
                            e.matmul(pB[0].ap[:, 0:qw],
                                     wuqT_s[:, cc * D + hp * 128:
                                            cc * D + hp * 128 + 128],
                                     cq[:, cc * WIN + q0:cc * WIN + q0 + qw],
                                     start=(cc == 0), stop=(cc == 3)),
                            after=(cq_evs + [wl["uq"]] + pB[0].wtoks() +
                                   ([last_exp[0]] if last_exp[0] else []))
                            if cc == 0 else ())
                    q_toks.append(pB[0].read(em.s(
                        lambda e, q0=q0, qw=qw:
                        e.activation(qraw[:, q0:q0 + qw], pB[0].ap[:, 0:qw],
                                     AF.Copy), after=[mm])))
                m1 = em.v(lambda e: e.tensor_tensor(qtp[:], qraw[:], qS_s[:],
                                                    ALU.mult),
                          after=q_toks + [wl["qS"]])
                m2 = em.v(lambda e: e.stream_shuffle(qsh[:], qtp[:], SHUF),
                          after=[m1])
                m3 = em.v(lambda e: e.tensor_tensor(qtp[:], qraw[:], qC_s[:],
                                                    ALU.mult),
                          after=[m2, wl["qC"]])
                qr = em.v(lambda e: e.tensor_add(qrot[:], qtp[:], qsh[:]),
                          after=[m3])

                if b == 0 and hp == 0:
                    did = False
                    for nm, buf, tok in (("krot", krot, k_toks[-1]),
                                         ("qrot", qrot, qr)):
                        if nm in dbg_aps:
                            em.dma(dbg_aps[nm], buf[:], after=[tok])
                            did = True
                    if did:
                        em.barrier()
                mx_last = None
                for hl in range(2):
                    for j in range(5):
                        rows = QT[j]
                        col = hl * 5 + j
                        for ktc in range(8):
                            bk = pA[ktc % 2]
                            mm = em.t(lambda e, bk=bk, hl=hl, j=j, ktc=ktc,
                                             rows=rows:
                                e.matmul(bk.ap[0:rows, 0:512],
                                         qrot[hl * 64:hl * 64 + 64,
                                              j * 128:j * 128 + rows],
                                         krot[hl * 64:hl * 64 + 64,
                                              ktc * 512:(ktc + 1) * 512],
                                         start=True, stop=True),
                                after=[qr] + k_toks + bk.wtoks())
                            rd = bk.read(em.v(
                                lambda e, bk=bk, rows=rows:
                                e.reduce_max(st5[0:rows, 20:21],
                                             bk.ap[0:rows, 0:512], axis=AX.X),
                                after=[mm] + ([mx_last] if mx_last else [])))
                            mx_last = em.v(
                                lambda e, ktc=ktc, col=col, rows=rows:
                                (e.tensor_copy(nmax[0:rows, col:col + 1],
                                               st5[0:rows, 20:21]) if ktc == 0
                                 else e.tensor_max(nmax[0:rows, col:col + 1],
                                                   nmax[0:rows, col:col + 1],
                                                   st5[0:rows, 20:21])),
                                after=[rd])
                ng = em.v(lambda e: e.tensor_scalar(nmax[:], nmax[:], -1.0,
                                                    None, ALU.mult),
                          after=[mx_last])
                if b == 0 and hp == 0 and "nmax" in dbg_aps:
                    em.dma(dbg_aps["nmax"], nmax[:], after=[ng])
                    em.barrier()
                evn = None
                for col in range(10):
                    trn = em.t(lambda e, col=col:
                        e.transpose(pR.ap[0:1, (col % 2) * 128:
                                          (col % 2) * 128 + 128],
                                    nmax[:, col:col + 1], id32[:]),
                        after=[ng] + pR.wtoks())
                    evn = pR.read(em.s(lambda e, col=col:
                        e.activation(nmaxT[0:1, col * 128:(col + 1) * 128],
                                     pR.ap[0:1, (col % 2) * 128:
                                           (col % 2) * 128 + 128],
                                     AF.Copy), after=[trn]))

                av_by_slot = [None, None, None]
                for hl in range(2):
                    h = 2 * hp + hl
                    pOb = pO[hl]
                    for ci, (q0, qw) in enumerate(QCH):
                        for kt in range(NKT):
                            slot = kt % 3
                            bk = pB[kt % 2]
                            war = bk.wtoks()
                            if kt == 0:
                                war += pOb.wtoks()
                            mm1 = em.t(lambda e, bk=bk, hl=hl, kt=kt, q0=q0,
                                              qw=qw:
                                e.matmul(bk.ap[:, 0:qw],
                                         krot[hl * 64:hl * 64 + 64,
                                              kt * 128:(kt + 1) * 128],
                                         qrot[hl * 64:hl * 64 + 64, q0:q0 + qw],
                                         start=True, stop=False),
                                after=[evn] + war)
                            segs = []
                            qpos = q0
                            while qpos < q0 + qw:
                                jt = qpos // 128
                                take = min((jt + 1) * 128 - qpos, q0 + qw - qpos)
                                segs.append((qpos - q0, jt, qpos - jt * 128, take))
                                qpos += take
                            r1tok = None
                            for si, (d0, jt, s0, take) in enumerate(segs):
                                lastseg = si == len(segs) - 1
                                r1tok = em.t(lambda e, bk=bk, d0=d0, jt=jt,
                                                    s0=s0, take=take, hl=hl,
                                                    lastseg=lastseg:
                                    e.matmul(bk.ap[:, d0:d0 + take], ones_b[:],
                                             nmaxT[0:1,
                                                   (hl * 5 + jt) * 128 + s0:
                                                   (hl * 5 + jt) * 128 + s0 + take],
                                             start=False, stop=lastseg),
                                    after=())
                            ex = bk.read(em.s(lambda e, slot=slot, bk=bk, qw=qw:
                                e.activation(expS[slot][:, 0:qw],
                                             bk.ap[:, 0:qw], AF.Exp),
                                after=[r1tok] +
                                      ([av_by_slot[slot]] if av_by_slot[slot]
                                       else [])))
                            last_exp[kt % 2] = ex
                            av = em.t(lambda e, pOb=pOb, kt=kt, slot=slot,
                                             qw=qw, h=h:
                                e.matmul(pOb.ap[0:65, 0:qw],
                                         vsb[:, kt * 520 + (h % 8) * 65:
                                             kt * 520 + (h % 8) * 65 + 65],
                                         expS[slot][:, 0:qw],
                                         start=(kt == 0), stop=(kt == NKT - 1)),
                                after=[ex])
                            av_by_slot[slot] = av
                        rc = pOb.read(em.v(lambda e, pOb=pOb, qw=qw:
                            e.reciprocal(recip[:, 0:qw], pOb.ap[64:65, 0:qw]),
                            after=[av_by_slot[(NKT - 1) % 3]]))
                        bc = em.t(lambda e, qw=qw:
                            e.matmul(pR.ap[0:64, 0:qw], ones_f[:, 0:64],
                                     recip[:, 0:qw], start=True, stop=True),
                            after=[rc] + pR.wtoks())
                        bce2 = pR.read(em.s(lambda e, qw=qw:
                            e.activation(rbc[:, 0:qw], pR.ap[0:64, 0:qw],
                                         AF.Copy), after=[bc]))
                        wr = em.v(lambda e, pOb=pOb, h=h, q0=q0, qw=qw:
                            e.tensor_tensor(
                                oacc[64 * (h % 2):64 * (h % 2) + 64,
                                     (h // 2) * WIN + q0:(h // 2) * WIN + q0 + qw],
                                pOb.ap[0:64, 0:qw], rbc[:, 0:qw], ALU.mult),
                            after=[bce2])
                        pOb.read(wr)

        if b == 0:
            did = False
            for nm, buf in (("ckv", ckv), ("cq", cq), ("oacc", oacc),
                            ("vsb", vsb)):
                if nm in dbg_aps:
                    em.dma(dbg_aps[nm], buf[:],
                           after=[("v", em.cnt["v"]), ("s", em.cnt["s"]),
                                  ("t", em.cnt["t"])])
                    did = True
            if did:
                em.barrier()
        woc = vsb[:, 0:8 * D]
        oacc_tok = [("v", em.cnt["v"]), ("s", em.cnt["s"])]
        ld_wo = em.dma(woc.rearrange("p (a c) -> p a c", a=8),
                       woutTa.rearrange("(a p) c -> p a c", p=128),
                       eng="gpsimd", after=oacc_tok + [("t", em.cnt["t"])])
        for j in range(5):
            rows = QT[j]
            ld_r1 = em.dma(r1t[0:rows, :], r1buf[b, j * 128:j * 128 + rows, :],
                           after=r1_toks)
            adds = []
            for dh in range(2):
                bk = pA[dh]
                mm = None
                for oc in range(8):
                    mm = em.t(lambda e, bk=bk, j=j, dh=dh, oc=oc, rows=rows:
                        e.matmul(bk.ap[0:rows, 0:512],
                                 oacc[:, oc * WIN + j * 128:
                                      oc * WIN + j * 128 + rows],
                                 woc[:, oc * D + dh * 512:oc * D + dh * 512 + 512],
                                 start=(oc == 0), stop=(oc == 7)),
                        after=(oacc_tok + [ld_wo] + bk.wtoks())
                        if oc == 0 else ())
                adds.append(bk.read(em.v(lambda e, bk=bk, dh=dh, rows=rows:
                    e.tensor_add(x1t[0:rows, dh * 512:(dh + 1) * 512],
                                 bk.ap[0:rows, 0:512],
                                 r1t[0:rows, dh * 512:(dh + 1) * 512]),
                    after=[mm, ld_r1])))
            em.dma(x1buf[b, j * 128:j * 128 + rows, :], x1t[0:rows, :],
                   after=adds)

    em.barrier()
    em.flush()
    g1.close()

    g2 = ExitStack()
    sb2 = lambda n, sh, dty: g2.enter_context(nc.sbuf_tensor(n, sh, dty))
    pg = [Bank(g2.enter_context(nc.psum_tensor(f"pg{i}", [128, 512], F32)))
          for i in range(7)]
    pF = [pg[0], pg[1]]
    pW = [pg[2], pg[3]]
    pSV = pg[4]
    pSb = pg[5]
    pM = pg[6]
    pTb2 = Bank(g2.enter_context(nc.psum_tensor("pTb2", [128, 1024], BF16)))

    xqT = sb2("xqT", [128, 8 * HTW], BF16)
    bigb = sb2("bigb", [128, 32 * 1024], BF16)
    xq2c = sb2("xq2c", [128, 32 * 512], BF16)
    qkv_c = [sb2(f"qkv_c{i}", [128, TP], BF16) for i in range(3)]
    w1sl = [sb2(f"w1sl{i}", [128, 1024], BF16) for i in range(2)]
    cvt = sb2("cvt", [128, TP + 8], F32)
    cvu = sb2("cvu", [128, TP + 8], F32)

    cw0s = sb2("cw0s", [128, 32], F32)
    cw1s = sb2("cw1s", [128, 32], F32)
    cw2s = sb2("cw2s", [128, 32], F32)
    cbs = sb2("cbs", [128, 32], F32)
    amaxp = sb2("amaxp", [128, 2 * 68], F32)
    akr = sb2("akr", [68, 32], F32)
    s2kr = sb2("s2kr", [68, 32], F32)
    ivkr = sb2("ivkr", [68, 32], F32)
    s2fm = sb2("s2fm", [1, 512], F32)
    iv_pu = sb2("iv_pu", [128, NT], F32)
    xq2i = sb2("xq2i", [128, 512], I32)
    h0f = sb2("h0f", [128, D], F32)
    xint = sb2("xint", [128, D], I32)
    xqd = sb2("xqd", [128, D], BF16)
    qs_s = sb2("qs_s", [128, NT], F32)
    qs_iv = sb2("qs_iv", [128, NT], F32)
    xt2 = sb2("xt2", [128, D], F32)
    xsqb = sb2("xsqb", [128, D], F32)
    htm2 = sb2("htm2", [128, D], BF16)
    bmask_s = sb2("bmask_s", [128, 128], F32)
    binb_s = sb2("binb_s", [128, 24], F32)
    boutb_s = sb2("boutb_s", [128, 8], F32)
    boutf_s = sb2("boutf_s", [1, D], BF16)
    winsl = [sb2(f"winsl{i}", [128, 8 * 128], BF16) for i in range(2)]
    vtm = sb2("vtm", [128, 128], BF16)
    pnt = sb2("pnt", [128, 128], BF16)
    pnr = sb2("pnr", [128, 128], BF16)
    smc = sb2("smc", [128, 8], F32)

    wld = cvt
    wq_i = xint
    wq_f = cvu
    wq_b = xqd
    x1t2_alias_check = True
    ld_bm = em.dma(bmask_s[:], bmask)
    ld_bb = em.dma(binb_s[:], binb)
    ld_bo = em.dma(boutb_s[:], boutb)
    ld_bf = em.dma(boutf_s[:], boutf, eng="gpsimd")
    cv_toks = [
        em.dma(cw0s[:], cw0),
        em.dma(cw1s[:], cw1),
        em.dma(cw2s[:], cw2),
        em.dma(cbs[:], cbv),
    ]

    def quant_weights(srcT, dstT, ntile, width, c0):
        toks = []
        prev = None
        ncc = width // 1024
        for i in range(ntile):
            for cc in range(ncc):
                ld = em.dma(wld[:, 0:1024], srcT[i * 128:(i + 1) * 128,
                                         cc * 1024:(cc + 1) * 1024],
                            after=[prev] if prev else [])
                rs = em.v(lambda e: e.tensor_reduce(
                    st5[:, 39:40], wld[:, 0:1024], axis=AX.X, op=ALU.add,
                    apply_absolute_value=True), after=[ld])
                rs = em.v(lambda e, i=i, cc=cc:
                    (e.tensor_copy(st5[:, i:i + 1], st5[:, 39:40]) if cc == 0
                     else e.tensor_add(st5[:, i:i + 1], st5[:, i:i + 1],
                                       st5[:, 39:40])), after=[rs])
                toks.append(rs)
                prev = rs
        smm = em.t(lambda e, ntile=ntile:
            e.matmul(pM.ap[0:1, 0:ntile], ones_p[:], st5[:, 0:ntile],
                     start=True, stop=True), after=toks + pM.wtoks())
        tot = pM.read(em.v(lambda e, ntile=ntile:
            e.reduce_sum(st5[0:1, 36:37], pM.ap[0:1, 0:ntile], axis=AX.X),
            after=[smm]))
        mn = em.v(lambda e, ntile=ntile, width=width:
            e.tensor_scalar(st5[0:1, 37:38], st5[0:1, 36:37],
                            1.0 / (ntile * 128 * width), 1e-5,
                            ALU.mult, ALU.max), after=[tot])
        rec = em.v(lambda e: e.reciprocal(st5[0:1, 38:39], st5[0:1, 37:38]),
                   after=[mn])
        bc = em.t(lambda e: e.matmul(pM.ap[:, 0:2], ones_f[:], st5[0:1, 37:39],
                                     start=True, stop=True),
                  after=[rec] + pM.wtoks())
        bce = pM.read(em.v(lambda e, c0=c0:
            e.tensor_copy(st4[:, c0:c0 + 2], pM.ap[:, 0:2]), after=[bc]))
        prev = None
        for i in range(ntile):
            for cc in range(ncc):
                ld = em.dma(wld[:, 0:1024], srcT[i * 128:(i + 1) * 128,
                                         cc * 1024:(cc + 1) * 1024],
                            after=[toks[-1]] if prev is None else [prev])
                q1 = em.v(lambda e, c0=c0:
                    e.tensor_scalar(wq_i[:], wld[:, 0:1024],
                                    st4[:, c0 + 1:c0 + 2],
                                    None, ALU.mult), after=[ld, bce])
                q2 = em.v(lambda e: e.tensor_scalar(wq_f[:, 0:1024], wq_i[:],
                                                    1.0, -1.0, ALU.min,
                                                    ALU.max), after=[q1])
                q3 = em.v(lambda e, c0=c0:
                    e.tensor_scalar(wq_b[:], wq_f[:, 0:1024],
                                    st4[:, c0:c0 + 1],
                                    None, ALU.mult), after=[q2])
                prev = em.dma(dstT[i * 128:(i + 1) * 128,
                                   cc * 1024:(cc + 1) * 1024], wq_b[:],
                              after=[q3])
        return prev

    wq1_tok = quant_weights(w1T, w1qc, 8, DFF, 2)
    wq2_tok = quant_weights(w2T, w2qc, 32, D, 4)

    gz = []
    for dc in range(8):
        gz.append(em.v(lambda e, dc=dc:
            e.memset(xqT[:, dc * HTW:dc * HTW + 4], 0.0)))
        gz.append(em.v(lambda e, dc=dc:
            e.memset(xqT[:, dc * HTW + 4 + TP:dc * HTW + 8 + TP], 0.0)))
    em.barrier()
    em.flush()

    def sb3(sbap):
        return sbap

    def plain_ln_to(dst_fn, srcrow, quant):
        prev = ones_tok
        out_toks = []
        xt2_last = []
        for j in range(NT):
            ap, nval = srcrow(j * 128, 128)
            if nval < 128:
                mz = em.v(lambda e: e.memset(xt2[:], 0.0), after=xt2_last)
                ld = [em.dma(xt2[0:nval, :], ap, after=xt2_last + [mz])]
            else:
                ld = [em.dma(xt2[:], ap, after=xt2_last)]
            xt2_last = []
            tsum = em.v(lambda e: e.reduce_sum(st1[:, 0:1], xt2[:], axis=AX.X),
                        after=ld + [prev])
            tsq = em.s(lambda e: e.activation(xsqb[:], xt2[:], AF.Square,
                                              accum_out=st1[:, 1:2]),
                       after=ld + [prev])
            t1 = em.v(lambda e: e.tensor_scalar(st2[:, 0:2], st1[:, 0:2],
                                                1.0 / D, None, ALU.mult),
                      after=[tsum, tsq])
            t2 = em.v(lambda e: e.tensor_tensor(st2[:, 2:3], st2[:, 0:1],
                                                st2[:, 0:1], ALU.mult),
                      after=[t1])
            t3 = em.v(lambda e: e.tensor_tensor(st2[:, 1:2], st2[:, 1:2],
                                                st2[:, 2:3], ALU.subtract),
                      after=[t2])
            q1 = em.s(lambda e: e.activation(st3[:, 0:1], st2[:, 1:2], AF.Sqrt,
                                             bias=np.float32(EPS), scale=1.0),
                      after=[t3])
            rsv = em.v(lambda e: e.reciprocal(st3[:, 0:1], st3[:, 0:1]),
                       after=[q1])
            if not quant:
                nb = em.v(lambda e: e.tensor_tensor(st3[:, 1:2], st2[:, 0:1],
                                                    st3[:, 0:1], ALU.mult),
                          after=[rsv])
                nb2 = em.v(lambda e: e.tensor_scalar(st3[:, 1:2], st3[:, 1:2],
                                                     -1.0, None, ALU.mult),
                          after=[nb])
                hh = em.s(lambda e: e.activation(htm2[:], xt2[:], AF.Identity,
                                                 bias=st3[:, 1:2],
                                                 scale=st3[:, 0:1]),
                          after=[nb2])
                prev = hh
                xt2_last = [hh]
                out_toks.append(dst_fn(j, hh))
                continue
            h0 = em.v(lambda e: e.tensor_scalar(h0f[:], xt2[:], st2[:, 0:1],
                                                st3[:, 0:1], ALU.subtract,
                                                ALU.mult), after=[rsv])
            xt2_last = [h0]
            am = em.v(lambda e, j=j: e.reduce_max(qs_s[:, j:j + 1], h0f[:],
                                                  axis=AX.X,
                                                  apply_absolute_value=True),
                      after=[h0])
            c1 = em.v(lambda e, j=j: e.tensor_scalar(qs_s[:, j:j + 1],
                                                     qs_s[:, j:j + 1], 1e-5,
                                                     None, ALU.max), after=[am])
            c2 = em.v(lambda e, j=j: e.tensor_scalar(qs_iv[:, j:j + 1],
                                                     qs_s[:, j:j + 1],
                                                     1.0 / 127.0,
                                                     maskv_s[:, j:j + 1],
                                                     ALU.mult, ALU.mult),
                      after=[c1, ld_mv])
            c3 = em.v(lambda e, j=j: e.reciprocal(qs_s[:, j:j + 1],
                                                  qs_s[:, j:j + 1]), after=[c2])
            c4 = em.v(lambda e, j=j: e.tensor_scalar(qs_s[:, j:j + 1],
                                                     qs_s[:, j:j + 1], 127.0,
                                                     None, ALU.mult), after=[c3])
            qi = em.v(lambda e, j=j: e.tensor_scalar(xint[:], h0f[:],
                                                     qs_s[:, j:j + 1], None,
                                                     ALU.mult), after=[c4])
            qd = em.s(lambda e, j=j: e.activation(xqd[:], xint[:], AF.Identity,
                                                  bias=0.0,
                                                  scale=qs_iv[:, j:j + 1]),
                      after=[qi])
            prev = qd
            out_toks.append(dst_fn(j, qd))
        return out_toks

    def transpose_to_xqT(j, src_tile, src_tok):
        evs = []
        for dc in range(8):
            bk = pTb2
            tr = em.t(lambda e, bk=bk, dc=dc:
                e.transpose(bk.ap[:, 0:128],
                            src_tile[:, dc * 128:(dc + 1) * 128], idb[:]),
                after=[src_tok, ld_idb] + bk.wtoks())
            evs.append(bk.read(em.s(lambda e, bk=bk, dc=dc, j=j:
                e.activation(xqT[:, dc * HTW + 4 + j * 128:
                                  dc * HTW + 4 + (j + 1) * 128],
                             bk.ap[:, 0:128], AF.Copy), after=[tr])))
        return evs[-1]

    def ffn(srcrow, dst, dbgname=None):
        f1 = plain_ln_to(lambda j, tok: transpose_to_xqT(j, xqd, tok),
                         srcrow, quant=True)
        em.barrier()
        amz = em.v(lambda e: e.memset(amaxp[:], 0.0))
        am_last = amz
        hs_toks = []
        w1_mm_last = {0: None, 1: None}
        for ct in range(32):
            ldw = em.dma(w1sl[ct % 2][:].rearrange("p (a c) -> p a c", a=8),
                         w1qc.rearrange("(a p) f -> p a f", p=128)
                         [:, :, ct * 128:(ct + 1) * 128],
                         after=[wq1_tok, w1_mm_last[ct % 2]])
            nch = [(0, 512), (512, 512), (1024, 512), (1536, 512), (2048, 136)]
            ev_toks = []
            for (n0, nw) in nch:
                bk = pF[(n0 // 512) % 2]
                mm = None
                for dc in range(8):
                    mm = em.t(lambda e, bk=bk, ct=ct, dc=dc, n0=n0, nw=nw:
                        e.matmul(bk.ap[:, 0:nw],
                                 w1sl[ct % 2][:, dc * 128:(dc + 1) * 128],
                                 xqT[:, dc * HTW + n0:dc * HTW + n0 + nw],
                                 start=(dc == 0), stop=(dc == 7)),
                        after=([ldw] + bk.wtoks()) if dc == 0 else ())
                ev_toks.append(bk.read(em.v(lambda e, bk=bk, n0=n0, nw=nw:
                    e.tensor_copy(cvt[:, n0:n0 + nw], bk.ap[:, 0:nw]),
                    after=[mm] + hs_toks[-1:])))
                w1_mm_last[ct % 2] = mm
            c1 = em.v(lambda e, ct=ct:
                e.tensor_scalar(cvu[:, 0:TP], cvt[:, 4:4 + TP],
                                cw1s[:, ct:ct + 1], cbs[:, ct:ct + 1],
                                ALU.mult, ALU.add),
                after=ev_toks + cv_toks)
            c2 = em.v(lambda e, ct=ct:
                e.scalar_tensor_tensor(cvu[:, 0:TP], cvt[:, 0:TP],
                                       cw0s[:, ct:ct + 1], cvu[:, 0:TP],
                                       ALU.mult, ALU.add), after=[c1])
            c3 = em.v(lambda e, ct=ct:
                e.scalar_tensor_tensor(cvu[:, 0:TP], cvt[:, 8:8 + TP],
                                       cw2s[:, ct:ct + 1], cvu[:, 0:TP],
                                       ALU.mult, ALU.add), after=[c2])
            sl = em.s(lambda e: e.activation(cvt[:, 0:TP], cvu[:, 0:TP],
                                             AF.Silu),
                      after=[c3] + hs_toks[-1:])
            amx = em.v(lambda e: e.reduce_max(
                amaxp[:, 0:68],
                cvt[:, 0:TP].rearrange("p (k u) -> p k u", u=32),
                axis=AX.X, apply_absolute_value=True), after=[sl, am_last])
            am_last = em.v(lambda e: e.tensor_max(amaxp[:, 68:136],
                                                  amaxp[:, 68:136],
                                                  amaxp[:, 0:68]), after=[amx])
            hs_toks.append(em.dma(hsbuf[ct * 128:(ct + 1) * 128, :],
                                  cvt[:, 0:TP], after=[sl, am_last]))
        trA = em.t(lambda e: e.transpose(pM.ap[0:68, 0:128], amaxp[:, 68:136],
                                         id32[:]),
                   after=[am_last, ld_id] + pM.wtoks())
        evA = pM.read(em.v(lambda e: e.tensor_copy(akr2[:, 0:128],
                                               pM.ap[0:68, 0:128]),
            after=[trA]))
        red = em.v(lambda e: e.tensor_reduce(
            akr[:, 0:32], akr2[:, 0:128].rearrange("k (g r) -> k r g", g=4),
            axis=AX.X, op=ALU.max), after=[evA])
        sc1 = em.v(lambda e: e.tensor_scalar(akr[:, 0:32], akr[:, 0:32],
                                             1e-5, None, ALU.max), after=[red])
        sc2 = em.v(lambda e: e.tensor_scalar(ivkr[:, 0:32], akr[:, 0:32],
                                             1.0 / 127.0, None, ALU.mult),
                   after=[sc1])
        sc3 = em.v(lambda e: e.reciprocal(s2kr[:, 0:32], akr[:, 0:32]),
                   after=[sc2])
        sc4 = em.v(lambda e: e.tensor_scalar(s2kr[:, 0:32], s2kr[:, 0:32],
                                             127.0, None, ALU.mult), after=[sc3])
        d1 = em.dma(sscr[0:TP].rearrange("(k r) -> k r", k=68), s2kr[:, 0:32],
                    after=[sc4])
        d3 = em.dma(sscr[TP:2 * TP].rearrange("(k r) -> k r", k=68),
                    ivkr[:, 0:32], after=[sc2])
        d4 = em.dma(iv_pu[:, 0:NT], sscr[TP:2 * TP].rearrange("(u p) -> p u",
                                                              p=128),
                    after=[d3], allow_slow_non_contiguous=True)
        ldw2 = em.dma(bigb[:].rearrange("p (a c) -> p a c", a=32),
                      w2qc.rearrange("(a p) c -> p a c", p=128),
                      eng="gpsimd",
                      after=[wq2_tok, ("v", em.cnt["v"]), ("s", em.cnt["s"]),
                             ("t", em.cnt["t"])])
        prev_w2 = None
        prev_epis = []
        for tc, (t0, tw) in enumerate(
                [(0, 512), (512, 512), (1024, 512), (1536, 512), (2048, 128)]):
            d2 = em.dma(s2fm[:, 0:tw], sscr[t0:t0 + tw][None, :],
                        after=[d1, ("t", em.cnt["t"])])
            sbc = em.t(lambda e, tw=tw:
                e.matmul(pSb.ap[:, 0:tw], ones_f[:], s2fm[:, 0:tw],
                         start=True, stop=True), after=[d2] + pSb.wtoks())
            q_toks = []
            for ct in range(32):
                ldh = em.dma(cvu[:, 0:tw],
                             hsbuf[ct * 128:(ct + 1) * 128, t0:t0 + tw],
                             after=[hs_toks[ct]] +
                                   ([q_toks[-1]] if q_toks else []))
                qa = pSb.read(em.v(lambda e, ct=ct, tw=tw:
                    e.tensor_tensor(xq2i[:, 0:tw], cvu[:, 0:tw],
                                    pSb.ap[:, 0:tw], ALU.mult),
                    after=[ldh, sbc, prev_w2] +
                          ([q_toks[-1]] if q_toks else [])))
                qb = em.s(lambda e, ct=ct, tw=tw:
                    e.activation(xq2c[:, ct * 512:ct * 512 + tw],
                                 xq2i[:, 0:tw], AF.Copy), after=[qa])
                q_toks.append(qb)
            nu = (tw + 127) // 128
            for u in range(nu):
                uw = min(128, tw - u * 128)
                tglob = t0 + u * 128
                jt = tglob // 128
                epis = []
                for dh in range(2):
                    bk = pW[dh]
                    mm = None
                    for ct in range(32):
                        mm = em.t(lambda e, bk=bk, ct=ct, u=u, dh=dh, uw=uw:
                            e.matmul(bk.ap[0:uw, 0:512],
                                     xq2c[:, ct * 512 + u * 128:
                                          ct * 512 + u * 128 + uw],
                                     bigb[:, ct * 1024 + dh * 512:
                                          ct * 1024 + (dh + 1) * 512],
                                     start=(ct == 0), stop=(ct == 31)),
                            after=([ldw2, q_toks[ct]] + bk.wtoks())
                            if ct == 0 else [q_toks[ct]])
                    w2_mm_last = mm
                    if dh == 0:
                        rap, rnval = srcrow(tglob, uw)
                        if rnval < uw:
                            mz = em.v(lambda e: e.memset(xt2[:], 0.0),
                                      after=prev_epis)
                            ldr = [em.dma(xt2[0:rnval, :], rap,
                                          after=prev_epis + [mz])]
                        else:
                            ldr = [em.dma(xt2[0:rnval, :], rap,
                                          after=prev_epis)]
                    epi = bk.read(em.v(lambda e, bk=bk, dh=dh, uw=uw, jt=jt:
                        e.scalar_tensor_tensor(
                            x1t2[0:uw, dh * 512:(dh + 1) * 512],
                            bk.ap[0:uw, 0:512], iv_pu[:, jt:jt + 1],
                            xt2[0:uw, dh * 512:(dh + 1) * 512],
                            ALU.mult, ALU.add),
                        after=[mm, d4] + ldr))
                    epis.append(epi)
                prev_epis = list(epis)
                prev_w2 = w2_mm_last
                if dst[0] == "dram":
                    em.dma(dst[1][tglob:tglob + uw, :], x1t2[0:uw, :],
                           after=epis)
                else:
                    lo, hi = max(8, tglob), min(2056, tglob + uw)
                    if lo < hi:
                        yv = yout.rearrange("b s d -> s b d")
                        em.dma(yv[(lo - 8) // 4:(hi - 8) // 4],
                               sb3(x1t2[lo - tglob:hi - tglob, :]),
                               after=epis)
        em.barrier()
        if dbgname and dbgname in dbg_aps and dst[0] == "dram":
            em.dma(dbg_aps[dbgname], dst[1], after=[("dsy", em.cnt["dsy"])])
            em.barrier()

    akr2 = sb2("akr2", [68, 128], F32)
    x1t2 = h0f

    x1v3 = x1buf.rearrange("b s d -> s b d")
    if "x1" in dbg_aps:
        em.dma(dbg_aps["x1"], x1buf, after=[("dsy", em.cnt["dsy"])])
        em.barrier()
    if "ffn1" in phases:
        ffn(lambda t0, rows: (x1v3[t0 // 4:min(t0 + rows, 4 * WIN) // 4],
                              max(0, min(t0 + rows, 4 * WIN) - t0)),
            ("dram", x2buf), dbgname="x2")
    em.flush()

    if "attnb" in phases:
        lnn = plain_ln_to(lambda j, tok: transpose_to_xqT(j, htm2, tok),
                          lambda t0, rows: (x2buf[t0:t0 + rows, :], rows),
                          quant=False)
        em.barrier()
        WOB0 = 8 * TP
        ld_wob = em.dma(bigb[:, WOB0:WOB0 + 8 * D].rearrange("p (a c) -> p a c", a=8),
                        woutTb.rearrange("(a p) c -> p a c", p=128), eng="gpsimd",
                        after=[("s", em.cnt["s"]), ("t", em.cnt["t"])])
        winv = winT.rearrange("(a p) c -> p a c", p=128)
        for pp in range(8):
            ch_evs = []
            for which in range(3):
                roff = which * D + pp * 128
                ldwin = em.dma(winsl[which % 2][:].rearrange("p (a c) -> p a c", a=8),
                               winv[:, :, roff:roff + 128], eng="gpsimd",
                               after=[("t", em.cnt["t"])])
                for (n0, nw) in [(0, 512), (512, 512), (1024, 512), (1536, 512),
                                 (2048, 128)]:
                    bk = pW[which % 2]
                    mm = None
                    for dc in range(8):
                        mm = em.t(lambda e, bk=bk, which=which, dc=dc, n0=n0, nw=nw:
                            e.matmul(bk.ap[:, 0:nw],
                                     winsl[which % 2][:, dc * 128:(dc + 1) * 128],
                                     xqT[:, dc * HTW + 4 + n0:dc * HTW + 4 + n0 + nw],
                                     start=(dc == 0), stop=(dc == 7)),
                            after=([ldwin] + lnn[-1:] + bk.wtoks())
                            if dc == 0 else ())
                    if which == 0:
                        ev = bk.read(em.v(lambda e, which=which, pp=pp, n0=n0, nw=nw,
                                                 bk=bk:
                            e.tensor_scalar(qkv_c[which][:, n0:n0 + nw],
                                            bk.ap[:, 0:nw],
                                            binb_s[:, which * 8 + pp:
                                                   which * 8 + pp + 1],
                                            0.125, ALU.add, ALU.mult),
                            after=[mm, ld_bb]))
                    else:
                        ev = bk.read(em.v(lambda e, which=which, pp=pp, n0=n0, nw=nw,
                                                 bk=bk:
                            e.tensor_scalar(qkv_c[which][:, n0:n0 + nw],
                                            bk.ap[:, 0:nw],
                                            binb_s[:, which * 8 + pp:
                                                   which * 8 + pp + 1],
                                            None, ALU.add),
                            after=[mm, ld_bb]))
                    ch_evs.append(ev)
            if pp == 0:
                did = False
                for wn in range(3):
                    nm = "qkb%d" % wn
                    if nm in dbg_aps:
                        em.dma(dbg_aps[nm], qkv_c[wn][:], after=ch_evs)
                        did = True
                if "lnb" in dbg_aps:
                    em.dma(dbg_aps["lnb"], xqT[:, 4:4 + TP], after=ch_evs)
                    did = True
                if did:
                    em.barrier()
            for j in range(NT):
                vt = em.t(lambda e, j=j:
                    e.transpose(pTb2.ap[:, 0:128],
                                qkv_c[2][:, j * 128:(j + 1) * 128], idb[:]),
                    after=ch_evs + pTb2.wtoks())
                vte = pTb2.read(em.s(lambda e:
                    e.activation(vtm[:], pTb2.ap[:, 0:128], AF.Copy), after=[vt]))
                for hl in range(2):
                    sm = em.t(lambda e, j=j, hl=hl:
                        e.matmul(pSV.ap[:, 0:128],
                                 qkv_c[0][hl * 64:hl * 64 + 64, j * 128:(j + 1) * 128],
                                 qkv_c[1][hl * 64:hl * 64 + 64, j * 128:(j + 1) * 128],
                                 start=True, stop=True),
                        after=ch_evs + pSV.wtoks())
                    ms = pSV.read(em.v(lambda e:
                        e.tensor_add(cvt[:, 0:128], pSV.ap[:, 0:128], bmask_s[:]),
                        after=[sm, ld_bm]))
                    mx = em.v(lambda e: e.reduce_max(smc[:, 0:1], cvt[:, 0:128],
                                                     axis=AX.X, negate=True),
                              after=[ms])
                    ex = em.s(lambda e: e.activation(pnr[:], cvt[:, 0:128], AF.Exp,
                                                     bias=smc[:, 0:1], scale=1.0,
                                                     accum_out=smc[:, 1:2]),
                              after=[mx])
                    rc = em.v(lambda e: e.reciprocal(smc[:, 2:3], smc[:, 1:2]),
                              after=[ex])
                    pn = em.v(lambda e: e.tensor_scalar(pnr[:], pnr[:], smc[:, 2:3],
                                                        None, ALU.mult), after=[rc])
                    tp_ = em.t(lambda e: e.transpose(pTb2.ap[:, 0:128], pnr[:],
                                                     idb[:]),
                               after=[pn, vte] + pTb2.wtoks())
                    pte = pTb2.read(em.s(lambda e:
                        e.activation(pnt[:], pTb2.ap[:, 0:128], AF.Copy),
                        after=[tp_]))
                    av = em.t(lambda e, hl=hl:
                        e.matmul(pSV.ap[hl * 64:hl * 64 + 64, 0:128],
                                 vtm[:, hl * 64:hl * 64 + 64], pnt[:],
                                 start=True, stop=True),
                        after=[pte] + pSV.wtoks())
                    oe = pSV.read(em.s(lambda e, pp=pp, j=j, hl=hl:
                        e.activation(bigb[:, pp * TP + j * 128:pp * TP + (j + 1) * 128]
                                     [hl * 64:hl * 64 + 64, :],
                                     pSV.ap[hl * 64:hl * 64 + 64, 0:128],
                                     AF.Copy), after=[av]))
            if pp == 0 and "ob0" in dbg_aps:
                em.barrier()
                em.dma(dbg_aps["ob0"], bigb[:, 0:TP],
                       after=[("s", em.cnt["s"])])
                em.barrier()
        em.barrier()
        prev_eps = []
        for j in range(NT):
            ldr = em.dma(xt2[:], x2buf[j * 128:(j + 1) * 128, :], after=prev_eps)
            epis = []
            for dh in range(2):
                bk = pW[dh]
                mm = None
                for oc in range(8):
                    mm = em.t(lambda e, bk=bk, j=j, dh=dh, oc=oc:
                        e.matmul(bk.ap[:, 0:512],
                                 bigb[:, oc * TP + j * 128:oc * TP + (j + 1) * 128],
                                 bigb[:, WOB0 + oc * D + dh * 512:
                                      WOB0 + oc * D + dh * 512 + 512],
                                 start=(oc == 0), stop=False),
                        after=([ld_wob, ("s", em.cnt["s"])] + bk.wtoks())
                        if oc == 0 else ())
                mm = em.t(lambda e, bk=bk, dh=dh:
                    e.matmul(bk.ap[:, 0:512], ones_b[0:1, 0:128],
                             boutf_s[0:1, dh * 512:(dh + 1) * 512],
                             start=False, stop=True),
                    after=[ld_bf])
                ep = bk.read(em.v(lambda e, bk=bk, dh=dh:
                    e.tensor_add(x1t2[:, dh * 512:(dh + 1) * 512],
                                 bk.ap[:, 0:512],
                                 xt2[:, dh * 512:(dh + 1) * 512]),
                    after=[mm, ldr]))
                epis.append(ep)
            prev_eps = list(epis)
            em.dma(x3buf[j * 128:(j + 1) * 128, :], x1t2[:], after=epis)
        em.barrier()
        if "x3" in dbg_aps:
            em.dma(dbg_aps["x3"], x3buf, after=[("dsy", em.cnt["dsy"])])
            em.barrier()
        em.flush()

    if "ffn2" in phases:
        ffn(lambda t0, rows: (x3buf[t0:t0 + rows, :], rows), ("yout",))
    for eng in Em.ENGS:
        em.waits(eng, [("dsy", em.cnt["dsy"]), ("dgp", em.cnt["dgp"]),
                       ("v", em.cnt["v"]), ("s", em.cnt["s"]),
                       ("t", em.cnt["t"])])
    em.flush()
    g2.close()
    octx.close()
    return nc


def _host_constants():
    inv_freq = 1.0 / 10000.0 ** (np.arange(0, DH, 2, dtype=np.float32) / DH)
    t = np.arange(S, dtype=np.float32)
    freqs = np.outer(t, inv_freq)
    cos = np.concatenate([np.cos(freqs), np.cos(freqs)], -1)[:, ::2].T
    sin = np.concatenate([np.sin(freqs), np.sin(freqs)], -1)[:, ::2].T
    C64 = np.zeros((64, S), np.float32)
    St64 = np.zeros((64, S), np.float32)
    for i in range(32):
        C64[2 * i] = cos[i]
        C64[2 * i + 1] = cos[i]
        St64[2 * i] = sin[i]
        St64[2 * i + 1] = -sin[i]
    kC = np.concatenate([C64, C64], 0)
    kS = np.concatenate([St64, St64], 0)
    ident = np.eye(128, dtype=np.float32)
    bmask = np.full((128, 128), -1e30, np.float32)
    for g in range(32):
        bmask[4 * g:4 * g + 4, 4 * g:4 * g + 4] = 0.0
    return kC, kS, ident, bmask


def kernel(**inputs):
    x = np.asarray(inputs["L0_attn_a_x"], np.float32)
    kC, kS, ident, bmask = _host_constants()
    asc = np.ascontiguousarray
    base = {
        "xfull": x,
        "wdqT": asc(np.asarray(inputs["L0_attn_a_w_down_q"], np.float32).T),
        "wdkvT": asc(np.asarray(inputs["L0_attn_a_w_down_kv"], np.float32).T),
        "wuqT": asc(np.asarray(inputs["L0_attn_a_w_up_q"], np.float32).T),
        "wukT": asc(np.asarray(inputs["L0_attn_a_w_up_k"], np.float32).T),
        "wuvT": asc(np.asarray(inputs["L0_attn_a_w_up_v"], np.float32).T),
        "woutTa": asc(np.asarray(inputs["L0_attn_a_w_out"], np.float32).T),
        "lnsc": asc(np.asarray(inputs["L0_attn_a_ln_scale"],
                                np.float32).reshape(8, 128).T),
        "lnbi": asc(np.asarray(inputs["L0_attn_a_ln_bias"],
                               np.float32).reshape(8, 128).T),
        "winT": asc(np.asarray(inputs["L0_attn_b_in_proj_weight"], np.float32).T),
        "binb": asc(np.asarray(inputs["L0_attn_b_in_proj_bias"],
                               np.float32).reshape(24, 128).T),
        "woutTb": asc(np.asarray(inputs["L0_attn_b_out_proj_weight"],
                                 np.float32).T),
        "boutb": asc(np.asarray(inputs["L0_attn_b_out_proj_bias"],
                                np.float32).reshape(8, 128).T),
        "boutf": asc(np.asarray(inputs["L0_attn_b_out_proj_bias"],
                                np.float32).reshape(1, D)),
        "w1T": asc(np.asarray(inputs["L0_ffn_w1"], np.float32).T),
        "w2T": asc(np.asarray(inputs["L0_ffn_w2"], np.float32).T),
        "cw0": asc(np.asarray(inputs["L0_ffn_conv_w"],
                              np.float32)[:, 0, 0].reshape(32, 128).T),
        "cw1": asc(np.asarray(inputs["L0_ffn_conv_w"],
                              np.float32)[:, 0, 1].reshape(32, 128).T),
        "cw2": asc(np.asarray(inputs["L0_ffn_conv_w"],
                              np.float32)[:, 0, 2].reshape(32, 128).T),
        "cbv": asc(np.asarray(inputs["L0_ffn_conv_b"],
                              np.float32).reshape(32, 128).T),
        "ident": ident, "kC": kC, "kS": kS, "bmask": bmask,
    }
    xpad = np.zeros((B, S + 4, D), np.float32)
    xpad[:, 2:2 + S, :] = x
    kCp = np.zeros((128, S + 4), np.float32)
    kSp = np.zeros((128, S + 4), np.float32)
    kCp[:, 2:2 + S] = kC / 8.0
    kSp[:, 2:2 + S] = kS / 8.0
    in_maps = []
    for c in range(NCORES):
        m = dict(base)
        w0 = 512 * c
        m["xwin"] = asc(xpad[:, w0:w0 + WIN, :])
        m["qC"] = asc(kCp[:, w0:w0 + WIN])
        m["qS"] = asc(kSp[:, w0:w0 + WIN])
        mv = np.zeros(TP, np.float32)
        for sl in range(WIN):
            g = w0 + sl - 2
            if 0 <= g < S:
                mv[4 * sl:4 * sl + 4] = 1.0
        m["maskv"] = asc(mv.reshape(17, 128).T)
        in_maps.append(m)

    nc = build_nc()
    res = run_bass_kernel_spmd(nc, in_maps, core_ids=list(range(NCORES)))
    out = np.empty((B, S, D), np.float32)
    for c in range(NCORES):
        out[:, 512 * c:512 * (c + 1), :] = res.results[c]["yout"]
    return out


# version 03
# speedup vs baseline: 1.1779x; 1.1779x over previous
import numpy as np
from contextlib import ExitStack

import concourse.bass as bass
import concourse.mybir as mybir
from concourse.bass_utils import run_bass_kernel_spmd

F32 = mybir.dt.float32
BF16 = mybir.dt.bfloat16
I32 = mybir.dt.int32
AF = mybir.ActivationFunctionType
ALU = mybir.AluOpType
AX = mybir.AxisListType

NCORES = 8
B, S, D = 4, 4096, 1024
DH = 64
CD = 512
DFF = 4096
OWN = 512
WIN = 516
NT = 17
TP = NT * 128
EPS = 1e-5
NKT = S // 128
QT = [128, 128, 128, 128, 4]
QCH = [(0, 258), (258, 258)]
SHUF = [i + 1 if i % 2 == 0 else i - 1 for i in range(32)]
HTW = TP + 8


class Em:
    ENGS = ("sync", "gpsimd", "vector", "scalar", "tensor")

    def __init__(self, nc, ctx):
        self.nc = nc
        self.plan = {e: [] for e in self.ENGS}
        self.sems = {}
        self.cnt = {}
        for name in ("dsy", "dgp", "v", "s", "t"):
            self.sems[name] = ctx.enter_context(nc.semaphore(name))
            self.cnt[name] = 0

    def waits(self, eng, toks):
        best = {}
        for tk in toks:
            if tk is None:
                continue
            n, v = tk
            assert v <= self.cnt[n], (n, v, self.cnt[n])
            best[n] = max(best.get(n, 0), v)
        for n, v in best.items():
            h = self.sems[n]
            self.plan[eng].append(lambda e, h=h, v=v: e.wait_ge(h, v))

    def op(self, eng, fn, inc=None, k=1):
        if inc is None:
            self.plan[eng].append(lambda e, fn=fn: fn(e))
            return None
        h = self.sems[inc]
        self.plan[eng].append(lambda e, fn=fn, h=h, k=k: fn(e).then_inc(h, k))
        self.cnt[inc] += k
        return (inc, self.cnt[inc])

    def dma(self, out, in_, after=(), eng="sync", **kw):
        self.waits(eng, after)
        sem = "dgp" if eng == "gpsimd" else "dsy"
        return self.op(eng, lambda e: e.dma_start(out=out, in_=in_, **kw),
                       inc=sem, k=16)

    def v(self, fn, after=()):
        self.waits("vector", after)
        return self.op("vector", fn, inc="v")

    def s(self, fn, after=()):
        self.waits("scalar", after)
        return self.op("scalar", fn, inc="s")

    def t(self, fn, after=()):
        self.waits("tensor", after)
        return self.op("tensor", fn, inc="t")

    def barrier(self):
        snap = [(n, v) for n, v in self.cnt.items() if v > 0]
        for eng in self.ENGS:
            self.waits(eng, snap)

    def flush(self):
        with self.nc.Block() as blk:
            for eng in self.ENGS:
                ops = self.plan[eng]
                if not ops:
                    continue
                def body(e, ops=ops):
                    for f in ops:
                        f(e)
                getattr(blk, eng)(body)
        self.plan = {e: [] for e in self.ENGS}


class Bank:
    def __init__(self, ap):
        self.ap = ap
        self.consumers = []

    def wtoks(self):
        t, self.consumers = self.consumers, []
        return t

    def read(self, tok):
        self.consumers.append(tok)
        return tok


def build_nc(dbg=(), phases=("g1", "ffn1", "attnb", "ffn2")):
    nc = bass.Bass("TRN2", target_bir_lowering=False)

    def din(n, sh):
        return nc.dram_tensor(n, sh, F32, kind="ExternalInput").ap()

    xwin = din("xwin", [B, WIN, D])
    xfull = din("xfull", [B, S, D])
    wdqT = din("wdqT", [D, CD])
    wdkvT = din("wdkvT", [D, CD])
    wuqT = din("wuqT", [CD, D])
    wukT = din("wukT", [CD, D])
    wuvT = din("wuvT", [CD, D])
    woutTa = din("woutTa", [D, D])
    lnsc = din("lnsc", [128, 8])
    lnbi = din("lnbi", [128, 8])
    winT = din("winT", [D, 3 * D])
    binb = din("binb", [128, 24])
    woutTb = din("woutTb", [D, D])
    boutb = din("boutb", [128, 8])
    boutf = din("boutf", [1, D])
    w1T = din("w1T", [D, DFF])
    w2T = din("w2T", [DFF, D])
    cw0 = din("cw0", [128, 32])
    cw1 = din("cw1", [128, 32])
    cw2 = din("cw2", [128, 32])
    cbv = din("cbv", [128, 32])
    ident = din("ident", [128, 128])
    kC = din("kC", [128, S])
    kS = din("kS", [128, S])
    qC = din("qC", [128, WIN])
    qS = din("qS", [128, WIN])
    bmask = din("bmask", [128, 128])
    maskv = din("maskv", [128, NT])

    yout = nc.dram_tensor("yout", [B, OWN, D], F32, kind="ExternalOutput").ap()

    r1buf = nc.dram_tensor("r1buf", [B, WIN, D], F32).ap()
    x1buf = nc.dram_tensor("x1buf", [B, WIN, D], F32).ap()
    x2buf = nc.dram_tensor("x2buf", [TP, D], F32).ap()
    x3buf = nc.dram_tensor("x3buf", [TP, D], F32).ap()
    w1qc = nc.dram_tensor("w1qc", [D, DFF], BF16).ap()
    w2qc = nc.dram_tensor("w2qc", [DFF, D], BF16).ap()
    hsbuf = nc.dram_tensor("hsbuf", [DFF, TP], F32).ap()
    sscr = nc.dram_tensor("sscr", [3 * TP], F32).ap()

    dbg_aps = {}
    _dbg_shapes = {"x1": ([B, WIN, D], F32), "x2": ([TP, D], F32),
                   "x3": ([TP, D], F32),
                   "ckv": ([128, 4 * S], BF16), "cq": ([128, 4 * WIN], BF16),
                   "krot": ([128, S], BF16), "qrot": ([128, WIN], BF16),
                   "nmax": ([128, 10], F32), "oacc": ([128, 8 * WIN], BF16),
                   "vsb": ([128, NKT * 520], BF16),
                   "htm": ([128, D], BF16), "hfm": ([128, 8 * 512], BF16),
                   "sts": ([128, 32], F32),
                   "qkb0": ([128, TP], BF16), "qkb1": ([128, TP], BF16),
                   "qkb2": ([128, TP], BF16), "ob0": ([128, TP], BF16),
                   "lnb": ([128, TP], BF16)}
    for name in dbg:
        sh, dty = _dbg_shapes[name]
        dbg_aps[name] = nc.dram_tensor("dbg_" + name, sh, dty,
                                       kind="ExternalOutput").ap()

    octx = ExitStack()
    sbg = lambda n, sh, dty: octx.enter_context(nc.sbuf_tensor(n, sh, dty))
    em = Em(nc, octx)

    id32 = sbg("id32", [128, 128], F32)
    idb = sbg("idb", [128, 128], BF16)
    ones_b = sbg("ones_b", [1, 128], BF16)
    ones_f = sbg("ones_f", [1, 128], F32)
    ones_p = sbg("ones_p", [128, 1], F32)
    lnsc_f = sbg("lnsc_f", [128, 8], F32)
    lnbi_f = sbg("lnbi_f", [128, 8], F32)
    lnbi_b = sbg("lnbi_b", [128, 8], BF16)
    maskv_s = sbg("maskv_s", [128, NT], F32)
    st1 = sbg("st1", [128, 8], F32)
    st2 = sbg("st2", [128, 8], F32)
    st3 = sbg("st3", [128, 8], F32)
    st4 = sbg("st4", [128, 8], F32)
    st5 = sbg("st5", [128, 40], F32)

    ld_id = em.dma(id32[:], ident)
    ld_idb = em.dma(idb[:], ident, eng="gpsimd")
    em.op("vector", lambda e: e.memset(ones_b[:], 1.0))
    em.op("vector", lambda e: e.memset(ones_f[:], 1.0))
    em.op("vector", lambda e: e.memset(ones_p[:], 1.0))
    ones_tok = em.v(lambda e: e.memset(st5[:], 0.0))
    ld_g = em.dma(lnsc_f[:], lnsc)
    ld_bi = em.dma(lnbi_f[:], lnbi)
    ld_bib = em.dma(lnbi_b[:], lnbi, eng="gpsimd")
    ld_mv = em.dma(maskv_s[:], maskv)

    def ln_stats(xa, toks, prev_norm, two, rows=128):
        R = rows
        t_sum = em.v(lambda e: e.reduce_sum(st1[0:R, 0:1], xa, axis=AX.X),
                     after=list(toks) + [prev_norm])
        t_sq = em.s(lambda e: e.activation(xsq[0:R, 0:D], xa, AF.Square,
                                           accum_out=st1[0:R, 1:2]),
                    after=list(toks) + [prev_norm])
        t1 = em.v(lambda e: e.tensor_scalar(st2[0:R, 0:2], st1[0:R, 0:2],
                                            1.0 / D, None, ALU.mult),
                  after=[t_sum, t_sq])
        t2 = em.v(lambda e: e.tensor_tensor(st2[0:R, 2:3], st2[0:R, 0:1],
                                            st2[0:R, 0:1], ALU.mult), after=[t1])
        t3 = em.v(lambda e: e.tensor_tensor(st2[0:R, 1:2], st2[0:R, 1:2],
                                            st2[0:R, 2:3], ALU.subtract), after=[t2])
        q1 = em.s(lambda e: e.activation(st3[0:R, 0:1], st2[0:R, 1:2], AF.Sqrt,
                                         bias=np.float32(EPS), scale=1.0), after=[t3])
        rsv = em.v(lambda e: e.reciprocal(st3[0:R, 0:1], st3[0:R, 0:1]), after=[q1])
        if not two:
            b1 = em.v(lambda e: e.tensor_tensor(st4[0:R, 1:2], st2[0:R, 0:1],
                                                st3[0:R, 0:1], ALU.mult), after=[rsv])
            b2 = em.v(lambda e: e.tensor_scalar(st4[0:R, 1:2], st4[0:R, 1:2],
                                                -1.0, None, ALU.mult), after=[b1])
            return b2
        v1 = em.v(lambda e: e.tensor_tensor(st2[0:R, 3:4], st3[0:R, 0:1],
                                            st3[0:R, 0:1], ALU.mult), after=[rsv])
        v2 = em.v(lambda e: e.tensor_tensor(st2[0:R, 4:5], st2[0:R, 1:2],
                                            st2[0:R, 3:4], ALU.mult), after=[v1])
        q2 = em.s(lambda e: e.activation(st3[0:R, 1:2], st2[0:R, 4:5], AF.Sqrt,
                                         bias=np.float32(EPS), scale=1.0), after=[v2])
        q3 = em.v(lambda e: e.reciprocal(st3[0:R, 1:2], st3[0:R, 1:2]), after=[q2])
        s1 = em.v(lambda e: e.tensor_tensor(st3[0:R, 3:4], st3[0:R, 0:1],
                                            st3[0:R, 1:2], ALU.mult), after=[q3])
        b1 = em.v(lambda e: e.tensor_tensor(st4[0:R, 0:1], st2[0:R, 0:1],
                                            st3[0:R, 3:4], ALU.mult), after=[s1])
        b2 = em.v(lambda e: e.tensor_scalar(st4[0:R, 0:1], st4[0:R, 0:1],
                                            -1.0, None, ALU.mult), after=[b1])
        return b2

    g1 = ExitStack()
    sb1 = lambda n, sh, dty: g1.enter_context(nc.sbuf_tensor(n, sh, dty))
    pb = [Bank(g1.enter_context(nc.psum_tensor(f"pb{i}", [128, 512], F32)))
          for i in range(7)]
    pA = [pb[0], pb[1]]
    pB = [pb[2], pb[3]]
    pO = [pb[4], pb[5]]
    pR = pb[6]
    pTb = Bank(g1.enter_context(nc.psum_tensor("pTb", [128, 1024], BF16)))

    wdqT_s = sb1("wdqT_s", [128, 8 * CD], BF16)
    wdkvT_s = sb1("wdkvT_s", [128, 8 * CD], BF16)
    wuqT_s = sb1("wuqT_s", [128, 4 * D], BF16)
    wukT_s = sb1("wukT_s", [128, 4 * D], BF16)
    wuvT_s = sb1("wuvT_s", [128, 4 * D], BF16)
    kC_s = sb1("kC_s", [128, S], BF16)
    kS_s = sb1("kS_s", [128, S], BF16)
    qC_s = sb1("qC_s", [128, WIN], BF16)
    qS_s = sb1("qS_s", [128, WIN], BF16)
    xta = sb1("xta", [128, D], F32)
    xtb = sb1("xtb", [128, D], F32)
    xsq = sb1("xsq", [128, D], F32)
    htm = sb1("htm", [128, D], BF16)
    hfm = sb1("hfm", [128, 8 * 512], BF16)
    hwin = sb1("hwin", [128, 8 * WIN], BF16)
    ckv = sb1("ckv", [128, 4 * S], BF16)
    cq = sb1("cq", [128, 4 * WIN], BF16)
    kraw = sb1("kraw", [128, 512], BF16)
    ksh = sb1("ksh", [128, 512], BF16)
    ktp = sb1("ktp", [128, 512], BF16)
    krot = sb1("krot", [128, S], BF16)
    qraw = sb1("qraw", [128, WIN], BF16)
    qsh = sb1("qsh", [128, WIN], BF16)
    qtp = sb1("qtp", [128, WIN], BF16)
    qrot = sb1("qrot", [128, WIN], BF16)
    vsb = sb1("vsb", [128, NKT * 520], BF16)
    expS = [sb1(f"expS{i}", [128, 258], BF16) for i in range(3)]
    oacc = sb1("oacc", [128, 8 * WIN], BF16)
    sqk = sb1("sqk", [128, 512], BF16)
    sqq = sb1("sqq", [128, WIN], BF16)
    bst = sb1("bst", [1, 8], F32)
    onec_b = sb1("onec_b", [128, 1], BF16)
    negBc = sb1("negBc", [128, 32], F32)
    recip = sb1("recip", [1, 258], F32)
    rbc = sb1("rbc", [64, 258], F32)
    bqkv = sb1("bqkv", [128, 8], F32)
    r1t = sb1("r1t", [128, D], F32)
    x1t = sb1("x1t", [128, D], F32)

    onec_tok = em.op("vector", lambda e: e.memset(onec_b[:], 1.0), inc="v")

    wl = {}
    wl["dq"] = em.dma(wdqT_s[:].rearrange("p (a c) -> p a c", a=8),
                      wdqT.rearrange("(a p) c -> p a c", p=128), eng="gpsimd")
    wl["dkv"] = em.dma(wdkvT_s[:].rearrange("p (a c) -> p a c", a=8),
                       wdkvT.rearrange("(a p) c -> p a c", p=128), eng="gpsimd")
    wl["uq"] = em.dma(wuqT_s[:].rearrange("p (a c) -> p a c", a=4),
                      wuqT.rearrange("(a p) c -> p a c", p=128), eng="gpsimd")
    wl["uk"] = em.dma(wukT_s[:].rearrange("p (a c) -> p a c", a=4),
                      wukT.rearrange("(a p) c -> p a c", p=128), eng="gpsimd")
    wl["uv"] = em.dma(wuvT_s[:].rearrange("p (a c) -> p a c", a=4),
                      wuvT.rearrange("(a p) c -> p a c", p=128), eng="gpsimd")
    wl["kC"] = em.dma(kC_s[:], kC, eng="gpsimd")
    wl["kS"] = em.dma(kS_s[:], kS, eng="gpsimd")
    wl["qC"] = em.dma(qC_s[:], qC, eng="gpsimd")
    wl["qS"] = em.dma(qS_s[:], qS, eng="gpsimd")

    mmtok = None
    for col, w_s, wt in ((0, wdqT_s, wl["dq"]), (4, wdkvT_s, wl["dkv"])):
        for jc in range(4):
            for dc in range(8):
                mmtok = em.t(lambda e, col=col, w_s=w_s, jc=jc, dc=dc:
                    e.matmul(pR.ap[:, col + jc:col + jc + 1],
                             w_s[:, dc * CD + jc * 128:dc * CD + jc * 128 + 128],
                             lnbi_b[:, dc:dc + 1],
                             start=(dc == 0), stop=(dc == 7)),
                    after=[wt, ld_bib] if dc == 0 else ())
    ev_bias = pR.read(em.v(lambda e: e.tensor_copy(bqkv[:], pR.ap[:, 0:8]),
                           after=[mmtok]))
    folds = []
    for w_s, wt in ((wdqT_s, wl["dq"]), (wdkvT_s, wl["dkv"])):
        for dc in range(8):
            folds.append(em.v(lambda e, w_s=w_s, dc=dc:
                e.tensor_scalar(w_s[:, dc * CD:(dc + 1) * CD],
                                w_s[:, dc * CD:(dc + 1) * CD],
                                lnsc_f[:, dc:dc + 1], None, ALU.mult),
                after=[wt, ld_g, ev_bias]))
    fold_tok = folds[-1]

    prev_norm = ones_tok
    last_exp = {0: None, 1: None}
    xbuf_last = {0: [], 1: []}
    htm_last = []

    for b in range(B):
        r1_toks = []
        hwin_evs = []
        for j in range(5):
            rows = QT[j]
            xv = xta if j % 2 == 0 else xtb
            ld = em.dma(xv[0:rows, :], xwin[b, j * 128:j * 128 + rows, :],
                        after=xbuf_last[j % 2])
            xbuf_last[j % 2] = []
            btok = ln_stats(xv[0:rows, :], [ld], prev_norm, two=True,
                            rows=rows)
            h0 = em.v(lambda e, xv=xv, rows=rows:
                e.tensor_scalar(r1t[0:rows, :], xv[0:rows, :], st2[0:rows, 0:1],
                                st3[0:rows, 0:1], ALU.subtract, ALU.mult),
                after=[btok])
            r1 = em.v(lambda e, xv=xv, rows=rows:
                e.tensor_add(r1t[0:rows, :], r1t[0:rows, :], xv[0:rows, :]),
                after=[h0])
            r1_toks.append(em.dma(r1buf[b, j * 128:j * 128 + rows, :],
                                  r1t[0:rows, :], after=[r1]))
            hh = em.s(lambda e, xv=xv, rows=rows:
                e.activation(htm[0:rows, :], xv[0:rows, :], AF.Identity,
                             bias=st4[0:rows, 0:1], scale=st3[0:rows, 3:4]),
                after=[btok, h0] + htm_last)
            htm_last = []
            prev_norm = hh
            xbuf_last[j % 2] = [hh, r1]
            for dc in range(8):
                bk = pTb
                tr = em.t(lambda e, bk=bk, dc=dc, rows=rows:
                    e.transpose(bk.ap[:, 0:rows],
                                htm[0:rows, dc * 128:(dc + 1) * 128],
                                idb[0:rows, 0:rows]),
                    after=[hh, ld_idb] + bk.wtoks())
                ev = bk.read(em.s(lambda e, bk=bk, dc=dc, j=j, rows=rows:
                    e.activation(hwin[:, dc * WIN + j * 128:
                                       dc * WIN + j * 128 + rows],
                                 bk.ap[:, 0:rows], AF.Copy), after=[tr]))
                hwin_evs.append(ev)
                htm_last.append(tr)
        last_exp = {0: None, 1: None}
        cq_evs = []
        for cc in range(4):
            for (q0, qw) in QCH:
                mm = None
                for dc in range(8):
                    mm = em.t(lambda e, cc=cc, dc=dc, q0=q0, qw=qw:
                        e.matmul(pB[0].ap[:, 0:qw],
                                 wdqT_s[:, dc * CD + cc * 128:
                                        dc * CD + cc * 128 + 128],
                                 hwin[:, dc * WIN + q0:dc * WIN + q0 + qw],
                                 start=(dc == 0), stop=(dc == 7)),
                        after=([fold_tok] + hwin_evs + pB[0].wtoks())
                        if dc == 0 else ())
                ev = pB[0].read(em.s(lambda e, cc=cc, q0=q0, qw=qw:
                    e.activation(cq[:, cc * WIN + q0:cc * WIN + q0 + qw],
                                 pB[0].ap[:, 0:qw], AF.Identity,
                                 bias=bqkv[:, cc:cc + 1], scale=1.0),
                    after=[mm, ev_bias]))
                cq_evs.append(ev)

        ckv_evs = []
        for stile in range(8):
            hfm_evs = []
            for i in range(4):
                tile = stile * 4 + i
                xv = xta if i % 2 == 0 else xtb
                ld = em.dma(xv[:], xfull[b, tile * 128:(tile + 1) * 128, :],
                            after=xbuf_last[i % 2])
                xbuf_last[i % 2] = []
                btok = ln_stats(xv[:, :], [ld], prev_norm, two=True)
                hh = em.s(lambda e, xv=xv:
                    e.activation(htm[:], xv[:], AF.Identity,
                                 bias=st4[:, 0:1], scale=st3[:, 3:4]),
                    after=[btok] + htm_last)
                htm_last = []
                prev_norm = hh
                xbuf_last[i % 2] = [hh]
                if b == 0 and stile == 0 and i == 0:
                    if "htm" in dbg_aps:
                        em.dma(dbg_aps["htm"], htm[:], after=[hh])
                    if "sts" in dbg_aps:
                        for si, stt_ in enumerate((st1, st2, st3, st4)):
                            em.dma(dbg_aps["sts"][:, si * 8:(si + 1) * 8],
                                   stt_[:], after=[hh])
                    if "htm" in dbg_aps or "sts" in dbg_aps:
                        em.barrier()
                for dc in range(8):
                    bk = pTb
                    tr = em.t(lambda e, bk=bk, dc=dc:
                        e.transpose(bk.ap[:, 0:128],
                                    htm[:, dc * 128:(dc + 1) * 128], idb[:]),
                        after=[hh, ld_idb] + bk.wtoks())
                    ev = bk.read(em.s(lambda e, bk=bk, dc=dc, i=i:
                        e.activation(hfm[:, dc * 512 + i * 128:
                                          dc * 512 + (i + 1) * 128],
                                     bk.ap[:, 0:128], AF.Copy), after=[tr]))
                    hfm_evs.append(ev)
                    htm_last.append(tr)
            if b == 0 and stile == 0 and "hfm" in dbg_aps:
                em.dma(dbg_aps["hfm"], hfm[:], after=hfm_evs)
                em.barrier()
            for cc in range(4):
                bk = pA[cc % 2]
                mm = None
                for dc in range(8):
                    mm = em.t(lambda e, bk=bk, cc=cc, dc=dc:
                        e.matmul(bk.ap[:, 0:512],
                                 wdkvT_s[:, dc * CD + cc * 128:
                                         dc * CD + cc * 128 + 128],
                                 hfm[:, dc * 512:(dc + 1) * 512],
                                 start=(dc == 0), stop=(dc == 7)),
                        after=([fold_tok] + hfm_evs + bk.wtoks())
                        if dc == 0 else ())
                ev = bk.read(em.s(lambda e, bk=bk, cc=cc, stile=stile:
                    e.activation(ckv[:, cc * S + stile * 512:
                                      cc * S + (stile + 1) * 512],
                                 bk.ap[:, 0:512], AF.Identity,
                                 bias=bqkv[:, 4 + cc:5 + cc], scale=1.0),
                    after=[mm, ev_bias]))
                ckv_evs.append(ev)

        for half in range(2):
            vz = em.v(lambda e: e.memset(
                vsb[:].rearrange("p (k h c) -> p k h c", k=NKT, h=8)
                [:, :, :, 64:65], 1.0), after=ckv_evs)
            v_evs = [vz]
            for kt in range(NKT):
                bk = pA[kt % 2]
                mm = None
                for cc in range(4):
                    mm = em.t(lambda e, bk=bk, kt=kt, cc=cc, half=half:
                        e.matmul(bk.ap[:, 0:512],
                                 ckv[:, cc * S + kt * 128:cc * S + (kt + 1) * 128],
                                 wuvT_s[:, cc * D + half * 512:
                                        cc * D + half * 512 + 512],
                                 start=(cc == 0), stop=(cc == 3)),
                        after=(ckv_evs + [wl["uv"], vz] + bk.wtoks())
                        if cc == 0 else ())
                v_evs.append(bk.read(em.s(lambda e, bk=bk, kt=kt:
                    e.activation(vsb[:].rearrange("p (k h c) -> p k h c",
                                                  k=NKT, h=8)[:, kt, :, 0:64],
                                 bk.ap[:, 0:512], AF.Copy), after=[mm])))

            for pp4 in range(4):
                hp = half * 4 + pp4
                k_toks = []
                for ktc in range(8):
                    bk = pA[ktc % 2]
                    mm = None
                    for cc in range(4):
                        mm = em.t(lambda e, bk=bk, ktc=ktc, cc=cc, hp=hp:
                            e.matmul(bk.ap[:, 0:512],
                                     wukT_s[:, cc * D + hp * 128:
                                            cc * D + hp * 128 + 128],
                                     ckv[:, cc * S + ktc * 512:
                                         cc * S + (ktc + 1) * 512],
                                     start=(cc == 0), stop=(cc == 3)),
                            after=(ckv_evs + [wl["uk"]] + bk.wtoks() + v_evs)
                            if cc == 0 else ())
                    evk = bk.read(em.s(lambda e, bk=bk:
                        e.activation(kraw[:, 0:512], bk.ap[:, 0:512], AF.Copy),
                        after=[mm] + k_toks[-1:]))
                    m1 = em.v(lambda e, ktc=ktc:
                        e.tensor_tensor(ktp[:], kraw[:],
                                        kS_s[:, ktc * 512:(ktc + 1) * 512],
                                        ALU.mult), after=[evk, wl["kS"]])
                    m2 = em.v(lambda e: e.stream_shuffle(ksh[:], ktp[:], SHUF),
                              after=[m1])
                    m3 = em.v(lambda e, ktc=ktc:
                        e.tensor_tensor(ktp[:], kraw[:],
                                        kC_s[:, ktc * 512:(ktc + 1) * 512],
                                        ALU.mult), after=[m2, wl["kC"]])
                    m4 = em.v(lambda e, ktc=ktc:
                        e.tensor_add(krot[:, ktc * 512:(ktc + 1) * 512],
                                     ktp[:], ksh[:]), after=[m3])
                    k_toks.append(m4)
                q_toks = []
                for (q0, qw) in QCH:
                    mm = None
                    for cc in range(4):
                        mm = em.t(lambda e, q0=q0, qw=qw, cc=cc, hp=hp:
                            e.matmul(pB[0].ap[:, 0:qw],
                                     wuqT_s[:, cc * D + hp * 128:
                                            cc * D + hp * 128 + 128],
                                     cq[:, cc * WIN + q0:cc * WIN + q0 + qw],
                                     start=(cc == 0), stop=(cc == 3)),
                            after=(cq_evs + [wl["uq"]] + pB[0].wtoks() +
                                   ([last_exp[0]] if last_exp[0] else []))
                            if cc == 0 else ())
                    q_toks.append(pB[0].read(em.s(
                        lambda e, q0=q0, qw=qw:
                        e.activation(qraw[:, q0:q0 + qw], pB[0].ap[:, 0:qw],
                                     AF.Copy), after=[mm])))
                m1 = em.v(lambda e: e.tensor_tensor(qtp[:], qraw[:], qS_s[:],
                                                    ALU.mult),
                          after=q_toks + [wl["qS"]])
                m2 = em.v(lambda e: e.stream_shuffle(qsh[:], qtp[:], SHUF),
                          after=[m1])
                m3 = em.v(lambda e: e.tensor_tensor(qtp[:], qraw[:], qC_s[:],
                                                    ALU.mult),
                          after=[m2, wl["qC"]])
                qr = em.v(lambda e: e.tensor_add(qrot[:], qtp[:], qsh[:]),
                          after=[m3])

                if b == 0 and hp == 0:
                    did = False
                    for nm, buf, tok in (("krot", krot, k_toks[-1]),
                                         ("qrot", qrot, qr)):
                        if nm in dbg_aps:
                            em.dma(dbg_aps[nm], buf[:], after=[tok])
                            did = True
                    if did:
                        em.barrier()
                kq_last = None
                mxacc = None
                for ktc in range(8):
                    sq = em.v(lambda e, ktc=ktc:
                        e.tensor_tensor(sqk[:], krot[:, ktc * 512:(ktc + 1) * 512],
                                        krot[:, ktc * 512:(ktc + 1) * 512],
                                        ALU.mult),
                        after=[k_toks[ktc]] + ([kq_last] if kq_last else []))
                    cs = em.t(lambda e: e.matmul(pR.ap[0:1, 0:512], onec_b[:],
                                                 sqk[:], start=True, stop=True),
                              after=[sq, onec_tok] + pR.wtoks())
                    kq_last = cs
                    rd = pR.read(em.v(lambda e:
                        e.reduce_max(bst[0:1, 6:7], pR.ap[0:1, 0:512],
                                     axis=AX.X),
                        after=[cs] + ([mxacc] if mxacc else [])))
                    mxacc = em.v(lambda e, ktc=ktc:
                        (e.tensor_copy(bst[0:1, 0:1], bst[0:1, 6:7]) if ktc == 0
                         else e.tensor_max(bst[0:1, 0:1], bst[0:1, 0:1],
                                           bst[0:1, 6:7])), after=[rd])
                sq = em.v(lambda e: e.tensor_tensor(sqq[:], qrot[:], qrot[:],
                                                    ALU.mult),
                          after=[qr, kq_last])
                for ci, (q0, qw) in enumerate(QCH):
                    cs = em.t(lambda e, q0=q0, qw=qw:
                        e.matmul(pR.ap[0:1, 0:qw], onec_b[:],
                                 sqq[:, q0:q0 + qw], start=True, stop=True),
                        after=[sq, onec_tok] + pR.wtoks())
                    rd = pR.read(em.v(lambda e:
                        e.reduce_max(bst[0:1, 6:7], pR.ap[0:1, 0:qw],
                                     axis=AX.X), after=[cs, mxacc]))
                    mxacc = em.v(lambda e, ci=ci:
                        (e.tensor_copy(bst[0:1, 1:2], bst[0:1, 6:7]) if ci == 0
                         else e.tensor_max(bst[0:1, 1:2], bst[0:1, 1:2],
                                           bst[0:1, 6:7])), after=[rd])
                b1 = em.v(lambda e: e.tensor_tensor(bst[0:1, 2:3], bst[0:1, 0:1],
                                                    bst[0:1, 1:2], ALU.mult),
                          after=[mxacc])
                b2 = em.s(lambda e: e.activation(bst[0:1, 3:4], bst[0:1, 2:3],
                                                 AF.Sqrt, bias=0.0, scale=1.0),
                          after=[b1])
                b3 = em.v(lambda e: e.tensor_scalar(bst[0:1, 3:4], bst[0:1, 3:4],
                                                    -1.0, None, ALU.mult),
                          after=[b2])
                bcm = em.t(lambda e: e.matmul(pR.ap[:, 0:1], ones_f[:],
                                              bst[0:1, 3:4], start=True,
                                              stop=True),
                           after=[b3] + pR.wtoks())
                bh = b * 8 + hp
                nbv = pR.read(em.v(lambda e, bh=bh:
                    e.tensor_copy(negBc[:, bh:bh + 1], pR.ap[:, 0:1]),
                    after=[bcm]))

                av_by_slot = [None, None, None]
                for hl in range(2):
                    h = 2 * hp + hl
                    pOb = pO[hl]
                    for ci, (q0, qw) in enumerate(QCH):
                        for kt in range(NKT):
                            slot = kt % 3
                            bk = pB[kt % 2]
                            war = bk.wtoks()
                            if kt == 0:
                                war += pOb.wtoks()
                            mm1 = em.t(lambda e, bk=bk, hl=hl, kt=kt, q0=q0,
                                              qw=qw:
                                e.matmul(bk.ap[:, 0:qw],
                                         krot[hl * 64:hl * 64 + 64,
                                              kt * 128:(kt + 1) * 128],
                                         qrot[hl * 64:hl * 64 + 64, q0:q0 + qw],
                                         start=True, stop=True),
                                after=[qr, k_toks[-1]] + war)
                            ex = bk.read(em.s(lambda e, slot=slot, bk=bk, qw=qw,
                                                     bh=bh:
                                e.activation(expS[slot][:, 0:qw],
                                             bk.ap[:, 0:qw], AF.Exp,
                                             bias=negBc[:, bh:bh + 1],
                                             scale=1.0),
                                after=[mm1, nbv] +
                                      ([av_by_slot[slot]] if av_by_slot[slot]
                                       else [])))
                            last_exp[kt % 2] = ex
                            av = em.t(lambda e, pOb=pOb, kt=kt, slot=slot,
                                             qw=qw, h=h:
                                e.matmul(pOb.ap[0:65, 0:qw],
                                         vsb[:, kt * 520 + (h % 8) * 65:
                                             kt * 520 + (h % 8) * 65 + 65],
                                         expS[slot][:, 0:qw],
                                         start=(kt == 0), stop=(kt == NKT - 1)),
                                after=[ex])
                            av_by_slot[slot] = av
                        rc = pOb.read(em.v(lambda e, pOb=pOb, qw=qw:
                            e.reciprocal(recip[:, 0:qw], pOb.ap[64:65, 0:qw]),
                            after=[av_by_slot[(NKT - 1) % 3]]))
                        bc = em.t(lambda e, qw=qw:
                            e.matmul(pR.ap[0:64, 0:qw], ones_f[:, 0:64],
                                     recip[:, 0:qw], start=True, stop=True),
                            after=[rc] + pR.wtoks())
                        bce2 = pR.read(em.s(lambda e, qw=qw:
                            e.activation(rbc[:, 0:qw], pR.ap[0:64, 0:qw],
                                         AF.Copy), after=[bc]))
                        wr = em.v(lambda e, pOb=pOb, h=h, q0=q0, qw=qw:
                            e.tensor_tensor(
                                oacc[64 * (h % 2):64 * (h % 2) + 64,
                                     (h // 2) * WIN + q0:(h // 2) * WIN + q0 + qw],
                                pOb.ap[0:64, 0:qw], rbc[:, 0:qw], ALU.mult),
                            after=[bce2])
                        pOb.read(wr)

        if b == 0:
            did = False
            for nm, buf in (("ckv", ckv), ("cq", cq), ("oacc", oacc),
                            ("vsb", vsb)):
                if nm in dbg_aps:
                    em.dma(dbg_aps[nm], buf[:],
                           after=[("v", em.cnt["v"]), ("s", em.cnt["s"]),
                                  ("t", em.cnt["t"])])
                    did = True
            if did:
                em.barrier()
        woc = vsb[:, 0:8 * D]
        oacc_tok = [("v", em.cnt["v"]), ("s", em.cnt["s"])]
        ld_wo = em.dma(woc.rearrange("p (a c) -> p a c", a=8),
                       woutTa.rearrange("(a p) c -> p a c", p=128),
                       eng="gpsimd", after=oacc_tok + [("t", em.cnt["t"])])
        for j in range(5):
            rows = QT[j]
            ld_r1 = em.dma(r1t[0:rows, :], r1buf[b, j * 128:j * 128 + rows, :],
                           after=r1_toks)
            adds = []
            for dh in range(2):
                bk = pA[dh]
                mm = None
                for oc in range(8):
                    mm = em.t(lambda e, bk=bk, j=j, dh=dh, oc=oc, rows=rows:
                        e.matmul(bk.ap[0:rows, 0:512],
                                 oacc[:, oc * WIN + j * 128:
                                      oc * WIN + j * 128 + rows],
                                 woc[:, oc * D + dh * 512:oc * D + dh * 512 + 512],
                                 start=(oc == 0), stop=(oc == 7)),
                        after=(oacc_tok + [ld_wo] + bk.wtoks())
                        if oc == 0 else ())
                adds.append(bk.read(em.v(lambda e, bk=bk, dh=dh, rows=rows:
                    e.tensor_add(x1t[0:rows, dh * 512:(dh + 1) * 512],
                                 bk.ap[0:rows, 0:512],
                                 r1t[0:rows, dh * 512:(dh + 1) * 512]),
                    after=[mm, ld_r1])))
            em.dma(x1buf[b, j * 128:j * 128 + rows, :], x1t[0:rows, :],
                   after=adds)

    em.barrier()
    em.flush()
    g1.close()

    g2 = ExitStack()
    sb2 = lambda n, sh, dty: g2.enter_context(nc.sbuf_tensor(n, sh, dty))
    pg = [Bank(g2.enter_context(nc.psum_tensor(f"pg{i}", [128, 512], F32)))
          for i in range(7)]
    pF = [pg[0], pg[1]]
    pW = [pg[2], pg[3]]
    pSV = pg[4]
    pSb = pg[5]
    pM = pg[6]
    pTb2 = Bank(g2.enter_context(nc.psum_tensor("pTb2", [128, 1024], BF16)))

    xqT = sb2("xqT", [128, 8 * HTW], BF16)
    bigb = sb2("bigb", [128, 32 * 1024], BF16)
    xq2c = sb2("xq2c", [128, 32 * 512], BF16)
    qkv_c = [sb2(f"qkv_c{i}", [128, TP], BF16) for i in range(3)]
    w1sl = [sb2(f"w1sl{i}", [128, 1024], BF16) for i in range(2)]
    cvt = sb2("cvt", [128, TP + 8], F32)
    cvu = sb2("cvu", [128, TP + 8], F32)

    cw0s = sb2("cw0s", [128, 32], F32)
    cw1s = sb2("cw1s", [128, 32], F32)
    cw2s = sb2("cw2s", [128, 32], F32)
    cbs = sb2("cbs", [128, 32], F32)
    amaxp = sb2("amaxp", [128, 2 * 68], F32)
    akr = sb2("akr", [68, 32], F32)
    s2kr = sb2("s2kr", [68, 32], F32)
    ivkr = sb2("ivkr", [68, 32], F32)
    s2fm = sb2("s2fm", [1, 512], F32)
    iv_pu = sb2("iv_pu", [128, NT], F32)
    xq2i = sb2("xq2i", [128, 512], I32)
    h0f = sb2("h0f", [128, D], F32)
    xint = sb2("xint", [128, D], I32)
    xqd = sb2("xqd", [128, D], BF16)
    qs_s = sb2("qs_s", [128, NT], F32)
    qs_iv = sb2("qs_iv", [128, NT], F32)
    xt2 = sb2("xt2", [128, D], F32)
    xsqb = sb2("xsqb", [128, D], F32)
    htm2 = sb2("htm2", [128, D], BF16)
    bmask_s = sb2("bmask_s", [128, 128], F32)
    binb_s = sb2("binb_s", [128, 24], F32)
    boutb_s = sb2("boutb_s", [128, 8], F32)
    boutf_s = sb2("boutf_s", [1, D], BF16)
    winsl = [sb2(f"winsl{i}", [128, 8 * 128], BF16) for i in range(2)]
    vtm = sb2("vtm", [128, 128], BF16)
    pnt = sb2("pnt", [128, 128], BF16)
    pnr = sb2("pnr", [128, 128], BF16)
    smc = sb2("smc", [128, 8], F32)

    wld = cvt
    wq_i = xint
    wq_f = cvu
    wq_b = xqd
    x1t2_alias_check = True
    ld_bm = em.dma(bmask_s[:], bmask)
    ld_bb = em.dma(binb_s[:], binb)
    ld_bo = em.dma(boutb_s[:], boutb)
    ld_bf = em.dma(boutf_s[:], boutf, eng="gpsimd")
    cv_toks = [
        em.dma(cw0s[:], cw0),
        em.dma(cw1s[:], cw1),
        em.dma(cw2s[:], cw2),
        em.dma(cbs[:], cbv),
    ]

    def quant_weights(srcT, dstT, ntile, width, c0):
        toks = []
        prev = None
        ncc = width // 1024
        for i in range(ntile):
            for cc in range(ncc):
                ld = em.dma(wld[:, 0:1024], srcT[i * 128:(i + 1) * 128,
                                         cc * 1024:(cc + 1) * 1024],
                            after=[prev] if prev else [])
                rs = em.v(lambda e: e.tensor_reduce(
                    st5[:, 39:40], wld[:, 0:1024], axis=AX.X, op=ALU.add,
                    apply_absolute_value=True), after=[ld])
                rs = em.v(lambda e, i=i, cc=cc:
                    (e.tensor_copy(st5[:, i:i + 1], st5[:, 39:40]) if cc == 0
                     else e.tensor_add(st5[:, i:i + 1], st5[:, i:i + 1],
                                       st5[:, 39:40])), after=[rs])
                toks.append(rs)
                prev = rs
        smm = em.t(lambda e, ntile=ntile:
            e.matmul(pM.ap[0:1, 0:ntile], ones_p[:], st5[:, 0:ntile],
                     start=True, stop=True), after=toks + pM.wtoks())
        tot = pM.read(em.v(lambda e, ntile=ntile:
            e.reduce_sum(st5[0:1, 36:37], pM.ap[0:1, 0:ntile], axis=AX.X),
            after=[smm]))
        mn = em.v(lambda e, ntile=ntile, width=width:
            e.tensor_scalar(st5[0:1, 37:38], st5[0:1, 36:37],
                            1.0 / (ntile * 128 * width), 1e-5,
                            ALU.mult, ALU.max), after=[tot])
        rec = em.v(lambda e: e.reciprocal(st5[0:1, 38:39], st5[0:1, 37:38]),
                   after=[mn])
        bc = em.t(lambda e: e.matmul(pM.ap[:, 0:2], ones_f[:], st5[0:1, 37:39],
                                     start=True, stop=True),
                  after=[rec] + pM.wtoks())
        bce = pM.read(em.v(lambda e, c0=c0:
            e.tensor_copy(st4[:, c0:c0 + 2], pM.ap[:, 0:2]), after=[bc]))
        prev = None
        for i in range(ntile):
            for cc in range(ncc):
                ld = em.dma(wld[:, 0:1024], srcT[i * 128:(i + 1) * 128,
                                         cc * 1024:(cc + 1) * 1024],
                            after=[toks[-1]] if prev is None else [prev])
                q1 = em.v(lambda e, c0=c0:
                    e.tensor_scalar(wq_i[:], wld[:, 0:1024],
                                    st4[:, c0 + 1:c0 + 2],
                                    None, ALU.mult), after=[ld, bce])
                q2 = em.v(lambda e: e.tensor_scalar(wq_f[:, 0:1024], wq_i[:],
                                                    1.0, -1.0, ALU.min,
                                                    ALU.max), after=[q1])
                q3 = em.v(lambda e, c0=c0:
                    e.tensor_scalar(wq_b[:], wq_f[:, 0:1024],
                                    st4[:, c0:c0 + 1],
                                    None, ALU.mult), after=[q2])
                prev = em.dma(dstT[i * 128:(i + 1) * 128,
                                   cc * 1024:(cc + 1) * 1024], wq_b[:],
                              after=[q3])
        return prev

    wq1_tok = quant_weights(w1T, w1qc, 8, DFF, 2)
    wq2_tok = quant_weights(w2T, w2qc, 32, D, 4)

    gz = []
    for dc in range(8):
        gz.append(em.v(lambda e, dc=dc:
            e.memset(xqT[:, dc * HTW:dc * HTW + 4], 0.0)))
        gz.append(em.v(lambda e, dc=dc:
            e.memset(xqT[:, dc * HTW + 4 + TP:dc * HTW + 8 + TP], 0.0)))
    em.barrier()
    em.flush()

    def sb3(sbap):
        return sbap

    def plain_ln_to(dst_fn, srcrow, quant):
        prev = ones_tok
        out_toks = []
        xt2_last = []
        for j in range(NT):
            ap, nval = srcrow(j * 128, 128)
            if nval < 128:
                mz = em.v(lambda e: e.memset(xt2[:], 0.0), after=xt2_last)
                ld = [em.dma(xt2[0:nval, :], ap, after=xt2_last + [mz])]
            else:
                ld = [em.dma(xt2[:], ap, after=xt2_last)]
            xt2_last = []
            tsum = em.v(lambda e: e.reduce_sum(st1[:, 0:1], xt2[:], axis=AX.X),
                        after=ld + [prev])
            tsq = em.s(lambda e: e.activation(xsqb[:], xt2[:], AF.Square,
                                              accum_out=st1[:, 1:2]),
                       after=ld + [prev])
            t1 = em.v(lambda e: e.tensor_scalar(st2[:, 0:2], st1[:, 0:2],
                                                1.0 / D, None, ALU.mult),
                      after=[tsum, tsq])
            t2 = em.v(lambda e: e.tensor_tensor(st2[:, 2:3], st2[:, 0:1],
                                                st2[:, 0:1], ALU.mult),
                      after=[t1])
            t3 = em.v(lambda e: e.tensor_tensor(st2[:, 1:2], st2[:, 1:2],
                                                st2[:, 2:3], ALU.subtract),
                      after=[t2])
            q1 = em.s(lambda e: e.activation(st3[:, 0:1], st2[:, 1:2], AF.Sqrt,
                                             bias=np.float32(EPS), scale=1.0),
                      after=[t3])
            rsv = em.v(lambda e: e.reciprocal(st3[:, 0:1], st3[:, 0:1]),
                       after=[q1])
            if not quant:
                nb = em.v(lambda e: e.tensor_tensor(st3[:, 1:2], st2[:, 0:1],
                                                    st3[:, 0:1], ALU.mult),
                          after=[rsv])
                nb2 = em.v(lambda e: e.tensor_scalar(st3[:, 1:2], st3[:, 1:2],
                                                     -1.0, None, ALU.mult),
                          after=[nb])
                hh = em.s(lambda e: e.activation(htm2[:], xt2[:], AF.Identity,
                                                 bias=st3[:, 1:2],
                                                 scale=st3[:, 0:1]),
                          after=[nb2])
                prev = hh
                xt2_last = [hh]
                out_toks.append(dst_fn(j, hh))
                continue
            h0 = em.v(lambda e: e.tensor_scalar(h0f[:], xt2[:], st2[:, 0:1],
                                                st3[:, 0:1], ALU.subtract,
                                                ALU.mult), after=[rsv])
            xt2_last = [h0]
            am = em.v(lambda e, j=j: e.reduce_max(qs_s[:, j:j + 1], h0f[:],
                                                  axis=AX.X,
                                                  apply_absolute_value=True),
                      after=[h0])
            c1 = em.v(lambda e, j=j: e.tensor_scalar(qs_s[:, j:j + 1],
                                                     qs_s[:, j:j + 1], 1e-5,
                                                     None, ALU.max), after=[am])
            c2 = em.v(lambda e, j=j: e.tensor_scalar(qs_iv[:, j:j + 1],
                                                     qs_s[:, j:j + 1],
                                                     1.0 / 127.0,
                                                     maskv_s[:, j:j + 1],
                                                     ALU.mult, ALU.mult),
                      after=[c1, ld_mv])
            c3 = em.v(lambda e, j=j: e.reciprocal(qs_s[:, j:j + 1],
                                                  qs_s[:, j:j + 1]), after=[c2])
            c4 = em.v(lambda e, j=j: e.tensor_scalar(qs_s[:, j:j + 1],
                                                     qs_s[:, j:j + 1], 127.0,
                                                     None, ALU.mult), after=[c3])
            qi = em.v(lambda e, j=j: e.tensor_scalar(xint[:], h0f[:],
                                                     qs_s[:, j:j + 1], None,
                                                     ALU.mult), after=[c4])
            qd = em.s(lambda e, j=j: e.activation(xqd[:], xint[:], AF.Identity,
                                                  bias=0.0,
                                                  scale=qs_iv[:, j:j + 1]),
                      after=[qi])
            prev = qd
            out_toks.append(dst_fn(j, qd))
        return out_toks

    def transpose_to_xqT(j, src_tile, src_tok):
        evs = []
        for dc in range(8):
            bk = pTb2
            tr = em.t(lambda e, bk=bk, dc=dc:
                e.transpose(bk.ap[:, 0:128],
                            src_tile[:, dc * 128:(dc + 1) * 128], idb[:]),
                after=[src_tok, ld_idb] + bk.wtoks())
            evs.append(bk.read(em.s(lambda e, bk=bk, dc=dc, j=j:
                e.activation(xqT[:, dc * HTW + 4 + j * 128:
                                  dc * HTW + 4 + (j + 1) * 128],
                             bk.ap[:, 0:128], AF.Copy), after=[tr])))
        return evs[-1]

    def ffn(srcrow, dst, dbgname=None):
        f1 = plain_ln_to(lambda j, tok: transpose_to_xqT(j, xqd, tok),
                         srcrow, quant=True)
        em.barrier()
        amz = em.v(lambda e: e.memset(amaxp[:], 0.0))
        am_last = amz
        hs_toks = []
        w1_mm_last = {0: None, 1: None}
        for ct in range(32):
            ldw = em.dma(w1sl[ct % 2][:].rearrange("p (a c) -> p a c", a=8),
                         w1qc.rearrange("(a p) f -> p a f", p=128)
                         [:, :, ct * 128:(ct + 1) * 128],
                         after=[wq1_tok, w1_mm_last[ct % 2]])
            nch = [(0, 512), (512, 512), (1024, 512), (1536, 512), (2048, 136)]
            ev_toks = []
            for (n0, nw) in nch:
                bk = pF[(n0 // 512) % 2]
                mm = None
                for dc in range(8):
                    mm = em.t(lambda e, bk=bk, ct=ct, dc=dc, n0=n0, nw=nw:
                        e.matmul(bk.ap[:, 0:nw],
                                 w1sl[ct % 2][:, dc * 128:(dc + 1) * 128],
                                 xqT[:, dc * HTW + n0:dc * HTW + n0 + nw],
                                 start=(dc == 0), stop=(dc == 7)),
                        after=([ldw] + bk.wtoks()) if dc == 0 else ())
                ev_toks.append(bk.read(em.v(lambda e, bk=bk, n0=n0, nw=nw:
                    e.tensor_copy(cvt[:, n0:n0 + nw], bk.ap[:, 0:nw]),
                    after=[mm] + hs_toks[-1:])))
                w1_mm_last[ct % 2] = mm
            c1 = em.v(lambda e, ct=ct:
                e.tensor_scalar(cvu[:, 0:TP], cvt[:, 4:4 + TP],
                                cw1s[:, ct:ct + 1], cbs[:, ct:ct + 1],
                                ALU.mult, ALU.add),
                after=ev_toks + cv_toks)
            c2 = em.v(lambda e, ct=ct:
                e.scalar_tensor_tensor(cvu[:, 0:TP], cvt[:, 0:TP],
                                       cw0s[:, ct:ct + 1], cvu[:, 0:TP],
                                       ALU.mult, ALU.add), after=[c1])
            c3 = em.v(lambda e, ct=ct:
                e.scalar_tensor_tensor(cvu[:, 0:TP], cvt[:, 8:8 + TP],
                                       cw2s[:, ct:ct + 1], cvu[:, 0:TP],
                                       ALU.mult, ALU.add), after=[c2])
            sl = em.s(lambda e: e.activation(cvt[:, 0:TP], cvu[:, 0:TP],
                                             AF.Silu),
                      after=[c3] + hs_toks[-1:])
            amx = em.v(lambda e: e.reduce_max(
                amaxp[:, 0:68],
                cvt[:, 0:TP].rearrange("p (k u) -> p k u", u=32),
                axis=AX.X, apply_absolute_value=True), after=[sl, am_last])
            am_last = em.v(lambda e: e.tensor_max(amaxp[:, 68:136],
                                                  amaxp[:, 68:136],
                                                  amaxp[:, 0:68]), after=[amx])
            hs_toks.append(em.dma(hsbuf[ct * 128:(ct + 1) * 128, :],
                                  cvt[:, 0:TP], after=[sl, am_last]))
        trA = em.t(lambda e: e.transpose(pM.ap[0:68, 0:128], amaxp[:, 68:136],
                                         id32[:]),
                   after=[am_last, ld_id] + pM.wtoks())
        evA = pM.read(em.v(lambda e: e.tensor_copy(akr2[:, 0:128],
                                               pM.ap[0:68, 0:128]),
            after=[trA]))
        red = em.v(lambda e: e.tensor_reduce(
            akr[:, 0:32], akr2[:, 0:128].rearrange("k (g r) -> k r g", g=4),
            axis=AX.X, op=ALU.max), after=[evA])
        sc1 = em.v(lambda e: e.tensor_scalar(akr[:, 0:32], akr[:, 0:32],
                                             1e-5, None, ALU.max), after=[red])
        sc2 = em.v(lambda e: e.tensor_scalar(ivkr[:, 0:32], akr[:, 0:32],
                                             1.0 / 127.0, None, ALU.mult),
                   after=[sc1])
        sc3 = em.v(lambda e: e.reciprocal(s2kr[:, 0:32], akr[:, 0:32]),
                   after=[sc2])
        sc4 = em.v(lambda e: e.tensor_scalar(s2kr[:, 0:32], s2kr[:, 0:32],
                                             127.0, None, ALU.mult), after=[sc3])
        d1 = em.dma(sscr[0:TP].rearrange("(k r) -> k r", k=68), s2kr[:, 0:32],
                    after=[sc4])
        d3 = em.dma(sscr[TP:2 * TP].rearrange("(k r) -> k r", k=68),
                    ivkr[:, 0:32], after=[sc2])
        d4 = em.dma(iv_pu[:, 0:NT], sscr[TP:2 * TP].rearrange("(u p) -> p u",
                                                              p=128),
                    after=[d3], allow_slow_non_contiguous=True)
        ldw2 = em.dma(bigb[:].rearrange("p (a c) -> p a c", a=32),
                      w2qc.rearrange("(a p) c -> p a c", p=128),
                      eng="gpsimd",
                      after=[wq2_tok, ("v", em.cnt["v"]), ("s", em.cnt["s"]),
                             ("t", em.cnt["t"])])
        prev_w2 = None
        prev_epis = []
        for tc, (t0, tw) in enumerate(
                [(0, 512), (512, 512), (1024, 512), (1536, 512), (2048, 128)]):
            d2 = em.dma(s2fm[:, 0:tw], sscr[t0:t0 + tw][None, :],
                        after=[d1, ("t", em.cnt["t"])])
            sbc = em.t(lambda e, tw=tw:
                e.matmul(pSb.ap[:, 0:tw], ones_f[:], s2fm[:, 0:tw],
                         start=True, stop=True), after=[d2] + pSb.wtoks())
            q_toks = []
            for ct in range(32):
                ldh = em.dma(cvu[:, 0:tw],
                             hsbuf[ct * 128:(ct + 1) * 128, t0:t0 + tw],
                             after=[hs_toks[ct]] +
                                   ([q_toks[-1]] if q_toks else []))
                qa = pSb.read(em.v(lambda e, ct=ct, tw=tw:
                    e.tensor_tensor(xq2i[:, 0:tw], cvu[:, 0:tw],
                                    pSb.ap[:, 0:tw], ALU.mult),
                    after=[ldh, sbc, prev_w2] +
                          ([q_toks[-1]] if q_toks else [])))
                qb = em.s(lambda e, ct=ct, tw=tw:
                    e.activation(xq2c[:, ct * 512:ct * 512 + tw],
                                 xq2i[:, 0:tw], AF.Copy), after=[qa])
                q_toks.append(qb)
            nu = (tw + 127) // 128
            for u in range(nu):
                uw = min(128, tw - u * 128)
                tglob = t0 + u * 128
                jt = tglob // 128
                epis = []
                for dh in range(2):
                    bk = pW[dh]
                    mm = None
                    for ct in range(32):
                        mm = em.t(lambda e, bk=bk, ct=ct, u=u, dh=dh, uw=uw:
                            e.matmul(bk.ap[0:uw, 0:512],
                                     xq2c[:, ct * 512 + u * 128:
                                          ct * 512 + u * 128 + uw],
                                     bigb[:, ct * 1024 + dh * 512:
                                          ct * 1024 + (dh + 1) * 512],
                                     start=(ct == 0), stop=(ct == 31)),
                            after=([ldw2, q_toks[ct]] + bk.wtoks())
                            if ct == 0 else [q_toks[ct]])
                    w2_mm_last = mm
                    if dh == 0:
                        rap, rnval = srcrow(tglob, uw)
                        if rnval < uw:
                            mz = em.v(lambda e: e.memset(xt2[:], 0.0),
                                      after=prev_epis)
                            ldr = [em.dma(xt2[0:rnval, :], rap,
                                          after=prev_epis + [mz])]
                        else:
                            ldr = [em.dma(xt2[0:rnval, :], rap,
                                          after=prev_epis)]
                    epi = bk.read(em.v(lambda e, bk=bk, dh=dh, uw=uw, jt=jt:
                        e.scalar_tensor_tensor(
                            x1t2[0:uw, dh * 512:(dh + 1) * 512],
                            bk.ap[0:uw, 0:512], iv_pu[:, jt:jt + 1],
                            xt2[0:uw, dh * 512:(dh + 1) * 512],
                            ALU.mult, ALU.add),
                        after=[mm, d4] + ldr))
                    epis.append(epi)
                prev_epis = list(epis)
                prev_w2 = w2_mm_last
                if dst[0] == "dram":
                    em.dma(dst[1][tglob:tglob + uw, :], x1t2[0:uw, :],
                           after=epis)
                else:
                    lo, hi = max(8, tglob), min(2056, tglob + uw)
                    if lo < hi:
                        yv = yout.rearrange("b s d -> s b d")
                        em.dma(yv[(lo - 8) // 4:(hi - 8) // 4],
                               sb3(x1t2[lo - tglob:hi - tglob, :]),
                               after=epis)
        em.barrier()
        if dbgname and dbgname in dbg_aps and dst[0] == "dram":
            em.dma(dbg_aps[dbgname], dst[1], after=[("dsy", em.cnt["dsy"])])
            em.barrier()

    akr2 = sb2("akr2", [68, 128], F32)
    x1t2 = h0f

    x1v3 = x1buf.rearrange("b s d -> s b d")
    if "x1" in dbg_aps:
        em.dma(dbg_aps["x1"], x1buf, after=[("dsy", em.cnt["dsy"])])
        em.barrier()
    if "ffn1" in phases:
        ffn(lambda t0, rows: (x1v3[t0 // 4:min(t0 + rows, 4 * WIN) // 4],
                              max(0, min(t0 + rows, 4 * WIN) - t0)),
            ("dram", x2buf), dbgname="x2")
    em.flush()

    if "attnb" in phases:
        lnn = plain_ln_to(lambda j, tok: transpose_to_xqT(j, htm2, tok),
                          lambda t0, rows: (x2buf[t0:t0 + rows, :], rows),
                          quant=False)
        em.barrier()
        WOB0 = 8 * TP
        ld_wob = em.dma(bigb[:, WOB0:WOB0 + 8 * D].rearrange("p (a c) -> p a c", a=8),
                        woutTb.rearrange("(a p) c -> p a c", p=128), eng="gpsimd",
                        after=[("s", em.cnt["s"]), ("t", em.cnt["t"])])
        winv = winT.rearrange("(a p) c -> p a c", p=128)
        for pp in range(8):
            ch_evs = []
            for which in range(3):
                roff = which * D + pp * 128
                ldwin = em.dma(winsl[which % 2][:].rearrange("p (a c) -> p a c", a=8),
                               winv[:, :, roff:roff + 128], eng="gpsimd",
                               after=[("t", em.cnt["t"])])
                for (n0, nw) in [(0, 512), (512, 512), (1024, 512), (1536, 512),
                                 (2048, 128)]:
                    bk = pW[which % 2]
                    mm = None
                    for dc in range(8):
                        mm = em.t(lambda e, bk=bk, which=which, dc=dc, n0=n0, nw=nw:
                            e.matmul(bk.ap[:, 0:nw],
                                     winsl[which % 2][:, dc * 128:(dc + 1) * 128],
                                     xqT[:, dc * HTW + 4 + n0:dc * HTW + 4 + n0 + nw],
                                     start=(dc == 0), stop=(dc == 7)),
                            after=([ldwin] + lnn[-1:] + bk.wtoks())
                            if dc == 0 else ())
                    if which == 0:
                        ev = bk.read(em.v(lambda e, which=which, pp=pp, n0=n0, nw=nw,
                                                 bk=bk:
                            e.tensor_scalar(qkv_c[which][:, n0:n0 + nw],
                                            bk.ap[:, 0:nw],
                                            binb_s[:, which * 8 + pp:
                                                   which * 8 + pp + 1],
                                            0.125, ALU.add, ALU.mult),
                            after=[mm, ld_bb]))
                    else:
                        ev = bk.read(em.v(lambda e, which=which, pp=pp, n0=n0, nw=nw,
                                                 bk=bk:
                            e.tensor_scalar(qkv_c[which][:, n0:n0 + nw],
                                            bk.ap[:, 0:nw],
                                            binb_s[:, which * 8 + pp:
                                                   which * 8 + pp + 1],
                                            None, ALU.add),
                            after=[mm, ld_bb]))
                    ch_evs.append(ev)
            if pp == 0:
                did = False
                for wn in range(3):
                    nm = "qkb%d" % wn
                    if nm in dbg_aps:
                        em.dma(dbg_aps[nm], qkv_c[wn][:], after=ch_evs)
                        did = True
                if "lnb" in dbg_aps:
                    em.dma(dbg_aps["lnb"], xqT[:, 4:4 + TP], after=ch_evs)
                    did = True
                if did:
                    em.barrier()
            for j in range(NT):
                vt = em.t(lambda e, j=j:
                    e.transpose(pTb2.ap[:, 0:128],
                                qkv_c[2][:, j * 128:(j + 1) * 128], idb[:]),
                    after=ch_evs + pTb2.wtoks())
                vte = pTb2.read(em.s(lambda e:
                    e.activation(vtm[:], pTb2.ap[:, 0:128], AF.Copy), after=[vt]))
                for hl in range(2):
                    sm = em.t(lambda e, j=j, hl=hl:
                        e.matmul(pSV.ap[:, 0:128],
                                 qkv_c[0][hl * 64:hl * 64 + 64, j * 128:(j + 1) * 128],
                                 qkv_c[1][hl * 64:hl * 64 + 64, j * 128:(j + 1) * 128],
                                 start=True, stop=True),
                        after=ch_evs + pSV.wtoks())
                    ms = pSV.read(em.v(lambda e:
                        e.tensor_add(cvt[:, 0:128], pSV.ap[:, 0:128], bmask_s[:]),
                        after=[sm, ld_bm]))
                    mx = em.v(lambda e: e.reduce_max(smc[:, 0:1], cvt[:, 0:128],
                                                     axis=AX.X, negate=True),
                              after=[ms])
                    ex = em.s(lambda e: e.activation(pnr[:], cvt[:, 0:128], AF.Exp,
                                                     bias=smc[:, 0:1], scale=1.0,
                                                     accum_out=smc[:, 1:2]),
                              after=[mx])
                    rc = em.v(lambda e: e.reciprocal(smc[:, 2:3], smc[:, 1:2]),
                              after=[ex])
                    pn = em.v(lambda e: e.tensor_scalar(pnr[:], pnr[:], smc[:, 2:3],
                                                        None, ALU.mult), after=[rc])
                    tp_ = em.t(lambda e: e.transpose(pTb2.ap[:, 0:128], pnr[:],
                                                     idb[:]),
                               after=[pn, vte] + pTb2.wtoks())
                    pte = pTb2.read(em.s(lambda e:
                        e.activation(pnt[:], pTb2.ap[:, 0:128], AF.Copy),
                        after=[tp_]))
                    av = em.t(lambda e, hl=hl:
                        e.matmul(pSV.ap[hl * 64:hl * 64 + 64, 0:128],
                                 vtm[:, hl * 64:hl * 64 + 64], pnt[:],
                                 start=True, stop=True),
                        after=[pte] + pSV.wtoks())
                    oe = pSV.read(em.s(lambda e, pp=pp, j=j, hl=hl:
                        e.activation(bigb[:, pp * TP + j * 128:pp * TP + (j + 1) * 128]
                                     [hl * 64:hl * 64 + 64, :],
                                     pSV.ap[hl * 64:hl * 64 + 64, 0:128],
                                     AF.Copy), after=[av]))
            if pp == 0 and "ob0" in dbg_aps:
                em.barrier()
                em.dma(dbg_aps["ob0"], bigb[:, 0:TP],
                       after=[("s", em.cnt["s"])])
                em.barrier()
        em.barrier()
        prev_eps = []
        for j in range(NT):
            ldr = em.dma(xt2[:], x2buf[j * 128:(j + 1) * 128, :], after=prev_eps)
            epis = []
            for dh in range(2):
                bk = pW[dh]
                mm = None
                for oc in range(8):
                    mm = em.t(lambda e, bk=bk, j=j, dh=dh, oc=oc:
                        e.matmul(bk.ap[:, 0:512],
                                 bigb[:, oc * TP + j * 128:oc * TP + (j + 1) * 128],
                                 bigb[:, WOB0 + oc * D + dh * 512:
                                      WOB0 + oc * D + dh * 512 + 512],
                                 start=(oc == 0), stop=False),
                        after=([ld_wob, ("s", em.cnt["s"])] + bk.wtoks())
                        if oc == 0 else ())
                mm = em.t(lambda e, bk=bk, dh=dh:
                    e.matmul(bk.ap[:, 0:512], ones_b[0:1, 0:128],
                             boutf_s[0:1, dh * 512:(dh + 1) * 512],
                             start=False, stop=True),
                    after=[ld_bf])
                ep = bk.read(em.v(lambda e, bk=bk, dh=dh:
                    e.tensor_add(x1t2[:, dh * 512:(dh + 1) * 512],
                                 bk.ap[:, 0:512],
                                 xt2[:, dh * 512:(dh + 1) * 512]),
                    after=[mm, ldr]))
                epis.append(ep)
            prev_eps = list(epis)
            em.dma(x3buf[j * 128:(j + 1) * 128, :], x1t2[:], after=epis)
        em.barrier()
        if "x3" in dbg_aps:
            em.dma(dbg_aps["x3"], x3buf, after=[("dsy", em.cnt["dsy"])])
            em.barrier()
        em.flush()

    if "ffn2" in phases:
        ffn(lambda t0, rows: (x3buf[t0:t0 + rows, :], rows), ("yout",))
    for eng in Em.ENGS:
        em.waits(eng, [("dsy", em.cnt["dsy"]), ("dgp", em.cnt["dgp"]),
                       ("v", em.cnt["v"]), ("s", em.cnt["s"]),
                       ("t", em.cnt["t"])])
    em.flush()
    g2.close()
    octx.close()
    return nc


def _host_constants():
    inv_freq = 1.0 / 10000.0 ** (np.arange(0, DH, 2, dtype=np.float32) / DH)
    t = np.arange(S, dtype=np.float32)
    freqs = np.outer(t, inv_freq)
    cos = np.concatenate([np.cos(freqs), np.cos(freqs)], -1)[:, ::2].T
    sin = np.concatenate([np.sin(freqs), np.sin(freqs)], -1)[:, ::2].T
    C64 = np.zeros((64, S), np.float32)
    St64 = np.zeros((64, S), np.float32)
    for i in range(32):
        C64[2 * i] = cos[i]
        C64[2 * i + 1] = cos[i]
        St64[2 * i] = sin[i]
        St64[2 * i + 1] = -sin[i]
    kC = np.concatenate([C64, C64], 0)
    kS = np.concatenate([St64, St64], 0)
    ident = np.eye(128, dtype=np.float32)
    bmask = np.full((128, 128), -1e30, np.float32)
    for g in range(32):
        bmask[4 * g:4 * g + 4, 4 * g:4 * g + 4] = 0.0
    return kC, kS, ident, bmask


def kernel(**inputs):
    x = np.asarray(inputs["L0_attn_a_x"], np.float32)
    kC, kS, ident, bmask = _host_constants()
    asc = np.ascontiguousarray
    base = {
        "xfull": x,
        "wdqT": asc(np.asarray(inputs["L0_attn_a_w_down_q"], np.float32).T),
        "wdkvT": asc(np.asarray(inputs["L0_attn_a_w_down_kv"], np.float32).T),
        "wuqT": asc(np.asarray(inputs["L0_attn_a_w_up_q"], np.float32).T),
        "wukT": asc(np.asarray(inputs["L0_attn_a_w_up_k"], np.float32).T),
        "wuvT": asc(np.asarray(inputs["L0_attn_a_w_up_v"], np.float32).T),
        "woutTa": asc(np.asarray(inputs["L0_attn_a_w_out"], np.float32).T),
        "lnsc": asc(np.asarray(inputs["L0_attn_a_ln_scale"],
                                np.float32).reshape(8, 128).T),
        "lnbi": asc(np.asarray(inputs["L0_attn_a_ln_bias"],
                               np.float32).reshape(8, 128).T),
        "winT": asc(np.asarray(inputs["L0_attn_b_in_proj_weight"], np.float32).T),
        "binb": asc(np.asarray(inputs["L0_attn_b_in_proj_bias"],
                               np.float32).reshape(24, 128).T),
        "woutTb": asc(np.asarray(inputs["L0_attn_b_out_proj_weight"],
                                 np.float32).T),
        "boutb": asc(np.asarray(inputs["L0_attn_b_out_proj_bias"],
                                np.float32).reshape(8, 128).T),
        "boutf": asc(np.asarray(inputs["L0_attn_b_out_proj_bias"],
                                np.float32).reshape(1, D)),
        "w1T": asc(np.asarray(inputs["L0_ffn_w1"], np.float32).T),
        "w2T": asc(np.asarray(inputs["L0_ffn_w2"], np.float32).T),
        "cw0": asc(np.asarray(inputs["L0_ffn_conv_w"],
                              np.float32)[:, 0, 0].reshape(32, 128).T),
        "cw1": asc(np.asarray(inputs["L0_ffn_conv_w"],
                              np.float32)[:, 0, 1].reshape(32, 128).T),
        "cw2": asc(np.asarray(inputs["L0_ffn_conv_w"],
                              np.float32)[:, 0, 2].reshape(32, 128).T),
        "cbv": asc(np.asarray(inputs["L0_ffn_conv_b"],
                              np.float32).reshape(32, 128).T),
        "ident": ident, "kC": kC, "kS": kS, "bmask": bmask,
    }
    xpad = np.zeros((B, S + 4, D), np.float32)
    xpad[:, 2:2 + S, :] = x
    kCp = np.zeros((128, S + 4), np.float32)
    kSp = np.zeros((128, S + 4), np.float32)
    kCp[:, 2:2 + S] = kC / 8.0
    kSp[:, 2:2 + S] = kS / 8.0
    in_maps = []
    for c in range(NCORES):
        m = dict(base)
        w0 = 512 * c
        m["xwin"] = asc(xpad[:, w0:w0 + WIN, :])
        m["qC"] = asc(kCp[:, w0:w0 + WIN])
        m["qS"] = asc(kSp[:, w0:w0 + WIN])
        mv = np.zeros(TP, np.float32)
        for sl in range(WIN):
            g = w0 + sl - 2
            if 0 <= g < S:
                mv[4 * sl:4 * sl + 4] = 1.0
        m["maskv"] = asc(mv.reshape(17, 128).T)
        in_maps.append(m)

    nc = build_nc()
    res = run_bass_kernel_spmd(nc, in_maps, core_ids=list(range(NCORES)))
    out = np.empty((B, S, D), np.float32)
    for c in range(NCORES):
        out[:, 512 * c:512 * (c + 1), :] = res.results[c]["yout"]
    return out


# version 04
# speedup vs baseline: 1.4003x; 1.1887x over previous
import numpy as np
from contextlib import ExitStack

import concourse.bass as bass
import concourse.mybir as mybir
from concourse.bass_utils import run_bass_kernel_spmd

F32 = mybir.dt.float32
BF16 = mybir.dt.bfloat16
I32 = mybir.dt.int32
AF = mybir.ActivationFunctionType
ALU = mybir.AluOpType
AX = mybir.AxisListType

NCORES = 8
B, S, D = 4, 4096, 1024
DH = 64
CD = 512
DFF = 4096
OWN = 512
WIN = 516
NT = 17
TP = NT * 128
EPS = 1e-5
NKT = S // 128
QT = [128, 128, 128, 128, 4]
QCH = [(0, 258), (258, 258)]
SHUF = [i + 1 if i % 2 == 0 else i - 1 for i in range(32)]
HTW = TP + 8


class Em:
    ENGS = ("sync", "gpsimd", "vector", "scalar", "tensor")

    def __init__(self, nc, ctx):
        self.nc = nc
        self.plan = {e: [] for e in self.ENGS}
        self.sems = {}
        self.cnt = {}
        for name in ("dsy", "dgp", "v", "s", "t"):
            self.sems[name] = ctx.enter_context(nc.semaphore(name))
            self.cnt[name] = 0

    def waits(self, eng, toks):
        best = {}
        for tk in toks:
            if tk is None:
                continue
            n, v = tk
            assert v <= self.cnt[n], (n, v, self.cnt[n])
            best[n] = max(best.get(n, 0), v)
        for n, v in best.items():
            h = self.sems[n]
            self.plan[eng].append(lambda e, h=h, v=v: e.wait_ge(h, v))

    def op(self, eng, fn, inc=None, k=1):
        if inc is None:
            self.plan[eng].append(lambda e, fn=fn: fn(e))
            return None
        h = self.sems[inc]
        self.plan[eng].append(lambda e, fn=fn, h=h, k=k: fn(e).then_inc(h, k))
        self.cnt[inc] += k
        return (inc, self.cnt[inc])

    def dma(self, out, in_, after=(), eng="sync", **kw):
        self.waits(eng, after)
        sem = "dgp" if eng == "gpsimd" else "dsy"
        return self.op(eng, lambda e: e.dma_start(out=out, in_=in_, **kw),
                       inc=sem, k=16)

    def v(self, fn, after=()):
        self.waits("vector", after)
        return self.op("vector", fn, inc="v")

    def s(self, fn, after=()):
        self.waits("scalar", after)
        return self.op("scalar", fn, inc="s")

    def t(self, fn, after=()):
        self.waits("tensor", after)
        return self.op("tensor", fn, inc="t")

    def barrier(self):
        snap = [(n, v) for n, v in self.cnt.items() if v > 0]
        for eng in self.ENGS:
            self.waits(eng, snap)

    def flush(self):
        with self.nc.Block() as blk:
            for eng in self.ENGS:
                ops = self.plan[eng]
                if not ops:
                    continue
                def body(e, ops=ops):
                    for f in ops:
                        f(e)
                getattr(blk, eng)(body)
        self.plan = {e: [] for e in self.ENGS}


class Bank:
    def __init__(self, ap):
        self.ap = ap
        self.consumers = []

    def wtoks(self):
        t, self.consumers = self.consumers, []
        return t

    def read(self, tok):
        self.consumers.append(tok)
        return tok


def build_nc(dbg=(), phases=("g1", "ffn1", "attnb", "ffn2")):
    nc = bass.Bass("TRN2", target_bir_lowering=False)

    def din(n, sh):
        return nc.dram_tensor(n, sh, F32, kind="ExternalInput").ap()

    xwin = din("xwin", [B, WIN, D])
    xfull = din("xfull", [B, S, D])
    wdqT = din("wdqT", [D, CD])
    wdkvT = din("wdkvT", [D, CD])
    wuqT = din("wuqT", [CD, D])
    wukT = din("wukT", [CD, D])
    wuvT = din("wuvT", [CD, D])
    woutTa = din("woutTa", [D, D])
    lnsc = din("lnsc", [128, 8])
    lnbi = din("lnbi", [128, 8])
    winT = din("winT", [D, 3 * D])
    binb = din("binb", [128, 24])
    woutTb = din("woutTb", [D, D])
    boutb = din("boutb", [128, 8])
    boutf = din("boutf", [1, D])
    w1T = din("w1T", [D, DFF])
    w2T = din("w2T", [DFF, D])
    cw0 = din("cw0", [128, 32])
    cw1 = din("cw1", [128, 32])
    cw2 = din("cw2", [128, 32])
    cbv = din("cbv", [128, 32])
    ident = din("ident", [128, 128])
    kC = din("kC", [128, S])
    kS = din("kS", [128, S])
    qC = din("qC", [128, WIN])
    qS = din("qS", [128, WIN])
    bmask = din("bmask", [128, 128])
    maskv = din("maskv", [128, NT])

    yout = nc.dram_tensor("yout", [B, OWN, D], F32, kind="ExternalOutput").ap()

    r1buf = nc.dram_tensor("r1buf", [B, WIN, D], F32).ap()
    x1buf = nc.dram_tensor("x1buf", [B, WIN, D], F32).ap()
    x2buf = nc.dram_tensor("x2buf", [TP, D], F32).ap()
    x3buf = nc.dram_tensor("x3buf", [TP, D], F32).ap()
    w1qc = nc.dram_tensor("w1qc", [D, DFF], BF16).ap()
    w2qc = nc.dram_tensor("w2qc", [DFF, D], BF16).ap()
    hsbuf = nc.dram_tensor("hsbuf", [DFF, TP], F32).ap()
    sscr = nc.dram_tensor("sscr", [3 * TP], F32).ap()

    dbg_aps = {}
    _dbg_shapes = {"x1": ([B, WIN, D], F32), "x2": ([TP, D], F32),
                   "x3": ([TP, D], F32),
                   "ckv": ([128, 4 * S], BF16), "cq": ([128, 4 * WIN], BF16),
                   "krot": ([128, S], BF16), "qrot": ([128, WIN], BF16),
                   "nmax": ([128, 10], F32), "oacc": ([128, 8 * WIN], BF16),
                   "vsb": ([128, NKT * 520], BF16),
                   "htm": ([128, D], BF16), "hfm": ([128, 8 * 512], BF16),
                   "sts": ([128, 32], F32),
                   "qkb0": ([128, TP], BF16), "qkb1": ([128, TP], BF16),
                   "qkb2": ([128, TP], BF16), "ob0": ([128, TP], BF16),
                   "lnb": ([128, TP], BF16)}
    for name in dbg:
        sh, dty = _dbg_shapes[name]
        dbg_aps[name] = nc.dram_tensor("dbg_" + name, sh, dty,
                                       kind="ExternalOutput").ap()

    octx = ExitStack()
    sbg = lambda n, sh, dty: octx.enter_context(nc.sbuf_tensor(n, sh, dty))
    em = Em(nc, octx)

    id32 = sbg("id32", [128, 128], F32)
    idb = sbg("idb", [128, 128], BF16)
    ones_b = sbg("ones_b", [1, 128], BF16)
    ones_f = sbg("ones_f", [1, 128], F32)
    ones_p = sbg("ones_p", [128, 1], F32)
    lnsc_f = sbg("lnsc_f", [128, 8], F32)
    lnbi_f = sbg("lnbi_f", [128, 8], F32)
    lnbi_b = sbg("lnbi_b", [128, 8], BF16)
    maskv_s = sbg("maskv_s", [128, NT], F32)
    st1 = sbg("st1", [128, 8], F32)
    st2 = sbg("st2", [128, 8], F32)
    st3 = sbg("st3", [128, 8], F32)
    st4 = sbg("st4", [128, 8], F32)
    st5 = sbg("st5", [128, 40], F32)

    ld_id = em.dma(id32[:], ident)
    ld_idb = em.dma(idb[:], ident, eng="gpsimd")
    em.op("vector", lambda e: e.memset(ones_b[:], 1.0))
    em.op("vector", lambda e: e.memset(ones_f[:], 1.0))
    em.op("vector", lambda e: e.memset(ones_p[:], 1.0))
    ones_tok = em.v(lambda e: e.memset(st5[:], 0.0))
    ld_g = em.dma(lnsc_f[:], lnsc)
    ld_bi = em.dma(lnbi_f[:], lnbi)
    ld_bib = em.dma(lnbi_b[:], lnbi, eng="gpsimd")
    ld_mv = em.dma(maskv_s[:], maskv)

    def ln_stats(xa, toks, prev_norm, two, rows=128):
        R = rows
        t_sum = em.v(lambda e: e.reduce_sum(st1[0:R, 0:1], xa, axis=AX.X),
                     after=list(toks) + [prev_norm])
        t_sq = em.s(lambda e: e.activation(xsq[0:R, 0:D], xa, AF.Square,
                                           accum_out=st1[0:R, 1:2]),
                    after=list(toks) + [prev_norm])
        t1 = em.v(lambda e: e.tensor_scalar(st2[0:R, 0:2], st1[0:R, 0:2],
                                            1.0 / D, None, ALU.mult),
                  after=[t_sum, t_sq])
        t2 = em.v(lambda e: e.tensor_tensor(st2[0:R, 2:3], st2[0:R, 0:1],
                                            st2[0:R, 0:1], ALU.mult), after=[t1])
        t3 = em.v(lambda e: e.tensor_tensor(st2[0:R, 1:2], st2[0:R, 1:2],
                                            st2[0:R, 2:3], ALU.subtract), after=[t2])
        q1 = em.s(lambda e: e.activation(st3[0:R, 0:1], st2[0:R, 1:2], AF.Sqrt,
                                         bias=np.float32(EPS), scale=1.0), after=[t3])
        rsv = em.v(lambda e: e.reciprocal(st3[0:R, 0:1], st3[0:R, 0:1]), after=[q1])
        if not two:
            b1 = em.v(lambda e: e.tensor_tensor(st4[0:R, 1:2], st2[0:R, 0:1],
                                                st3[0:R, 0:1], ALU.mult), after=[rsv])
            b2 = em.v(lambda e: e.tensor_scalar(st4[0:R, 1:2], st4[0:R, 1:2],
                                                -1.0, None, ALU.mult), after=[b1])
            return b2
        v1 = em.v(lambda e: e.tensor_tensor(st2[0:R, 3:4], st3[0:R, 0:1],
                                            st3[0:R, 0:1], ALU.mult), after=[rsv])
        v2 = em.v(lambda e: e.tensor_tensor(st2[0:R, 4:5], st2[0:R, 1:2],
                                            st2[0:R, 3:4], ALU.mult), after=[v1])
        q2 = em.s(lambda e: e.activation(st3[0:R, 1:2], st2[0:R, 4:5], AF.Sqrt,
                                         bias=np.float32(EPS), scale=1.0), after=[v2])
        q3 = em.v(lambda e: e.reciprocal(st3[0:R, 1:2], st3[0:R, 1:2]), after=[q2])
        s1 = em.v(lambda e: e.tensor_tensor(st3[0:R, 3:4], st3[0:R, 0:1],
                                            st3[0:R, 1:2], ALU.mult), after=[q3])
        b1 = em.v(lambda e: e.tensor_tensor(st4[0:R, 0:1], st2[0:R, 0:1],
                                            st3[0:R, 3:4], ALU.mult), after=[s1])
        b2 = em.v(lambda e: e.tensor_scalar(st4[0:R, 0:1], st4[0:R, 0:1],
                                            -1.0, None, ALU.mult), after=[b1])
        return b2

    g1 = ExitStack()
    sb1 = lambda n, sh, dty: g1.enter_context(nc.sbuf_tensor(n, sh, dty))
    pb = [Bank(g1.enter_context(nc.psum_tensor(f"pb{i}", [128, 512], F32)))
          for i in range(7)]
    pA = [pb[0], pb[1]]
    pB = [pb[2], pb[3]]
    pO = [pb[4], pb[5]]
    pR = pb[6]
    pTb = Bank(g1.enter_context(nc.psum_tensor("pTb", [128, 1024], BF16)))

    wdqT_s = sb1("wdqT_s", [128, 8 * CD], BF16)
    wdkvT_s = sb1("wdkvT_s", [128, 8 * CD], BF16)
    wuqT_s = sb1("wuqT_s", [128, 4 * D], BF16)
    wukT_s = sb1("wukT_s", [128, 4 * D], BF16)
    wuvT_s = sb1("wuvT_s", [128, 4 * D], BF16)
    kC_s = sb1("kC_s", [128, S], BF16)
    kS_s = sb1("kS_s", [128, S], BF16)
    qC_s = sb1("qC_s", [128, WIN], BF16)
    qS_s = sb1("qS_s", [128, WIN], BF16)
    xta = sb1("xta", [128, D], F32)
    xtb = sb1("xtb", [128, D], F32)
    xsq = sb1("xsq", [128, D], F32)
    htm = sb1("htm", [128, D], BF16)
    hfm = sb1("hfm", [128, 8 * 512], BF16)
    hwin = sb1("hwin", [128, 8 * WIN], BF16)
    ckv = sb1("ckv", [128, 4 * S], BF16)
    cq = sb1("cq", [128, 4 * WIN], BF16)
    kraw = sb1("kraw", [128, 512], BF16)
    ksh = sb1("ksh", [128, 512], BF16)
    ktp = sb1("ktp", [128, 512], BF16)
    krot = sb1("krot", [128, S], BF16)
    qraw = sb1("qraw", [128, WIN], BF16)
    qsh = sb1("qsh", [128, WIN], BF16)
    qtp = sb1("qtp", [128, WIN], BF16)
    qrot = sb1("qrot", [128, WIN], BF16)
    vsb = sb1("vsb", [128, NKT * 520], BF16)
    expS = [sb1(f"expS{i}", [128, 258], BF16) for i in range(3)]
    oacc = sb1("oacc", [128, 8 * WIN], BF16)
    sqk = sb1("sqk", [128, 512], BF16)
    sqq = sb1("sqq", [128, WIN], BF16)
    bst = sb1("bst", [1, 8], F32)
    onec_b = sb1("onec_b", [128, 1], BF16)
    negBc = sb1("negBc", [128, 32], F32)
    recip = sb1("recip", [1, 258], F32)
    rbc = sb1("rbc", [64, 258], F32)
    bqkv = sb1("bqkv", [128, 8], F32)
    r1t = sb1("r1t", [128, D], F32)
    x1t = sb1("x1t", [128, D], F32)

    onec_tok = em.op("vector", lambda e: e.memset(onec_b[:], 1.0), inc="v")

    wl = {}
    wl["dq"] = em.dma(wdqT_s[:].rearrange("p (a c) -> p a c", a=8),
                      wdqT.rearrange("(a p) c -> p a c", p=128), eng="gpsimd")
    wl["dkv"] = em.dma(wdkvT_s[:].rearrange("p (a c) -> p a c", a=8),
                       wdkvT.rearrange("(a p) c -> p a c", p=128), eng="gpsimd")
    wl["uq"] = em.dma(wuqT_s[:].rearrange("p (a c) -> p a c", a=4),
                      wuqT.rearrange("(a p) c -> p a c", p=128), eng="gpsimd")
    wl["uk"] = em.dma(wukT_s[:].rearrange("p (a c) -> p a c", a=4),
                      wukT.rearrange("(a p) c -> p a c", p=128), eng="gpsimd")
    wl["uv"] = em.dma(wuvT_s[:].rearrange("p (a c) -> p a c", a=4),
                      wuvT.rearrange("(a p) c -> p a c", p=128), eng="gpsimd")
    wl["kC"] = em.dma(kC_s[:], kC, eng="gpsimd")
    wl["kS"] = em.dma(kS_s[:], kS, eng="gpsimd")
    wl["qC"] = em.dma(qC_s[:], qC, eng="gpsimd")
    wl["qS"] = em.dma(qS_s[:], qS, eng="gpsimd")

    mmtok = None
    for col, w_s, wt in ((0, wdqT_s, wl["dq"]), (4, wdkvT_s, wl["dkv"])):
        for jc in range(4):
            for dc in range(8):
                mmtok = em.t(lambda e, col=col, w_s=w_s, jc=jc, dc=dc:
                    e.matmul(pR.ap[:, col + jc:col + jc + 1],
                             w_s[:, dc * CD + jc * 128:dc * CD + jc * 128 + 128],
                             lnbi_b[:, dc:dc + 1],
                             start=(dc == 0), stop=(dc == 7)),
                    after=[wt, ld_bib] if dc == 0 else ())
    ev_bias = pR.read(em.v(lambda e: e.tensor_copy(bqkv[:], pR.ap[:, 0:8]),
                           after=[mmtok]))
    folds = []
    for w_s, wt in ((wdqT_s, wl["dq"]), (wdkvT_s, wl["dkv"])):
        for dc in range(8):
            folds.append(em.v(lambda e, w_s=w_s, dc=dc:
                e.tensor_scalar(w_s[:, dc * CD:(dc + 1) * CD],
                                w_s[:, dc * CD:(dc + 1) * CD],
                                lnsc_f[:, dc:dc + 1], None, ALU.mult),
                after=[wt, ld_g, ev_bias]))
    fold_tok = folds[-1]

    prev_norm = ones_tok
    last_exp = {0: None, 1: None}
    xbuf_last = {0: [], 1: []}
    htm_last = []

    for b in range(B):
        r1_toks = []
        hwin_evs = []
        for j in range(5):
            rows = QT[j]
            xv = xta if j % 2 == 0 else xtb
            ld = em.dma(xv[0:rows, :], xwin[b, j * 128:j * 128 + rows, :],
                        after=xbuf_last[j % 2])
            xbuf_last[j % 2] = []
            btok = ln_stats(xv[0:rows, :], [ld], prev_norm, two=True,
                            rows=rows)
            h0 = em.v(lambda e, xv=xv, rows=rows:
                e.tensor_scalar(r1t[0:rows, :], xv[0:rows, :], st2[0:rows, 0:1],
                                st3[0:rows, 0:1], ALU.subtract, ALU.mult),
                after=[btok])
            r1 = em.v(lambda e, xv=xv, rows=rows:
                e.tensor_add(r1t[0:rows, :], r1t[0:rows, :], xv[0:rows, :]),
                after=[h0])
            r1_toks.append(em.dma(r1buf[b, j * 128:j * 128 + rows, :],
                                  r1t[0:rows, :], after=[r1]))
            hh = em.s(lambda e, xv=xv, rows=rows:
                e.activation(htm[0:rows, :], xv[0:rows, :], AF.Identity,
                             bias=st4[0:rows, 0:1], scale=st3[0:rows, 3:4]),
                after=[btok, h0] + htm_last)
            htm_last = []
            prev_norm = hh
            xbuf_last[j % 2] = [hh, r1]
            for dc in range(8):
                bk = pTb
                tr = em.t(lambda e, bk=bk, dc=dc, rows=rows:
                    e.transpose(bk.ap[:, 0:rows],
                                htm[0:rows, dc * 128:(dc + 1) * 128],
                                idb[0:rows, 0:rows]),
                    after=[hh, ld_idb] + bk.wtoks())
                ev = bk.read(em.s(lambda e, bk=bk, dc=dc, j=j, rows=rows:
                    e.activation(hwin[:, dc * WIN + j * 128:
                                       dc * WIN + j * 128 + rows],
                                 bk.ap[:, 0:rows], AF.Copy), after=[tr]))
                hwin_evs.append(ev)
                htm_last.append(tr)
        last_exp = {0: None, 1: None}
        cq_evs = []
        for cc in range(4):
            for (q0, qw) in QCH:
                mm = None
                for dc in range(8):
                    mm = em.t(lambda e, cc=cc, dc=dc, q0=q0, qw=qw:
                        e.matmul(pB[0].ap[:, 0:qw],
                                 wdqT_s[:, dc * CD + cc * 128:
                                        dc * CD + cc * 128 + 128],
                                 hwin[:, dc * WIN + q0:dc * WIN + q0 + qw],
                                 start=(dc == 0), stop=(dc == 7)),
                        after=([fold_tok] + hwin_evs + pB[0].wtoks())
                        if dc == 0 else ())
                ev = pB[0].read(em.s(lambda e, cc=cc, q0=q0, qw=qw:
                    e.activation(cq[:, cc * WIN + q0:cc * WIN + q0 + qw],
                                 pB[0].ap[:, 0:qw], AF.Identity,
                                 bias=bqkv[:, cc:cc + 1], scale=1.0),
                    after=[mm, ev_bias]))
                cq_evs.append(ev)

        ckv_evs = []
        for stile in range(8):
            hfm_evs = []
            for i in range(4):
                tile = stile * 4 + i
                xv = xta if i % 2 == 0 else xtb
                ld = em.dma(xv[:], xfull[b, tile * 128:(tile + 1) * 128, :],
                            after=xbuf_last[i % 2])
                xbuf_last[i % 2] = []
                btok = ln_stats(xv[:, :], [ld], prev_norm, two=True)
                hh = em.s(lambda e, xv=xv:
                    e.activation(htm[:], xv[:], AF.Identity,
                                 bias=st4[:, 0:1], scale=st3[:, 3:4]),
                    after=[btok] + htm_last)
                htm_last = []
                prev_norm = hh
                xbuf_last[i % 2] = [hh]
                if b == 0 and stile == 0 and i == 0:
                    if "htm" in dbg_aps:
                        em.dma(dbg_aps["htm"], htm[:], after=[hh])
                    if "sts" in dbg_aps:
                        for si, stt_ in enumerate((st1, st2, st3, st4)):
                            em.dma(dbg_aps["sts"][:, si * 8:(si + 1) * 8],
                                   stt_[:], after=[hh])
                    if "htm" in dbg_aps or "sts" in dbg_aps:
                        em.barrier()
                for dc in range(8):
                    bk = pTb
                    tr = em.t(lambda e, bk=bk, dc=dc:
                        e.transpose(bk.ap[:, 0:128],
                                    htm[:, dc * 128:(dc + 1) * 128], idb[:]),
                        after=[hh, ld_idb] + bk.wtoks())
                    ev = bk.read(em.s(lambda e, bk=bk, dc=dc, i=i:
                        e.activation(hfm[:, dc * 512 + i * 128:
                                          dc * 512 + (i + 1) * 128],
                                     bk.ap[:, 0:128], AF.Copy), after=[tr]))
                    hfm_evs.append(ev)
                    htm_last.append(tr)
            if b == 0 and stile == 0 and "hfm" in dbg_aps:
                em.dma(dbg_aps["hfm"], hfm[:], after=hfm_evs)
                em.barrier()
            for cc in range(4):
                bk = pA[cc % 2]
                mm = None
                for dc in range(8):
                    mm = em.t(lambda e, bk=bk, cc=cc, dc=dc:
                        e.matmul(bk.ap[:, 0:512],
                                 wdkvT_s[:, dc * CD + cc * 128:
                                         dc * CD + cc * 128 + 128],
                                 hfm[:, dc * 512:(dc + 1) * 512],
                                 start=(dc == 0), stop=(dc == 7)),
                        after=([fold_tok] + hfm_evs + bk.wtoks())
                        if dc == 0 else ())
                ev = bk.read(em.s(lambda e, bk=bk, cc=cc, stile=stile:
                    e.activation(ckv[:, cc * S + stile * 512:
                                      cc * S + (stile + 1) * 512],
                                 bk.ap[:, 0:512], AF.Identity,
                                 bias=bqkv[:, 4 + cc:5 + cc], scale=1.0),
                    after=[mm, ev_bias]))
                ckv_evs.append(ev)

        for half in range(2):
            vz = em.v(lambda e: e.memset(
                vsb[:].rearrange("p (k h c) -> p k h c", k=NKT, h=8)
                [:, :, :, 64:65], 1.0), after=ckv_evs)
            v_evs = [vz]
            for kt in range(NKT):
                bk = pA[kt % 2]
                mm = None
                for cc in range(4):
                    mm = em.t(lambda e, bk=bk, kt=kt, cc=cc, half=half:
                        e.matmul(bk.ap[:, 0:512],
                                 ckv[:, cc * S + kt * 128:cc * S + (kt + 1) * 128],
                                 wuvT_s[:, cc * D + half * 512:
                                        cc * D + half * 512 + 512],
                                 start=(cc == 0), stop=(cc == 3)),
                        after=(ckv_evs + [wl["uv"], vz] + bk.wtoks())
                        if cc == 0 else ())
                v_evs.append(bk.read(em.s(lambda e, bk=bk, kt=kt:
                    e.activation(vsb[:].rearrange("p (k h c) -> p k h c",
                                                  k=NKT, h=8)[:, kt, :, 0:64],
                                 bk.ap[:, 0:512], AF.Copy), after=[mm])))

            for pp4 in range(4):
                hp = half * 4 + pp4
                k_toks = []
                for ktc in range(8):
                    bk = pA[ktc % 2]
                    mm = None
                    for cc in range(4):
                        mm = em.t(lambda e, bk=bk, ktc=ktc, cc=cc, hp=hp:
                            e.matmul(bk.ap[:, 0:512],
                                     wukT_s[:, cc * D + hp * 128:
                                            cc * D + hp * 128 + 128],
                                     ckv[:, cc * S + ktc * 512:
                                         cc * S + (ktc + 1) * 512],
                                     start=(cc == 0), stop=(cc == 3)),
                            after=(ckv_evs + [wl["uk"]] + bk.wtoks() + v_evs)
                            if cc == 0 else ())
                    evk = bk.read(em.s(lambda e, bk=bk:
                        e.activation(kraw[:, 0:512], bk.ap[:, 0:512], AF.Copy),
                        after=[mm] + k_toks[-1:]))
                    m1 = em.v(lambda e, ktc=ktc:
                        e.tensor_tensor(ktp[:], kraw[:],
                                        kS_s[:, ktc * 512:(ktc + 1) * 512],
                                        ALU.mult), after=[evk, wl["kS"]])
                    m2 = em.v(lambda e: e.stream_shuffle(ksh[:], ktp[:], SHUF),
                              after=[m1])
                    m3 = em.v(lambda e, ktc=ktc:
                        e.tensor_tensor(ktp[:], kraw[:],
                                        kC_s[:, ktc * 512:(ktc + 1) * 512],
                                        ALU.mult), after=[m2, wl["kC"]])
                    m4 = em.v(lambda e, ktc=ktc:
                        e.tensor_add(krot[:, ktc * 512:(ktc + 1) * 512],
                                     ktp[:], ksh[:]), after=[m3])
                    k_toks.append(m4)
                q_toks = []
                for (q0, qw) in QCH:
                    mm = None
                    for cc in range(4):
                        mm = em.t(lambda e, q0=q0, qw=qw, cc=cc, hp=hp:
                            e.matmul(pB[0].ap[:, 0:qw],
                                     wuqT_s[:, cc * D + hp * 128:
                                            cc * D + hp * 128 + 128],
                                     cq[:, cc * WIN + q0:cc * WIN + q0 + qw],
                                     start=(cc == 0), stop=(cc == 3)),
                            after=(cq_evs + [wl["uq"]] + pB[0].wtoks() +
                                   ([last_exp[0]] if last_exp[0] else []))
                            if cc == 0 else ())
                    q_toks.append(pB[0].read(em.s(
                        lambda e, q0=q0, qw=qw:
                        e.activation(qraw[:, q0:q0 + qw], pB[0].ap[:, 0:qw],
                                     AF.Copy), after=[mm])))
                m1 = em.v(lambda e: e.tensor_tensor(qtp[:], qraw[:], qS_s[:],
                                                    ALU.mult),
                          after=q_toks + [wl["qS"]])
                m2 = em.v(lambda e: e.stream_shuffle(qsh[:], qtp[:], SHUF),
                          after=[m1])
                m3 = em.v(lambda e: e.tensor_tensor(qtp[:], qraw[:], qC_s[:],
                                                    ALU.mult),
                          after=[m2, wl["qC"]])
                qr = em.v(lambda e: e.tensor_add(qrot[:], qtp[:], qsh[:]),
                          after=[m3])

                if b == 0 and hp == 0:
                    did = False
                    for nm, buf, tok in (("krot", krot, k_toks[-1]),
                                         ("qrot", qrot, qr)):
                        if nm in dbg_aps:
                            em.dma(dbg_aps[nm], buf[:], after=[tok])
                            did = True
                    if did:
                        em.barrier()
                kq_last = None
                mxacc = None
                for ktc in range(8):
                    sq = em.v(lambda e, ktc=ktc:
                        e.tensor_tensor(sqk[:], krot[:, ktc * 512:(ktc + 1) * 512],
                                        krot[:, ktc * 512:(ktc + 1) * 512],
                                        ALU.mult),
                        after=[k_toks[ktc]] + ([kq_last] if kq_last else []))
                    cs = em.t(lambda e: e.matmul(pR.ap[0:1, 0:512], onec_b[:],
                                                 sqk[:], start=True, stop=True),
                              after=[sq, onec_tok] + pR.wtoks())
                    kq_last = cs
                    rd = pR.read(em.v(lambda e:
                        e.reduce_max(bst[0:1, 6:7], pR.ap[0:1, 0:512],
                                     axis=AX.X),
                        after=[cs] + ([mxacc] if mxacc else [])))
                    mxacc = em.v(lambda e, ktc=ktc:
                        (e.tensor_copy(bst[0:1, 0:1], bst[0:1, 6:7]) if ktc == 0
                         else e.tensor_max(bst[0:1, 0:1], bst[0:1, 0:1],
                                           bst[0:1, 6:7])), after=[rd])
                sq = em.v(lambda e: e.tensor_tensor(sqq[:], qrot[:], qrot[:],
                                                    ALU.mult),
                          after=[qr, kq_last])
                for ci, (q0, qw) in enumerate(QCH):
                    cs = em.t(lambda e, q0=q0, qw=qw:
                        e.matmul(pR.ap[0:1, 0:qw], onec_b[:],
                                 sqq[:, q0:q0 + qw], start=True, stop=True),
                        after=[sq, onec_tok] + pR.wtoks())
                    rd = pR.read(em.v(lambda e:
                        e.reduce_max(bst[0:1, 6:7], pR.ap[0:1, 0:qw],
                                     axis=AX.X), after=[cs, mxacc]))
                    mxacc = em.v(lambda e, ci=ci:
                        (e.tensor_copy(bst[0:1, 1:2], bst[0:1, 6:7]) if ci == 0
                         else e.tensor_max(bst[0:1, 1:2], bst[0:1, 1:2],
                                           bst[0:1, 6:7])), after=[rd])
                b1 = em.v(lambda e: e.tensor_tensor(bst[0:1, 2:3], bst[0:1, 0:1],
                                                    bst[0:1, 1:2], ALU.mult),
                          after=[mxacc])
                b2 = em.s(lambda e: e.activation(bst[0:1, 3:4], bst[0:1, 2:3],
                                                 AF.Sqrt, bias=0.0, scale=1.0),
                          after=[b1])
                b3 = em.v(lambda e: e.tensor_scalar(bst[0:1, 3:4], bst[0:1, 3:4],
                                                    -1.0, None, ALU.mult),
                          after=[b2])
                bcm = em.t(lambda e: e.matmul(pR.ap[:, 0:1], ones_f[:],
                                              bst[0:1, 3:4], start=True,
                                              stop=True),
                           after=[b3] + pR.wtoks())
                bh = b * 8 + hp
                nbv = pR.read(em.v(lambda e, bh=bh:
                    e.tensor_copy(negBc[:, bh:bh + 1], pR.ap[:, 0:1]),
                    after=[bcm]))

                av_by_slot = [None, None, None]
                for hl in range(2):
                    h = 2 * hp + hl
                    pOb = pO[hl]
                    for ci, (q0, qw) in enumerate(QCH):
                        pend = None
                        for kt in range(NKT):
                            slot = kt % 3
                            bk = pB[kt % 2]
                            war = bk.wtoks()
                            if kt == 0:
                                war += pOb.wtoks()
                            mm1 = em.t(lambda e, bk=bk, hl=hl, kt=kt, q0=q0,
                                              qw=qw:
                                e.matmul(bk.ap[:, 0:qw],
                                         krot[hl * 64:hl * 64 + 64,
                                              kt * 128:(kt + 1) * 128],
                                         qrot[hl * 64:hl * 64 + 64, q0:q0 + qw],
                                         start=True, stop=True),
                                after=[qr, k_toks[-1]] + war)
                            ex = bk.read(em.s(lambda e, slot=slot, bk=bk, qw=qw,
                                                     bh=bh:
                                e.activation(expS[slot][:, 0:qw],
                                             bk.ap[:, 0:qw], AF.Exp,
                                             bias=negBc[:, bh:bh + 1],
                                             scale=1.0),
                                after=[mm1, nbv] +
                                      ([av_by_slot[slot]] if av_by_slot[slot]
                                       else [])))
                            last_exp[kt % 2] = ex
                            if pend is not None:
                                pkt, pslot, pex = pend
                                av_by_slot[pslot] = em.t(
                                    lambda e, pOb=pOb, pkt=pkt, pslot=pslot,
                                           qw=qw, h=h:
                                    e.matmul(pOb.ap[0:65, 0:qw],
                                             vsb[:, pkt * 520 + (h % 8) * 65:
                                                 pkt * 520 + (h % 8) * 65 + 65],
                                             expS[pslot][:, 0:qw],
                                             start=(pkt == 0),
                                             stop=(pkt == NKT - 1)),
                                    after=[pex])
                            pend = (kt, slot, ex)
                        pkt, pslot, pex = pend
                        av_by_slot[pslot] = em.t(
                            lambda e, pOb=pOb, pkt=pkt, pslot=pslot, qw=qw, h=h:
                            e.matmul(pOb.ap[0:65, 0:qw],
                                     vsb[:, pkt * 520 + (h % 8) * 65:
                                         pkt * 520 + (h % 8) * 65 + 65],
                                     expS[pslot][:, 0:qw],
                                     start=(pkt == 0), stop=(pkt == NKT - 1)),
                            after=[pex])
                        rc = pOb.read(em.v(lambda e, pOb=pOb, qw=qw:
                            e.reciprocal(recip[:, 0:qw], pOb.ap[64:65, 0:qw]),
                            after=[av_by_slot[(NKT - 1) % 3]]))
                        bc = em.t(lambda e, qw=qw:
                            e.matmul(pR.ap[0:64, 0:qw], ones_f[:, 0:64],
                                     recip[:, 0:qw], start=True, stop=True),
                            after=[rc] + pR.wtoks())
                        bce2 = pR.read(em.s(lambda e, qw=qw:
                            e.activation(rbc[:, 0:qw], pR.ap[0:64, 0:qw],
                                         AF.Copy), after=[bc]))
                        wr = em.v(lambda e, pOb=pOb, h=h, q0=q0, qw=qw:
                            e.tensor_tensor(
                                oacc[64 * (h % 2):64 * (h % 2) + 64,
                                     (h // 2) * WIN + q0:(h // 2) * WIN + q0 + qw],
                                pOb.ap[0:64, 0:qw], rbc[:, 0:qw], ALU.mult),
                            after=[bce2])
                        pOb.read(wr)

        if b == 0:
            did = False
            for nm, buf in (("ckv", ckv), ("cq", cq), ("oacc", oacc),
                            ("vsb", vsb)):
                if nm in dbg_aps:
                    em.dma(dbg_aps[nm], buf[:],
                           after=[("v", em.cnt["v"]), ("s", em.cnt["s"]),
                                  ("t", em.cnt["t"])])
                    did = True
            if did:
                em.barrier()
        woc = vsb[:, 0:8 * D]
        oacc_tok = [("v", em.cnt["v"]), ("s", em.cnt["s"])]
        ld_wo = em.dma(woc.rearrange("p (a c) -> p a c", a=8),
                       woutTa.rearrange("(a p) c -> p a c", p=128),
                       eng="gpsimd", after=oacc_tok + [("t", em.cnt["t"])])
        for j in range(5):
            rows = QT[j]
            ld_r1 = em.dma(r1t[0:rows, :], r1buf[b, j * 128:j * 128 + rows, :],
                           after=r1_toks)
            adds = []
            for dh in range(2):
                bk = pA[dh]
                mm = None
                for oc in range(8):
                    mm = em.t(lambda e, bk=bk, j=j, dh=dh, oc=oc, rows=rows:
                        e.matmul(bk.ap[0:rows, 0:512],
                                 oacc[:, oc * WIN + j * 128:
                                      oc * WIN + j * 128 + rows],
                                 woc[:, oc * D + dh * 512:oc * D + dh * 512 + 512],
                                 start=(oc == 0), stop=(oc == 7)),
                        after=(oacc_tok + [ld_wo] + bk.wtoks())
                        if oc == 0 else ())
                adds.append(bk.read(em.v(lambda e, bk=bk, dh=dh, rows=rows:
                    e.tensor_add(x1t[0:rows, dh * 512:(dh + 1) * 512],
                                 bk.ap[0:rows, 0:512],
                                 r1t[0:rows, dh * 512:(dh + 1) * 512]),
                    after=[mm, ld_r1])))
            em.dma(x1buf[b, j * 128:j * 128 + rows, :], x1t[0:rows, :],
                   after=adds)

    em.barrier()
    em.flush()
    g1.close()

    g2 = ExitStack()
    sb2 = lambda n, sh, dty: g2.enter_context(nc.sbuf_tensor(n, sh, dty))
    pg = [Bank(g2.enter_context(nc.psum_tensor(f"pg{i}", [128, 512], F32)))
          for i in range(7)]
    pF = [pg[0], pg[1]]
    pW = [pg[2], pg[3]]
    pSV = pg[4]
    pSb = pg[5]
    pM = pg[6]
    pTb2 = Bank(g2.enter_context(nc.psum_tensor("pTb2", [128, 1024], BF16)))

    xqT = sb2("xqT", [128, 8 * HTW], BF16)
    bigb = sb2("bigb", [128, 32 * 1024], BF16)
    xq2c = sb2("xq2c", [128, 32 * 512], BF16)
    qkv_c = [sb2(f"qkv_c{i}", [128, TP], BF16) for i in range(3)]
    w1sl = [sb2(f"w1sl{i}", [128, 1024], BF16) for i in range(2)]
    cvt = sb2("cvt", [128, TP + 8], F32)
    cvu = sb2("cvu", [128, TP + 8], F32)

    cw0s = sb2("cw0s", [128, 32], F32)
    cw1s = sb2("cw1s", [128, 32], F32)
    cw2s = sb2("cw2s", [128, 32], F32)
    cbs = sb2("cbs", [128, 32], F32)
    amaxp = sb2("amaxp", [128, 2 * 68], F32)
    akr = sb2("akr", [68, 32], F32)
    s2kr = sb2("s2kr", [68, 32], F32)
    ivkr = sb2("ivkr", [68, 32], F32)
    s2fm = sb2("s2fm", [1, 512], F32)
    iv_pu = sb2("iv_pu", [128, NT], F32)
    xq2i = sb2("xq2i", [128, 512], I32)
    h0f = sb2("h0f", [128, D], F32)
    xint = sb2("xint", [128, D], I32)
    xqd = sb2("xqd", [128, D], BF16)
    qs_s = sb2("qs_s", [128, NT], F32)
    qs_iv = sb2("qs_iv", [128, NT], F32)
    xt2 = sb2("xt2", [128, D], F32)
    xsqb = sb2("xsqb", [128, D], F32)
    htm2 = sb2("htm2", [128, D], BF16)
    bmask_s = sb2("bmask_s", [128, 128], F32)
    binb_s = sb2("binb_s", [128, 24], F32)
    boutb_s = sb2("boutb_s", [128, 8], F32)
    boutf_s = sb2("boutf_s", [1, D], BF16)
    winsl = [sb2(f"winsl{i}", [128, 8 * 128], BF16) for i in range(2)]
    vtm = sb2("vtm", [128, 128], BF16)
    pnt = sb2("pnt", [128, 128], BF16)
    pnr = sb2("pnr", [128, 128], BF16)
    smc = sb2("smc", [128, 8], F32)

    wld = cvt
    wq_i = xint
    wq_f = cvu
    wq_b = xqd
    x1t2_alias_check = True
    ld_bm = em.dma(bmask_s[:], bmask)
    ld_bb = em.dma(binb_s[:], binb)
    ld_bo = em.dma(boutb_s[:], boutb)
    ld_bf = em.dma(boutf_s[:], boutf, eng="gpsimd")
    cv_toks = [
        em.dma(cw0s[:], cw0),
        em.dma(cw1s[:], cw1),
        em.dma(cw2s[:], cw2),
        em.dma(cbs[:], cbv),
    ]

    def quant_weights(srcT, dstT, ntile, width, c0):
        toks = []
        prev = None
        ncc = width // 1024
        for i in range(ntile):
            for cc in range(ncc):
                ld = em.dma(wld[:, 0:1024], srcT[i * 128:(i + 1) * 128,
                                         cc * 1024:(cc + 1) * 1024],
                            after=[prev] if prev else [])
                rs = em.v(lambda e: e.tensor_reduce(
                    st5[:, 39:40], wld[:, 0:1024], axis=AX.X, op=ALU.add,
                    apply_absolute_value=True), after=[ld])
                rs = em.v(lambda e, i=i, cc=cc:
                    (e.tensor_copy(st5[:, i:i + 1], st5[:, 39:40]) if cc == 0
                     else e.tensor_add(st5[:, i:i + 1], st5[:, i:i + 1],
                                       st5[:, 39:40])), after=[rs])
                toks.append(rs)
                prev = rs
        smm = em.t(lambda e, ntile=ntile:
            e.matmul(pM.ap[0:1, 0:ntile], ones_p[:], st5[:, 0:ntile],
                     start=True, stop=True), after=toks + pM.wtoks())
        tot = pM.read(em.v(lambda e, ntile=ntile:
            e.reduce_sum(st5[0:1, 36:37], pM.ap[0:1, 0:ntile], axis=AX.X),
            after=[smm]))
        mn = em.v(lambda e, ntile=ntile, width=width:
            e.tensor_scalar(st5[0:1, 37:38], st5[0:1, 36:37],
                            1.0 / (ntile * 128 * width), 1e-5,
                            ALU.mult, ALU.max), after=[tot])
        rec = em.v(lambda e: e.reciprocal(st5[0:1, 38:39], st5[0:1, 37:38]),
                   after=[mn])
        bc = em.t(lambda e: e.matmul(pM.ap[:, 0:2], ones_f[:], st5[0:1, 37:39],
                                     start=True, stop=True),
                  after=[rec] + pM.wtoks())
        bce = pM.read(em.v(lambda e, c0=c0:
            e.tensor_copy(st4[:, c0:c0 + 2], pM.ap[:, 0:2]), after=[bc]))
        prev = None
        for i in range(ntile):
            for cc in range(ncc):
                ld = em.dma(wld[:, 0:1024], srcT[i * 128:(i + 1) * 128,
                                         cc * 1024:(cc + 1) * 1024],
                            after=[toks[-1]] if prev is None else [prev])
                q1 = em.v(lambda e, c0=c0:
                    e.tensor_scalar(wq_i[:], wld[:, 0:1024],
                                    st4[:, c0 + 1:c0 + 2],
                                    None, ALU.mult), after=[ld, bce])
                q2 = em.v(lambda e: e.tensor_scalar(wq_f[:, 0:1024], wq_i[:],
                                                    1.0, -1.0, ALU.min,
                                                    ALU.max), after=[q1])
                q3 = em.v(lambda e, c0=c0:
                    e.tensor_scalar(wq_b[:], wq_f[:, 0:1024],
                                    st4[:, c0:c0 + 1],
                                    None, ALU.mult), after=[q2])
                prev = em.dma(dstT[i * 128:(i + 1) * 128,
                                   cc * 1024:(cc + 1) * 1024], wq_b[:],
                              after=[q3])
        return prev

    wq1_tok = quant_weights(w1T, w1qc, 8, DFF, 2)
    wq2_tok = quant_weights(w2T, w2qc, 32, D, 4)

    gz = []
    for dc in range(8):
        gz.append(em.v(lambda e, dc=dc:
            e.memset(xqT[:, dc * HTW:dc * HTW + 4], 0.0)))
        gz.append(em.v(lambda e, dc=dc:
            e.memset(xqT[:, dc * HTW + 4 + TP:dc * HTW + 8 + TP], 0.0)))
    em.barrier()
    em.flush()

    def sb3(sbap):
        return sbap

    def plain_ln_to(dst_fn, srcrow, quant):
        prev = ones_tok
        out_toks = []
        xt2_last = []
        for j in range(NT):
            ap, nval = srcrow(j * 128, 128)
            if nval < 128:
                mz = em.v(lambda e: e.memset(xt2[:], 0.0), after=xt2_last)
                ld = [em.dma(xt2[0:nval, :], ap, after=xt2_last + [mz])]
            else:
                ld = [em.dma(xt2[:], ap, after=xt2_last)]
            xt2_last = []
            tsum = em.v(lambda e: e.reduce_sum(st1[:, 0:1], xt2[:], axis=AX.X),
                        after=ld + [prev])
            tsq = em.s(lambda e: e.activation(xsqb[:], xt2[:], AF.Square,
                                              accum_out=st1[:, 1:2]),
                       after=ld + [prev])
            t1 = em.v(lambda e: e.tensor_scalar(st2[:, 0:2], st1[:, 0:2],
                                                1.0 / D, None, ALU.mult),
                      after=[tsum, tsq])
            t2 = em.v(lambda e: e.tensor_tensor(st2[:, 2:3], st2[:, 0:1],
                                                st2[:, 0:1], ALU.mult),
                      after=[t1])
            t3 = em.v(lambda e: e.tensor_tensor(st2[:, 1:2], st2[:, 1:2],
                                                st2[:, 2:3], ALU.subtract),
                      after=[t2])
            q1 = em.s(lambda e: e.activation(st3[:, 0:1], st2[:, 1:2], AF.Sqrt,
                                             bias=np.float32(EPS), scale=1.0),
                      after=[t3])
            rsv = em.v(lambda e: e.reciprocal(st3[:, 0:1], st3[:, 0:1]),
                       after=[q1])
            if not quant:
                nb = em.v(lambda e: e.tensor_tensor(st3[:, 1:2], st2[:, 0:1],
                                                    st3[:, 0:1], ALU.mult),
                          after=[rsv])
                nb2 = em.v(lambda e: e.tensor_scalar(st3[:, 1:2], st3[:, 1:2],
                                                     -1.0, None, ALU.mult),
                          after=[nb])
                hh = em.s(lambda e: e.activation(htm2[:], xt2[:], AF.Identity,
                                                 bias=st3[:, 1:2],
                                                 scale=st3[:, 0:1]),
                          after=[nb2])
                prev = hh
                xt2_last = [hh]
                out_toks.append(dst_fn(j, hh))
                continue
            h0 = em.v(lambda e: e.tensor_scalar(h0f[:], xt2[:], st2[:, 0:1],
                                                st3[:, 0:1], ALU.subtract,
                                                ALU.mult), after=[rsv])
            xt2_last = [h0]
            am = em.v(lambda e, j=j: e.reduce_max(qs_s[:, j:j + 1], h0f[:],
                                                  axis=AX.X,
                                                  apply_absolute_value=True),
                      after=[h0])
            c1 = em.v(lambda e, j=j: e.tensor_scalar(qs_s[:, j:j + 1],
                                                     qs_s[:, j:j + 1], 1e-5,
                                                     None, ALU.max), after=[am])
            c2 = em.v(lambda e, j=j: e.tensor_scalar(qs_iv[:, j:j + 1],
                                                     qs_s[:, j:j + 1],
                                                     1.0 / 127.0,
                                                     maskv_s[:, j:j + 1],
                                                     ALU.mult, ALU.mult),
                      after=[c1, ld_mv])
            c3 = em.v(lambda e, j=j: e.reciprocal(qs_s[:, j:j + 1],
                                                  qs_s[:, j:j + 1]), after=[c2])
            c4 = em.v(lambda e, j=j: e.tensor_scalar(qs_s[:, j:j + 1],
                                                     qs_s[:, j:j + 1], 127.0,
                                                     None, ALU.mult), after=[c3])
            qi = em.v(lambda e, j=j: e.tensor_scalar(xint[:], h0f[:],
                                                     qs_s[:, j:j + 1], None,
                                                     ALU.mult), after=[c4])
            qd = em.s(lambda e, j=j: e.activation(xqd[:], xint[:], AF.Identity,
                                                  bias=0.0,
                                                  scale=qs_iv[:, j:j + 1]),
                      after=[qi])
            prev = qd
            out_toks.append(dst_fn(j, qd))
        return out_toks

    def transpose_to_xqT(j, src_tile, src_tok):
        evs = []
        for dc in range(8):
            bk = pTb2
            tr = em.t(lambda e, bk=bk, dc=dc:
                e.transpose(bk.ap[:, 0:128],
                            src_tile[:, dc * 128:(dc + 1) * 128], idb[:]),
                after=[src_tok, ld_idb] + bk.wtoks())
            evs.append(bk.read(em.s(lambda e, bk=bk, dc=dc, j=j:
                e.activation(xqT[:, dc * HTW + 4 + j * 128:
                                  dc * HTW + 4 + (j + 1) * 128],
                             bk.ap[:, 0:128], AF.Copy), after=[tr])))
        return evs[-1]

    def ffn(srcrow, dst, dbgname=None):
        f1 = plain_ln_to(lambda j, tok: transpose_to_xqT(j, xqd, tok),
                         srcrow, quant=True)
        em.barrier()
        amz = em.v(lambda e: e.memset(amaxp[:], 0.0))
        am_last = amz
        hs_toks = []
        w1_mm_last = {0: None, 1: None}
        for ct in range(32):
            ldw = em.dma(w1sl[ct % 2][:].rearrange("p (a c) -> p a c", a=8),
                         w1qc.rearrange("(a p) f -> p a f", p=128)
                         [:, :, ct * 128:(ct + 1) * 128],
                         after=[wq1_tok, w1_mm_last[ct % 2]])
            nch = [(0, 512), (512, 512), (1024, 512), (1536, 512), (2048, 136)]
            ev_toks = []
            for (n0, nw) in nch:
                bk = pF[(n0 // 512) % 2]
                mm = None
                for dc in range(8):
                    mm = em.t(lambda e, bk=bk, ct=ct, dc=dc, n0=n0, nw=nw:
                        e.matmul(bk.ap[:, 0:nw],
                                 w1sl[ct % 2][:, dc * 128:(dc + 1) * 128],
                                 xqT[:, dc * HTW + n0:dc * HTW + n0 + nw],
                                 start=(dc == 0), stop=(dc == 7)),
                        after=([ldw] + bk.wtoks()) if dc == 0 else ())
                ev_toks.append(bk.read(em.v(lambda e, bk=bk, n0=n0, nw=nw:
                    e.tensor_copy(cvt[:, n0:n0 + nw], bk.ap[:, 0:nw]),
                    after=[mm] + hs_toks[-1:])))
                w1_mm_last[ct % 2] = mm
            c1 = em.v(lambda e, ct=ct:
                e.tensor_scalar(cvu[:, 0:TP], cvt[:, 4:4 + TP],
                                cw1s[:, ct:ct + 1], cbs[:, ct:ct + 1],
                                ALU.mult, ALU.add),
                after=ev_toks + cv_toks)
            c2 = em.v(lambda e, ct=ct:
                e.scalar_tensor_tensor(cvu[:, 0:TP], cvt[:, 0:TP],
                                       cw0s[:, ct:ct + 1], cvu[:, 0:TP],
                                       ALU.mult, ALU.add), after=[c1])
            c3 = em.v(lambda e, ct=ct:
                e.scalar_tensor_tensor(cvu[:, 0:TP], cvt[:, 8:8 + TP],
                                       cw2s[:, ct:ct + 1], cvu[:, 0:TP],
                                       ALU.mult, ALU.add), after=[c2])
            sl = em.s(lambda e: e.activation(cvt[:, 0:TP], cvu[:, 0:TP],
                                             AF.Silu),
                      after=[c3] + hs_toks[-1:])
            amx = em.v(lambda e: e.reduce_max(
                amaxp[:, 0:68],
                cvt[:, 0:TP].rearrange("p (k u) -> p k u", u=32),
                axis=AX.X, apply_absolute_value=True), after=[sl, am_last])
            am_last = em.v(lambda e: e.tensor_max(amaxp[:, 68:136],
                                                  amaxp[:, 68:136],
                                                  amaxp[:, 0:68]), after=[amx])
            hs_toks.append(em.dma(hsbuf[ct * 128:(ct + 1) * 128, :],
                                  cvt[:, 0:TP], after=[sl, am_last]))
        trA = em.t(lambda e: e.transpose(pM.ap[0:68, 0:128], amaxp[:, 68:136],
                                         id32[:]),
                   after=[am_last, ld_id] + pM.wtoks())
        evA = pM.read(em.v(lambda e: e.tensor_copy(akr2[:, 0:128],
                                               pM.ap[0:68, 0:128]),
            after=[trA]))
        red = em.v(lambda e: e.tensor_reduce(
            akr[:, 0:32], akr2[:, 0:128].rearrange("k (g r) -> k r g", g=4),
            axis=AX.X, op=ALU.max), after=[evA])
        sc1 = em.v(lambda e: e.tensor_scalar(akr[:, 0:32], akr[:, 0:32],
                                             1e-5, None, ALU.max), after=[red])
        sc2 = em.v(lambda e: e.tensor_scalar(ivkr[:, 0:32], akr[:, 0:32],
                                             1.0 / 127.0, None, ALU.mult),
                   after=[sc1])
        sc3 = em.v(lambda e: e.reciprocal(s2kr[:, 0:32], akr[:, 0:32]),
                   after=[sc2])
        sc4 = em.v(lambda e: e.tensor_scalar(s2kr[:, 0:32], s2kr[:, 0:32],
                                             127.0, None, ALU.mult), after=[sc3])
        d1 = em.dma(sscr[0:TP].rearrange("(k r) -> k r", k=68), s2kr[:, 0:32],
                    after=[sc4])
        d3 = em.dma(sscr[TP:2 * TP].rearrange("(k r) -> k r", k=68),
                    ivkr[:, 0:32], after=[sc2])
        d4 = em.dma(iv_pu[:, 0:NT], sscr[TP:2 * TP].rearrange("(u p) -> p u",
                                                              p=128),
                    after=[d3], allow_slow_non_contiguous=True)
        ldw2 = em.dma(bigb[:].rearrange("p (a c) -> p a c", a=32),
                      w2qc.rearrange("(a p) c -> p a c", p=128),
                      eng="gpsimd",
                      after=[wq2_tok, ("v", em.cnt["v"]), ("s", em.cnt["s"]),
                             ("t", em.cnt["t"])])
        prev_w2 = None
        prev_epis = []
        for tc, (t0, tw) in enumerate(
                [(0, 512), (512, 512), (1024, 512), (1536, 512), (2048, 128)]):
            d2 = em.dma(s2fm[:, 0:tw], sscr[t0:t0 + tw][None, :],
                        after=[d1, ("t", em.cnt["t"])])
            sbc = em.t(lambda e, tw=tw:
                e.matmul(pSb.ap[:, 0:tw], ones_f[:], s2fm[:, 0:tw],
                         start=True, stop=True), after=[d2] + pSb.wtoks())
            q_toks = []
            for ct in range(32):
                ldh = em.dma(cvu[:, 0:tw],
                             hsbuf[ct * 128:(ct + 1) * 128, t0:t0 + tw],
                             after=[hs_toks[ct]] +
                                   ([q_toks[-1]] if q_toks else []))
                qa = pSb.read(em.v(lambda e, ct=ct, tw=tw:
                    e.tensor_tensor(xq2i[:, 0:tw], cvu[:, 0:tw],
                                    pSb.ap[:, 0:tw], ALU.mult),
                    after=[ldh, sbc, prev_w2] +
                          ([q_toks[-1]] if q_toks else [])))
                qb = em.s(lambda e, ct=ct, tw=tw:
                    e.activation(xq2c[:, ct * 512:ct * 512 + tw],
                                 xq2i[:, 0:tw], AF.Copy), after=[qa])
                q_toks.append(qb)
            nu = (tw + 127) // 128
            for u in range(nu):
                uw = min(128, tw - u * 128)
                tglob = t0 + u * 128
                jt = tglob // 128
                epis = []
                for dh in range(2):
                    bk = pW[dh]
                    mm = None
                    for ct in range(32):
                        mm = em.t(lambda e, bk=bk, ct=ct, u=u, dh=dh, uw=uw:
                            e.matmul(bk.ap[0:uw, 0:512],
                                     xq2c[:, ct * 512 + u * 128:
                                          ct * 512 + u * 128 + uw],
                                     bigb[:, ct * 1024 + dh * 512:
                                          ct * 1024 + (dh + 1) * 512],
                                     start=(ct == 0), stop=(ct == 31)),
                            after=([ldw2, q_toks[ct]] + bk.wtoks())
                            if ct == 0 else [q_toks[ct]])
                    w2_mm_last = mm
                    if dh == 0:
                        rap, rnval = srcrow(tglob, uw)
                        if rnval < uw:
                            mz = em.v(lambda e: e.memset(xt2[:], 0.0),
                                      after=prev_epis)
                            ldr = [em.dma(xt2[0:rnval, :], rap,
                                          after=prev_epis + [mz])]
                        else:
                            ldr = [em.dma(xt2[0:rnval, :], rap,
                                          after=prev_epis)]
                    epi = bk.read(em.v(lambda e, bk=bk, dh=dh, uw=uw, jt=jt:
                        e.scalar_tensor_tensor(
                            x1t2[0:uw, dh * 512:(dh + 1) * 512],
                            bk.ap[0:uw, 0:512], iv_pu[:, jt:jt + 1],
                            xt2[0:uw, dh * 512:(dh + 1) * 512],
                            ALU.mult, ALU.add),
                        after=[mm, d4] + ldr))
                    epis.append(epi)
                prev_epis = list(epis)
                prev_w2 = w2_mm_last
                if dst[0] == "dram":
                    em.dma(dst[1][tglob:tglob + uw, :], x1t2[0:uw, :],
                           after=epis)
                else:
                    lo, hi = max(8, tglob), min(2056, tglob + uw)
                    if lo < hi:
                        yv = yout.rearrange("b s d -> s b d")
                        em.dma(yv[(lo - 8) // 4:(hi - 8) // 4],
                               sb3(x1t2[lo - tglob:hi - tglob, :]),
                               after=epis)
        em.barrier()
        if dbgname and dbgname in dbg_aps and dst[0] == "dram":
            em.dma(dbg_aps[dbgname], dst[1], after=[("dsy", em.cnt["dsy"])])
            em.barrier()

    akr2 = sb2("akr2", [68, 128], F32)
    x1t2 = h0f

    x1v3 = x1buf.rearrange("b s d -> s b d")
    if "x1" in dbg_aps:
        em.dma(dbg_aps["x1"], x1buf, after=[("dsy", em.cnt["dsy"])])
        em.barrier()
    if "ffn1" in phases:
        ffn(lambda t0, rows: (x1v3[t0 // 4:min(t0 + rows, 4 * WIN) // 4],
                              max(0, min(t0 + rows, 4 * WIN) - t0)),
            ("dram", x2buf), dbgname="x2")
    em.flush()

    if "attnb" in phases:
        lnn = plain_ln_to(lambda j, tok: transpose_to_xqT(j, htm2, tok),
                          lambda t0, rows: (x2buf[t0:t0 + rows, :], rows),
                          quant=False)
        em.barrier()
        WOB0 = 8 * TP
        ld_wob = em.dma(bigb[:, WOB0:WOB0 + 8 * D].rearrange("p (a c) -> p a c", a=8),
                        woutTb.rearrange("(a p) c -> p a c", p=128), eng="gpsimd",
                        after=[("s", em.cnt["s"]), ("t", em.cnt["t"])])
        winv = winT.rearrange("(a p) c -> p a c", p=128)
        for pp in range(8):
            ch_evs = []
            for which in range(3):
                roff = which * D + pp * 128
                ldwin = em.dma(winsl[which % 2][:].rearrange("p (a c) -> p a c", a=8),
                               winv[:, :, roff:roff + 128], eng="gpsimd",
                               after=[("t", em.cnt["t"])])
                for (n0, nw) in [(0, 512), (512, 512), (1024, 512), (1536, 512),
                                 (2048, 128)]:
                    bk = pW[which % 2]
                    mm = None
                    for dc in range(8):
                        mm = em.t(lambda e, bk=bk, which=which, dc=dc, n0=n0, nw=nw:
                            e.matmul(bk.ap[:, 0:nw],
                                     winsl[which % 2][:, dc * 128:(dc + 1) * 128],
                                     xqT[:, dc * HTW + 4 + n0:dc * HTW + 4 + n0 + nw],
                                     start=(dc == 0), stop=(dc == 7)),
                            after=([ldwin] + lnn[-1:] + bk.wtoks())
                            if dc == 0 else ())
                    if which == 0:
                        ev = bk.read(em.v(lambda e, which=which, pp=pp, n0=n0, nw=nw,
                                                 bk=bk:
                            e.tensor_scalar(qkv_c[which][:, n0:n0 + nw],
                                            bk.ap[:, 0:nw],
                                            binb_s[:, which * 8 + pp:
                                                   which * 8 + pp + 1],
                                            0.125, ALU.add, ALU.mult),
                            after=[mm, ld_bb]))
                    else:
                        ev = bk.read(em.v(lambda e, which=which, pp=pp, n0=n0, nw=nw,
                                                 bk=bk:
                            e.tensor_scalar(qkv_c[which][:, n0:n0 + nw],
                                            bk.ap[:, 0:nw],
                                            binb_s[:, which * 8 + pp:
                                                   which * 8 + pp + 1],
                                            None, ALU.add),
                            after=[mm, ld_bb]))
                    ch_evs.append(ev)
            if pp == 0:
                did = False
                for wn in range(3):
                    nm = "qkb%d" % wn
                    if nm in dbg_aps:
                        em.dma(dbg_aps[nm], qkv_c[wn][:], after=ch_evs)
                        did = True
                if "lnb" in dbg_aps:
                    em.dma(dbg_aps["lnb"], xqT[:, 4:4 + TP], after=ch_evs)
                    did = True
                if did:
                    em.barrier()
            for j in range(NT):
                vt = em.t(lambda e, j=j:
                    e.transpose(pTb2.ap[:, 0:128],
                                qkv_c[2][:, j * 128:(j + 1) * 128], idb[:]),
                    after=ch_evs + pTb2.wtoks())
                vte = pTb2.read(em.s(lambda e:
                    e.activation(vtm[:], pTb2.ap[:, 0:128], AF.Copy), after=[vt]))
                for hl in range(2):
                    sm = em.t(lambda e, j=j, hl=hl:
                        e.matmul(pSV.ap[:, 0:128],
                                 qkv_c[0][hl * 64:hl * 64 + 64, j * 128:(j + 1) * 128],
                                 qkv_c[1][hl * 64:hl * 64 + 64, j * 128:(j + 1) * 128],
                                 start=True, stop=True),
                        after=ch_evs + pSV.wtoks())
                    ms = pSV.read(em.v(lambda e:
                        e.tensor_add(cvt[:, 0:128], pSV.ap[:, 0:128], bmask_s[:]),
                        after=[sm, ld_bm]))
                    mx = em.v(lambda e: e.reduce_max(smc[:, 0:1], cvt[:, 0:128],
                                                     axis=AX.X, negate=True),
                              after=[ms])
                    ex = em.s(lambda e: e.activation(pnr[:], cvt[:, 0:128], AF.Exp,
                                                     bias=smc[:, 0:1], scale=1.0,
                                                     accum_out=smc[:, 1:2]),
                              after=[mx])
                    rc = em.v(lambda e: e.reciprocal(smc[:, 2:3], smc[:, 1:2]),
                              after=[ex])
                    pn = em.v(lambda e: e.tensor_scalar(pnr[:], pnr[:], smc[:, 2:3],
                                                        None, ALU.mult), after=[rc])
                    tp_ = em.t(lambda e: e.transpose(pTb2.ap[:, 0:128], pnr[:],
                                                     idb[:]),
                               after=[pn, vte] + pTb2.wtoks())
                    pte = pTb2.read(em.s(lambda e:
                        e.activation(pnt[:], pTb2.ap[:, 0:128], AF.Copy),
                        after=[tp_]))
                    av = em.t(lambda e, hl=hl:
                        e.matmul(pSV.ap[hl * 64:hl * 64 + 64, 0:128],
                                 vtm[:, hl * 64:hl * 64 + 64], pnt[:],
                                 start=True, stop=True),
                        after=[pte] + pSV.wtoks())
                    oe = pSV.read(em.s(lambda e, pp=pp, j=j, hl=hl:
                        e.activation(bigb[:, pp * TP + j * 128:pp * TP + (j + 1) * 128]
                                     [hl * 64:hl * 64 + 64, :],
                                     pSV.ap[hl * 64:hl * 64 + 64, 0:128],
                                     AF.Copy), after=[av]))
            if pp == 0 and "ob0" in dbg_aps:
                em.barrier()
                em.dma(dbg_aps["ob0"], bigb[:, 0:TP],
                       after=[("s", em.cnt["s"])])
                em.barrier()
        em.barrier()
        prev_eps = []
        for j in range(NT):
            ldr = em.dma(xt2[:], x2buf[j * 128:(j + 1) * 128, :], after=prev_eps)
            epis = []
            for dh in range(2):
                bk = pW[dh]
                mm = None
                for oc in range(8):
                    mm = em.t(lambda e, bk=bk, j=j, dh=dh, oc=oc:
                        e.matmul(bk.ap[:, 0:512],
                                 bigb[:, oc * TP + j * 128:oc * TP + (j + 1) * 128],
                                 bigb[:, WOB0 + oc * D + dh * 512:
                                      WOB0 + oc * D + dh * 512 + 512],
                                 start=(oc == 0), stop=False),
                        after=([ld_wob, ("s", em.cnt["s"])] + bk.wtoks())
                        if oc == 0 else ())
                mm = em.t(lambda e, bk=bk, dh=dh:
                    e.matmul(bk.ap[:, 0:512], ones_b[0:1, 0:128],
                             boutf_s[0:1, dh * 512:(dh + 1) * 512],
                             start=False, stop=True),
                    after=[ld_bf])
                ep = bk.read(em.v(lambda e, bk=bk, dh=dh:
                    e.tensor_add(x1t2[:, dh * 512:(dh + 1) * 512],
                                 bk.ap[:, 0:512],
                                 xt2[:, dh * 512:(dh + 1) * 512]),
                    after=[mm, ldr]))
                epis.append(ep)
            prev_eps = list(epis)
            em.dma(x3buf[j * 128:(j + 1) * 128, :], x1t2[:], after=epis)
        em.barrier()
        if "x3" in dbg_aps:
            em.dma(dbg_aps["x3"], x3buf, after=[("dsy", em.cnt["dsy"])])
            em.barrier()
        em.flush()

    if "ffn2" in phases:
        ffn(lambda t0, rows: (x3buf[t0:t0 + rows, :], rows), ("yout",))
    for eng in Em.ENGS:
        em.waits(eng, [("dsy", em.cnt["dsy"]), ("dgp", em.cnt["dgp"]),
                       ("v", em.cnt["v"]), ("s", em.cnt["s"]),
                       ("t", em.cnt["t"])])
    em.flush()
    g2.close()
    octx.close()
    return nc


def _host_constants():
    inv_freq = 1.0 / 10000.0 ** (np.arange(0, DH, 2, dtype=np.float32) / DH)
    t = np.arange(S, dtype=np.float32)
    freqs = np.outer(t, inv_freq)
    cos = np.concatenate([np.cos(freqs), np.cos(freqs)], -1)[:, ::2].T
    sin = np.concatenate([np.sin(freqs), np.sin(freqs)], -1)[:, ::2].T
    C64 = np.zeros((64, S), np.float32)
    St64 = np.zeros((64, S), np.float32)
    for i in range(32):
        C64[2 * i] = cos[i]
        C64[2 * i + 1] = cos[i]
        St64[2 * i] = sin[i]
        St64[2 * i + 1] = -sin[i]
    kC = np.concatenate([C64, C64], 0)
    kS = np.concatenate([St64, St64], 0)
    ident = np.eye(128, dtype=np.float32)
    bmask = np.full((128, 128), -1e30, np.float32)
    for g in range(32):
        bmask[4 * g:4 * g + 4, 4 * g:4 * g + 4] = 0.0
    return kC, kS, ident, bmask


def kernel(**inputs):
    x = np.asarray(inputs["L0_attn_a_x"], np.float32)
    kC, kS, ident, bmask = _host_constants()
    asc = np.ascontiguousarray
    base = {
        "xfull": x,
        "wdqT": asc(np.asarray(inputs["L0_attn_a_w_down_q"], np.float32).T),
        "wdkvT": asc(np.asarray(inputs["L0_attn_a_w_down_kv"], np.float32).T),
        "wuqT": asc(np.asarray(inputs["L0_attn_a_w_up_q"], np.float32).T),
        "wukT": asc(np.asarray(inputs["L0_attn_a_w_up_k"], np.float32).T),
        "wuvT": asc(np.asarray(inputs["L0_attn_a_w_up_v"], np.float32).T),
        "woutTa": asc(np.asarray(inputs["L0_attn_a_w_out"], np.float32).T),
        "lnsc": asc(np.asarray(inputs["L0_attn_a_ln_scale"],
                                np.float32).reshape(8, 128).T),
        "lnbi": asc(np.asarray(inputs["L0_attn_a_ln_bias"],
                               np.float32).reshape(8, 128).T),
        "winT": asc(np.asarray(inputs["L0_attn_b_in_proj_weight"], np.float32).T),
        "binb": asc(np.asarray(inputs["L0_attn_b_in_proj_bias"],
                               np.float32).reshape(24, 128).T),
        "woutTb": asc(np.asarray(inputs["L0_attn_b_out_proj_weight"],
                                 np.float32).T),
        "boutb": asc(np.asarray(inputs["L0_attn_b_out_proj_bias"],
                                np.float32).reshape(8, 128).T),
        "boutf": asc(np.asarray(inputs["L0_attn_b_out_proj_bias"],
                                np.float32).reshape(1, D)),
        "w1T": asc(np.asarray(inputs["L0_ffn_w1"], np.float32).T),
        "w2T": asc(np.asarray(inputs["L0_ffn_w2"], np.float32).T),
        "cw0": asc(np.asarray(inputs["L0_ffn_conv_w"],
                              np.float32)[:, 0, 0].reshape(32, 128).T),
        "cw1": asc(np.asarray(inputs["L0_ffn_conv_w"],
                              np.float32)[:, 0, 1].reshape(32, 128).T),
        "cw2": asc(np.asarray(inputs["L0_ffn_conv_w"],
                              np.float32)[:, 0, 2].reshape(32, 128).T),
        "cbv": asc(np.asarray(inputs["L0_ffn_conv_b"],
                              np.float32).reshape(32, 128).T),
        "ident": ident, "kC": kC, "kS": kS, "bmask": bmask,
    }
    xpad = np.zeros((B, S + 4, D), np.float32)
    xpad[:, 2:2 + S, :] = x
    kCp = np.zeros((128, S + 4), np.float32)
    kSp = np.zeros((128, S + 4), np.float32)
    kCp[:, 2:2 + S] = kC / 8.0
    kSp[:, 2:2 + S] = kS / 8.0
    in_maps = []
    for c in range(NCORES):
        m = dict(base)
        w0 = 512 * c
        m["xwin"] = asc(xpad[:, w0:w0 + WIN, :])
        m["qC"] = asc(kCp[:, w0:w0 + WIN])
        m["qS"] = asc(kSp[:, w0:w0 + WIN])
        mv = np.zeros(TP, np.float32)
        for sl in range(WIN):
            g = w0 + sl - 2
            if 0 <= g < S:
                mv[4 * sl:4 * sl + 4] = 1.0
        m["maskv"] = asc(mv.reshape(17, 128).T)
        in_maps.append(m)

    nc = build_nc()
    res = run_bass_kernel_spmd(nc, in_maps, core_ids=list(range(NCORES)))
    out = np.empty((B, S, D), np.float32)
    for c in range(NCORES):
        out[:, 512 * c:512 * (c + 1), :] = res.results[c]["yout"]
    return out
